# version 13
# baseline (speedup 1.0000x reference)
import numpy as np
from contextlib import ExitStack

import concourse.bass as bass
import concourse.mybir as mybir
from concourse.bass_utils import run_bass_kernel_spmd
from concourse import dve_ops

F32 = mybir.dt.float32
BF16 = mybir.dt.bfloat16
AF = mybir.ActivationFunctionType
ALU = mybir.AluOpType

D = 2048
S = 2048
HID = 8192
NCH = D // 128
NHID = HID // 128
NHEADS = 16
SEG = 512
NSEGS = S // SEG
PAR = 2
SEGH = SEG // PAR
INV_SQRT_HD = 0.08838834764831845
EXP_SHIFT = -15.0
LN_EPS = 1e-5
FWHT_SCALE = float(1.0 / np.sqrt(HID))


def _hadamard(n):
    h = np.array([[1.0]])
    while h.shape[0] < n:
        h = np.block([[h, h], [h, -h]])
    return h


def _bf16_u16(a):
    import ml_dtypes
    return np.ascontiguousarray(np.asarray(a, dtype=ml_dtypes.bfloat16).view(np.uint16))


class Bld:

    def __init__(self, nc, st):
        self.nc = nc
        self.st = st
        self.sems = {}
        self.val = {}
        self.waited = {}
        for name in ("pe", "act", "dve", "bar"):
            self.sems[name] = st.enter_context(nc.semaphore(name))
            self.val[name] = 0
        self.streams = {}
        self.engs = {
            "pe": nc.tensor, "act": nc.scalar, "dve": nc.vector,
            "gp": nc.gpsimd, "sy": nc.sync,
        }
        self.alias = {"ac": "act"}

    def inc(self, name, ins, n=1):
        ins.then_inc(self.sems[name], n)
        self.val[name] += n
        return self.val[name]

    def pre(self, name):
        self.wait(name, name)
        return self

    def wait(self, eng_name, sem_name, v=None):
        v = self.val[sem_name] if v is None else v
        if v <= 0:
            return
        eng_name = self.alias.get(eng_name, eng_name)
        key = (eng_name, sem_name)
        if self.waited.get(key, -1) < v:
            self.engs[eng_name].wait_ge(self.sems[sem_name], v)
            self.waited[key] = v

    def ds(self, name):
        if name not in self.streams:
            self.streams[name] = [self.st.enter_context(self.nc.semaphore(name)), 0]
        return self.streams[name]

    def dma(self, q, stream, out, in_, **kw):
        s = self.ds(stream)
        ins = self.engs[self.alias.get(q, q)].dma_start(out, in_, **kw)
        ins.then_inc(s[0], 16)
        s[1] += 16
        return s[1]

    def dwait(self, eng_name, stream, v=None):
        s = self.ds(stream)
        v = s[1] if v is None else v
        if v <= 0:
            return
        eng_name = self.alias.get(eng_name, eng_name)
        key = (eng_name, "D" + stream)
        if self.waited.get(key, -1) < v:
            self.engs[eng_name].wait_ge(s[0], v)
            self.waited[key] = v

    def bar(self):
        self.val["bar"] += len(self.engs)
        tgt = self.val["bar"]
        for e in self.engs.values():
            e.sem_inc(self.sems["bar"], 1)
        for name, e in self.engs.items():
            e.wait_ge(self.sems["bar"], tgt)
            self.waited[(name, "bar")] = tgt
            for s in ("pe", "act", "dve"):
                k = (name, s)
                self.waited[k] = max(self.waited.get(k, -1), self.val[s])

    _uid = 0

    def sb(self, name, shape, dt, st=None):
        Bld._uid += 1
        return (st or self.st).enter_context(
            self.nc.sbuf_tensor(f"{name}_{Bld._uid}", shape, dt))

    def ps(self, name, shape, dt=F32, st=None):
        Bld._uid += 1
        return (st or self.st).enter_context(
            self.nc.psum_tensor(f"{name}_{Bld._uid}", shape, dt))


def emit_consts(b):
    nc = b.nc
    h128 = _hadamard(128)
    h64 = _hadamard(64)
    h64blk = np.zeros((128, 128))
    h64blk[:64, :64] = h64
    h64blk[64:, 64:] = h64
    ident = np.eye(128)
    jj, ii = np.meshgrid(np.arange(128), np.arange(128), indexing="ij")
    dmask = np.where(jj <= ii, 0.0, -1e30).astype(np.float32)

    consts = {
        "h128": b.sb("c_h128", [128, 128], BF16),
        "h64blk": b.sb("c_h64b", [128, 128], BF16),
        "ident": b.sb("c_id", [128, 128], BF16),
        "dmask": b.sb("c_dmask", [128, 128], F32),
        "ones_bf": b.sb("c_onesb", [128, 1], BF16),
        "ones_f32": b.sb("c_onesf", [1, 128], F32),
        "eps": b.sb("c_eps", [128, 1], F32),
        "shift": b.sb("c_shift", [128, 1], F32),
    }
    d_h128 = nc.inline_tensor(_bf16_u16(h128), "i_h128")
    d_h64b = nc.inline_tensor(_bf16_u16(h64blk), "i_h64b")
    d_id = nc.inline_tensor(_bf16_u16(ident), "i_id")
    d_mask = nc.inline_tensor(dmask, "i_dmask")
    b.dma("sy", "cst", consts["h128"][:, :], d_h128.ap().bitcast(BF16))
    b.dma("sy", "cst", consts["h64blk"][:, :], d_h64b.ap().bitcast(BF16))
    b.dma("sy", "cst", consts["ident"][:, :], d_id.ap().bitcast(BF16))
    b.dma("sy", "cst", consts["dmask"][:, :], d_mask.ap())
    b.pre("dve").inc("dve", nc.vector.memset(consts["ones_bf"][:, :], 1.0))
    b.pre("dve").inc("dve", nc.vector.memset(consts["ones_f32"][:, :], 1.0))
    b.pre("dve").inc("dve", nc.vector.memset(consts["eps"][:, :], LN_EPS))
    b.pre("dve").inc("dve", nc.vector.memset(consts["shift"][:, :], EXP_SHIFT))
    b.dwait("pe", "cst")
    b.wait("pe", "dve")
    b.dwait("dve", "cst")
    b.dwait("act", "cst")
    return consts


def emit_ln(b, src_hbm, xn, cols0, ncols, consts, par_split=None, src_sbuf=None):
    nc = b.nc
    onesb, onesf = consts["ones_bf"], consts["ones_f32"]
    ngr = (ncols + 511) // 512
    with ExitStack() as st:
        xbf = xn if par_split is None else b.sb("ln_xbf", [128, NCH * ncols], BF16, st)
        rows = b.sb("ln_rows", [1, 4 * ncols], F32, st)
        ab = b.sb("ln_ab", [128, 2 * ncols], F32, st)
        scr = ExitStack()
        xf = [b.sb(f"ln_xf{i}", [128, ncols], F32, scr) for i in range(2)]
        m_r = rows[:, 0:ncols]
        rs_r = rows[:, ncols:2 * ncols]
        bm_r = rows[:, 2 * ncols:3 * ncols]
        sc_r = rows[:, 3 * ncols:4 * ncols]
        with ExitStack() as pst:
            zx = b.ps("ln_zx", [1, ncols], F32, pst)
            vd = 0
            vp = 0
            for dc in range(NCH):
                if src_sbuf is not None:
                    sb_t, sb_stream = src_sbuf
                    b.dwait("dve", sb_stream)
                    buf = sb_t[:, dc * ncols:(dc + 1) * ncols]
                else:
                    buf = xf[dc % 2]
                    if dc >= 2:
                        b.wait("sy", "dve", ld2[0])
                    b.dma("sy", f"lnx{dc % 2}", buf[:, :],
                          src_hbm[dc * 128:(dc + 1) * 128, cols0:cols0 + ncols])
                    b.dwait("dve", f"lnx{dc % 2}")
                    buf = buf[:, :]
                    ld2 = (vd,)
                vd = b.pre("dve").inc("dve", nc.vector.tensor_copy(
                    xbf[:, dc * ncols:(dc + 1) * ncols], buf))
                b.wait("pe", "dve", vd)
                for g in range(ngr):
                    w = min(512, ncols - g * 512)
                    vp = b.inc("pe", nc.tensor.matmul(
                        zx[:, g * 512:g * 512 + w], onesb[:, :],
                        xbf[:, dc * ncols + g * 512: dc * ncols + g * 512 + w],
                        start=(dc == 0), stop=(dc == NCH - 1)))
            b.wait("dve", "pe", vp)
            vd = b.pre("dve").inc("dve", nc.vector.tensor_scalar_mul(m_r, zx[:, :], 1.0 / D))
            scr.close()
            scr = ExitStack()
            x2b = [b.sb(f"ln_x2b{i}", [128, ncols], BF16, scr) for i in range(2)]
            b.wait("pe", "dve", vd)
            b.wait("act", "dve", vd)
            vsq = {}
            for dc in range(NCH):
                if dc >= 2:
                    b.wait("act", "pe", vsq[dc - 2])
                va = b.pre("act").inc("act", nc.scalar.activation(
                    x2b[dc % 2][:, :], xbf[:, dc * ncols:(dc + 1) * ncols],
                    AF.Square))
                b.wait("pe", "act", va)
                for g in range(ngr):
                    w = min(512, ncols - g * 512)
                    vp = b.inc("pe", nc.tensor.matmul(
                        zx[:, g * 512:g * 512 + w], onesb[:, :],
                        x2b[dc % 2][:, g * 512:g * 512 + w],
                        start=(dc == 0), stop=(dc == NCH - 1)))
                vsq[dc] = vp
            b.wait("dve", "pe", vp)
            b.pre("dve").inc("dve", nc.vector.tensor_scalar_mul(sc_r, zx[:, :], 1.0 / D))
            b.pre("dve").inc("dve", nc.vector.tensor_mul(rs_r, m_r, m_r))
            vd = b.pre("dve").inc("dve", nc.vector.tensor_sub(sc_r, sc_r, rs_r))
            b.wait("act", "dve", vd)
            b.pre("act").inc("act", nc.scalar.activation(sc_r, sc_r, AF.Ln, bias=consts["eps"][0:1, 0:1]))
            va = b.pre("act").inc("act", nc.scalar.activation(rs_r, sc_r, AF.Exp, scale=-0.5))
            b.wait("dve", "act", va)
            vd = b.pre("dve").inc("dve", nc.vector.tensor_mul(bm_r, m_r, rs_r))
        with ExitStack() as pst:
            nbc = min(ncols, 1024)
            pa = b.ps("ln_pa", [128, nbc], F32, pst)
            pb = b.ps("ln_pb", [128, nbc], F32, pst)
            b.wait("pe", "act", va)
            b.wait("pe", "dve", vd)
            for c0 in range(0, ncols, nbc):
                if c0 > 0:
                    b.wait("pe", "act", va)
                for g in range(0, nbc, 512):
                    sl = slice(c0 + g, c0 + g + min(512, nbc - g))
                    psl = slice(g, g + min(512, nbc - g))
                    b.inc("pe", nc.tensor.matmul(pa[:, psl], onesf[:, :],
                                                 rs_r[:, sl], start=True, stop=True))
                    vp = b.inc("pe", nc.tensor.matmul(pb[:, psl], onesf[:, :],
                                                      bm_r[:, sl], start=True, stop=True))
                b.wait("act", "pe", vp)
                b.pre("act").inc("act", nc.scalar.copy(
                    ab[:, c0:c0 + nbc], pa[:, :]))
                va = b.pre("act").inc("act", nc.scalar.copy(
                    ab[:, ncols + c0:ncols + c0 + nbc], pb[:, :]))
        scr.close()
        scr = ExitStack()
        t1 = b.sb("ln_t1", [128, ncols], F32, scr)
        A = ab[:, 0:ncols]
        B = ab[:, ncols:2 * ncols]
        b.wait("dve", "act", va)
        if par_split is None:
            for dc in range(NCH):
                xsl = xbf[:, dc * ncols:(dc + 1) * ncols]
                b.pre("dve").inc("dve", nc.vector.tensor_mul(t1[:, :], xsl, A))
                vd = b.pre("dve").inc("dve", nc.vector.tensor_sub(
                    xn[:, dc * ncols:(dc + 1) * ncols], t1[:, :], B))
        else:
            xnp, chlists = par_split
            for par in range(PAR):
                for slot, dc in enumerate(chlists[par]):
                    xsl = xbf[:, dc * ncols + par:(dc + 1) * ncols:PAR]
                    b.pre("dve").inc("dve", nc.vector.tensor_mul(
                        t1[:, 0:ncols // PAR], xsl, A[:, par::PAR]))
                    vd = b.pre("dve").inc("dve", nc.vector.tensor_sub(
                        xnp[par][:, slot * (ncols // PAR):(slot + 1) * (ncols // PAR)],
                        t1[:, 0:ncols // PAR], B[:, par::PAR]))
        scr.close()
    return vd


def emit_w2_prep(b, w2t, w2tmp, w2dd, consts):
    nc = b.nc
    with ExitStack() as st:
        blk = [b.sb(f"wp_blk{i}", [128, D], BF16, st) for i in range(3)]
        ev = [b.sb(f"wp_ev{i}", [128, D], BF16, st) for i in range(2)]
        pp = [b.ps(f"wp_p{i}", [128, 1024], F32, st) for i in range(2)]
        vcopy = {}
        vstore = {}
        vmm = {}

        def p1_load(j1):
            b.dma("gp", f"wpg{j1 % 3}", blk[j1 % 3][:, :],
                  w2t[j1 * 128:(j1 + 1) * 128, :])

        for j in range(3):
            p1_load(j)
        for j1 in range(64):
            k = j1 % 3
            e = j1 % 2
            b.dwait("pe", f"wpg{k}")
            for half in range(2):
                b.wait("pe", "act", vcopy.get((j1 - 1, half), 0))
                for g in range(2):
                    o = half * 1024 + g * 512
                    vp = b.inc("pe", nc.tensor.matmul(
                        pp[half][:, g * 512:(g + 1) * 512], consts["h128"][:, :],
                        blk[k][:, o:o + 512], start=True, stop=True))
                vmm[j1] = vp
                b.wait("act", "pe", vp)
                if half == 0 and j1 >= 2:
                    b.dwait("act", f"wpo{e}")
                vcopy[(j1, half)] = b.pre("act").inc("act", nc.scalar.copy(
                    ev[e][:, half * 1024:(half + 1) * 1024], pp[half][:, :]))
            b.wait("sy", "act", vcopy[(j1, 1)])
            b.dma("sy", f"wpo{e}", w2tmp[j1 * 128:(j1 + 1) * 128, :], ev[e][:, :])
            if j1 + 3 < 64:
                b.wait("gp", "pe", vmm[j1])
                p1_load(j1 + 3)
        b.dwait("sy", "wpo0")
        b.dwait("sy", "wpo1")
        b.wait("gp", "pe")
        tmp_r = w2tmp.rearrange("(j1 k2) d -> k2 j1 d", k2=128)
        dd_r = w2dd.rearrange("(k1 k2) d -> k2 k1 d", k2=128)
        b.dwait("gp", "wpo0")
        b.dwait("gp", "wpo1")

        def p2_load(i):
            b.dma("gp", f"wps{i % 3}", blk[i % 3][:, :], tmp_r[2 * i:2 * i + 2])

        for i in range(3):
            p2_load(i)
        vcopy = {}
        vmm = {}
        for i in range(64):
            k = i % 3
            e = i % 2
            b.dwait("pe", f"wps{k}")
            for half in range(2):
                b.wait("pe", "act", vcopy.get((i - 1, half), 0))
                for g in range(2):
                    o = half * 1024 + g * 512
                    vp = b.inc("pe", nc.tensor.matmul(
                        pp[half][:, g * 512:(g + 1) * 512], consts["h64blk"][:, :],
                        blk[k][:, o:o + 512], start=True, stop=True))
                vmm[i] = vp
                b.wait("act", "pe", vp)
                if half == 0 and i >= 2:
                    b.dwait("act", f"wpd{e}")
                vcopy[(i, half)] = b.pre("act").inc("act", nc.scalar.activation(
                    ev[e][:, half * 1024:(half + 1) * 1024], pp[half][:, :],
                    AF.Copy, scale=FWHT_SCALE))
            b.wait("sy", "act", vcopy[(i, 1)])
            b.dma("sy", f"wpd{e}", dd_r[2 * i:2 * i + 2], ev[e][:, :])
            if i + 3 < 64:
                b.wait("gp", "pe", vmm[i])
                p2_load(i + 3)
        b.dwait("sy", "wpd0")
        b.dwait("sy", "wpd1")


def _load_band(b, w1t, w3t, d1, d3, band):
    sl = slice(band * 256, (band + 1) * 256)
    ds = f"fab{band % 2}"
    b.dma("gp", ds, d1[:, :].rearrange("dp (dc h) -> dp dc h", dc=NCH),
          w1t[:, sl].rearrange("(dc dp) h -> dp dc h", dp=128))
    return b.dma("gp", ds, d3[:, :].rearrange("dp (dc h) -> dp dc h", dc=NCH),
                 w3t[:, sl].rearrange("(dc dp) h -> dp dc h", dp=128))


def emit_ffn(b, xres_src, dst_hbm, w1t, w3t, w2dd, hid, consts, prep_args=None):
    nc = b.nc
    with ExitStack() as st:
        with ExitStack() as ast:
            xn = b.sb("fa_xn", [128, NCH * S], BF16, ast)
            prep_gen = None
            if prep_args is not None:
                prep_gen = w2_prep_steps(b, prep_args[0], prep_args[1],
                                         w2dd, consts, ast)
            w1b = [b.sb(f"fa_w1{i}", [128, NCH * 256], BF16, ast) for i in range(2)]
            w3b = [b.sb(f"fa_w3{i}", [128, NCH * 256], BF16, ast) for i in range(2)]
            sg = [b.sb(f"fa_sg{i}", [128, 1024], F32, ast) for i in range(2)]
            hbuf = [b.sb(f"fa_hb{i}", [128, 1024], BF16, ast) for i in range(2)]
            wload = [_load_band(b, w1t, w3t, w1b[0], w3b[0], 0),
                     _load_band(b, w1t, w3t, w1b[1], w3b[1], 1)]
            vxn = emit_ln(b, xres_src, xn, 0, S, consts)
            b.wait("pe", "dve", vxn)
            pg = [b.ps(f"fa_pg{i}", [128, 1024], F32, ast) for i in range(2)]
            pl = [b.ps("fa_pl", [128, 1024], F32, ast)]
            hseq = []
            for band in range(32):
                k = band % 2
                b.dwait("pe", f"fab{band % 2}")
                for hcl in range(2):
                    hc = band * 2 + hcl
                    for th in range(2):
                        idx = hc * 2 + th
                        q = idx % 2
                        if len(hseq) >= 2:
                            b.wait("pe", "act", hseq[-2][0])
                        xo0 = th * 1024
                        for dc in range(NCH):
                            l1 = w1b[k][:, dc * 256 + hcl * 128:
                                        dc * 256 + hcl * 128 + 128]
                            xb = dc * S + xo0
                            b.inc("pe", nc.tensor.matmul(
                                pg[q][:, 0:512], l1, xn[:, xb:xb + 512],
                                start=(dc == 0), stop=(dc == NCH - 1)))
                            vpg = b.inc("pe", nc.tensor.matmul(
                                pg[q][:, 512:1024], l1, xn[:, xb + 512:xb + 1024],
                                start=(dc == 0), stop=(dc == NCH - 1)))
                        if len(hseq) >= 2:
                            b.wait("act", "dve", hseq[-2][1])
                        b.wait("act", "pe", vpg)
                        va = b.pre("act").inc("act", nc.scalar.activation(
                            sg[q][:, :], pg[q][:, :], AF.Silu))
                        if hseq:
                            b.wait("pe", "dve", hseq[-1][1])
                        for dc in range(NCH):
                            l3 = w3b[k][:, dc * 256 + hcl * 128:
                                        dc * 256 + hcl * 128 + 128]
                            xb = dc * S + xo0
                            b.inc("pe", nc.tensor.matmul(
                                pl[0][:, 0:512], l3, xn[:, xb:xb + 512],
                                start=(dc == 0), stop=(dc == NCH - 1)))
                            vp = b.inc("pe", nc.tensor.matmul(
                                pl[0][:, 512:1024], l3, xn[:, xb + 512:xb + 1024],
                                start=(dc == 0), stop=(dc == NCH - 1)))
                        b.dwait("dve", f"fah{q}")
                        b.wait("dve", "act", va)
                        b.wait("dve", "pe", vp)
                        vd = b.pre("dve").inc("dve", nc.vector.tensor_mul(
                            hbuf[q][:, :], sg[q][:, :], pl[0][:, :]))
                        b.wait("sy", "dve", vd)
                        vo = b.dma("sy", f"fah{q}", hid.ap()[th, hc], hbuf[q][:, :])
                        if prep_gen is not None:
                            next(prep_gen, None)
                        hseq.append((va, vd, vo))
                if band < 30:
                    b.wait("gp", "pe")
                    wload.append(_load_band(b, w1t, w3t, w1b[k], w3b[k], band + 2))
            b.dwait("sy", "fah0")
            b.dwait("sy", "fah1")
            if prep_gen is not None:
                for _ in prep_gen:
                    pass
                b.dwait("sy", "wpd0")
                b.dwait("sy", "wpd1")
        b.bar()
        with ExitStack() as bst:
            hstr = b.sb("fb_h", [128, NHID * 1024], BF16, bst)
            w2s = [b.sb(f"fb_w2{i}", [128, 256], BF16, bst) for i in range(8)]
            xr = [b.sb(f"fb_xr{i}", [128, 1024], F32, bst) for i in range(2)]
            ox = [b.sb(f"fb_ox{i}", [128, 1024], F32, bst) for i in range(2)]
            pb = [[b.ps(f"fb_p{i}{j}", [128, 1024], F32, bst) for j in range(2)]
                  for i in range(2)]
            hs_r = hstr[:, :].rearrange("dp (hc t) -> dp hc t", hc=NHID)
            for ts in range(2):
                if ts > 0:
                    b.wait("sy", "pe")
                    b.wait("ac", "pe")
                for part in range(8):
                    qn = "sy" if part % 2 == 0 else "ac"
                    b.dma(qn, "fbh" + qn, hs_r[:, part * 8:(part + 1) * 8],
                          hid.ap()[ts, part * 8:(part + 1) * 8].rearrange(
                              "hc dp t -> dp hc t"))
                pv = {}
                b.wait("ac", "pe")
                for pre in range(6):
                    b.dma("ac", f"fbw{pre}", w2s[pre][:, :],
                          w2dd[pre * 128:(pre + 1) * 128, 0:256])
                b.dwait("pe", "fbhsy")
                b.dwait("pe", "fbhac")
                ev = []
                evx = []
                for db in range(8):
                    q = db % 2
                    if db >= 2:
                        b.wait("pe", "dve", ev[db - 2][0])
                    for h in range(NHID):
                        idx = db * NHID + h
                        b.dwait("pe", f"fbw{idx % 8}")
                        ws = w2s[idx % 8]
                        for ds in range(2):
                            lh = ws[:, ds * 128:(ds + 1) * 128]
                            for th in range(2):
                                vp = b.inc("pe", nc.tensor.matmul(
                                    pb[q][ds][:, th * 512:(th + 1) * 512], lh,
                                    hstr[:, h * 1024 + th * 512:h * 1024 + th * 512 + 512],
                                    start=(h == 0), stop=(h == NHID - 1)))
                        pv[idx] = vp
                        nidx = idx + 6
                        if nidx < 8 * NHID:
                            ndb, nh = divmod(nidx, NHID)
                            if nidx >= 8:
                                b.wait("ac", "pe", pv[nidx - 8])
                            b.dma("ac", f"fbw{nidx % 8}", w2s[nidx % 8][:, :],
                                  w2dd[nh * 128:(nh + 1) * 128,
                                       ndb * 256:(ndb + 1) * 256])
                    for ds in range(2):
                        r0 = db * 256 + ds * 128
                        kx = (db * 2 + ds) % 2
                        if len(evx) >= 2:
                            b.wait("sy", "dve", evx[-2][0])
                        b.dma("sy", f"fbx{kx}", xr[kx][:, :],
                              xres_src[r0:r0 + 128, ts * 1024:(ts + 1) * 1024])
                        if len(evx) >= 2:
                            b.dwait("dve", f"fbo{kx}")
                        b.dwait("dve", f"fbx{kx}")
                        b.wait("dve", "pe", vp)
                        vd = b.pre("dve").inc("dve", nc.vector.tensor_add(
                            ox[kx][:, :], pb[q][ds][:, :], xr[kx][:, :]))
                        b.wait("sy", "dve", vd)
                        vo = b.dma("sy", f"fbo{kx}", dst_hbm[r0:r0 + 128,
                                   ts * 1024:(ts + 1) * 1024], ox[kx][:, :])
                        evx.append((vd, vo))
                    ev.append((vd, vo))
                b.dwait("sy", "fbo0")
                b.dwait("sy", "fbo1")
        b.bar()


def w2_prep_steps(b, w2t, w2tmp, w2dd, consts, st):
    nc = b.nc
    blk = [b.sb(f"wp_blk{i}", [128, D], BF16, st) for i in range(3)]
    ev = [b.sb(f"wp_ev{i}", [128, D], BF16, st) for i in range(2)]
    pp = [b.ps(f"wp_p{i}", [128, 512], F32, st) for i in range(2)]

    def p1_load(j1):
        b.dma("gp", f"wpg{j1 % 3}", blk[j1 % 3][:, :],
              w2t[j1 * 128:(j1 + 1) * 128, :])

    for j in range(3):
        p1_load(j)
    return _w2_gen(b, w2t, w2tmp, w2dd, consts, blk, ev, pp, p1_load)


def _w2_gen(b, w2t, w2tmp, w2dd, consts, blk, ev, pp, p1_load):
    nc = b.nc

    def one_pass(load_stream, out_stream, hmat, scale, dst_r, loader):
        vq = {}
        vmm = {}
        for it in range(64):
            k = it % 3
            e = it % 2
            b.dwait("pe", f"{load_stream}{k}")
            for qr in range(4):
                p = qr % 2
                eng = "act" if p == 0 else "dve"
                b.wait("pe", eng, vq.get(qr - 2 if qr >= 2 else (2 + qr, it - 1), 0)
                       if False else vq.get((it, qr - 2), vq.get((it - 1, qr + 2), 0)))
                vp = b.inc("pe", nc.tensor.matmul(
                    pp[p][:, :], hmat[:, :],
                    blk[k][:, qr * 512:(qr + 1) * 512], start=True, stop=True))
                if qr == 3:
                    vmm[it] = vp
                b.wait(eng, "pe", vp)
                if qr < 2 and it >= 2:
                    b.dwait(eng, f"{out_stream}{e}")
                dsl = ev[e][:, qr * 512:(qr + 1) * 512]
                if eng == "act":
                    vq[(it, qr)] = b.pre("act").inc("act", nc.scalar.activation(
                        dsl, pp[p][:, :], AF.Copy, scale=scale))
                else:
                    if scale != 1.0:
                        vq[(it, qr)] = b.pre("dve").inc(
                            "dve", nc.vector.tensor_scalar_mul(dsl, pp[p][:, :], scale))
                    else:
                        vq[(it, qr)] = b.pre("dve").inc(
                            "dve", nc.vector.tensor_copy(dsl, pp[p][:, :]))
            b.wait("sy", "act", vq[(it, 2)])
            b.wait("sy", "dve", vq[(it, 3)])
            b.dma("sy", f"{out_stream}{e}", dst_r(it), ev[e][:, :])
            if it + 3 < 64:
                b.wait("gp", "pe", vmm[it])
                loader(it + 3)
            yield

    yield from one_pass("wpg", "wpo", consts["h128"], 1.0,
                        lambda it: w2tmp[it * 128:(it + 1) * 128, :], p1_load)
    tmp_r = w2tmp.rearrange("(j1 k2) d -> k2 j1 d", k2=128)
    dd_r = w2dd.rearrange("(k1 k2) d -> k2 k1 d", k2=128)
    b.dwait("gp", "wpo0")
    b.dwait("gp", "wpo1")
    b.wait("gp", "pe")

    def p2_load(i):
        b.dma("gp", f"wps{i % 3}", blk[i % 3][:, :], tmp_r[2 * i:2 * i + 2])

    for i in range(3):
        p2_load(i)
    yield from one_pass("wps", "wpd", consts["h64blk"], FWHT_SCALE,
                        lambda it: dd_r[2 * it:2 * it + 2], p2_load)


def emit_attn_a(b, x_in, xres, consts, prep_gen=None):
    nc = b.nc
    chl = [[h for h in range(NHEADS) if h % 2 == p] for p in range(PAR)]
    slots = [(par, sl) for par in range(PAR) for sl in range(8)]
    for seg in range(NSEGS):
        c0 = seg * SEG
        with ExitStack() as st:
            xf = b.sb("aa_xf", [128, NCH * SEG], F32, st)
            xnp = [b.sb(f"aa_xnp{p}", [128, 8 * SEGH], BF16, st) for p in range(PAR)]
            gt = [b.sb(f"aa_gt{i}", [128, 2 * SEGH], BF16, st) for i in range(2)]
            et = [[b.sb(f"aa_et{i}{jb}", [128, SEGH], BF16, st) for jb in range(2)]
                  for i in range(2)]
            avs = [b.sb(f"aa_avs{i}", [128, SEGH], F32, st) for i in range(2)]
            rzr = [b.sb(f"aa_rz{i}", [1, SEGH], F32, st) for i in range(2)]
            an = b.sb("aa_an", [128, SEGH], F32, st)
            for dc in range(NCH):
                b.dma("sy", "aax", xf[:, dc * SEG:(dc + 1) * SEG],
                      x_in[dc * 128:(dc + 1) * 128, c0:c0 + SEG])
            vxn = emit_ln(b, x_in, None, c0, SEG, consts,
                          par_split=(xnp, chl), src_sbuf=(xf, "aax"))
            b.wait("pe", "dve", vxn)
            b.wait("sy", "dve", vxn)
            with ExitStack() as pst:
                psc = [b.ps(f"aa_psc{i}", [128, SEGH], F32, pst) for i in range(2)]
                pz = [b.ps(f"aa_pz{i}", [1, SEGH], F32, pst) for i in range(2)]
                pav = [b.ps(f"aa_pav{i}", [128, SEGH], F32, pst) for i in range(2)]
                prz = [b.ps(f"aa_prz{i}", [128, SEGH], F32, pst) for i in range(2)]
                stt = {"vas": {}, "vzav": {}, "vrz": [0, 0], "vavs": [0, 0],
                       "vpb": [0, 0], "vtail": [0, 0], "tail": 0}

                def stage1(s):
                    par, sl = slots[s]
                    g = xnp[par][:, sl * SEGH:(sl + 1) * SEGH]
                    sb = s % 2
                    if s >= 2:
                        b.wait("sy", "pe", stt["vzav"].get(s - 2, 0))
                    for jb in range(2):
                        b.dma("sy", f"agt{sb}", gt[sb][:, jb * 128:(jb + 1) * 128],
                              g[:, jb * 128:(jb + 1) * 128], transpose=True)
                    for jb in range(2):
                        b.wait("pe", "act", stt["vas"].get((s - 1, jb), 0))
                        vp = b.inc("pe", nc.tensor.matmul(
                            psc[jb][:, :], g[:, jb * 128:(jb + 1) * 128], g,
                            start=True, stop=True))
                        b.wait("act", "pe", vp)
                        if s >= 2:
                            b.wait("act", "pe", stt["vzav"].get(s - 2, 0))
                        stt["vas"][(s, jb)] = b.pre("act").inc(
                            "act", nc.scalar.activation(
                                et[sb][jb][:, :], psc[jb][:, :], AF.Exp,
                                bias=consts["shift"][:, :], scale=INV_SQRT_HD))

                def zav_recip(s):
                    par, sl = slots[s]
                    sb = s % 2
                    e = s % 2
                    b.wait("pe", "act", stt["vrz"][e])
                    b.wait("pe", "act", stt["vavs"][e])
                    b.dwait("pe", f"agt{sb}")
                    for jb in range(2):
                        b.wait("pe", "act", stt["vas"][(s, jb)])
                        b.inc("pe", nc.tensor.matmul(
                            pz[e][:, :], consts["ones_bf"][:, :], et[sb][jb][:, :],
                            start=(jb == 0), stop=(jb == 1)))
                        vp = b.inc("pe", nc.tensor.matmul(
                            pav[e][:, :], gt[sb][:, jb * 128:(jb + 1) * 128],
                            et[sb][jb][:, :], start=(jb == 0), stop=(jb == 1)))
                    stt["vzav"][s] = vp
                    b.wait("act", "pe", vp)
                    if stt["vpb"][e]:
                        b.wait("act", "pe", stt["vpb"][e])
                    b.pre("act").inc("act", nc.scalar.activation(
                        rzr[e][:, :], pz[e][:, :], AF.Ln))
                    stt["vrz"][e] = b.pre("act").inc("act", nc.scalar.activation(
                        rzr[e][:, :], rzr[e][:, :], AF.Exp, scale=-1.0))

                def tail(s):
                    par, sl = slots[s]
                    h = chl[par][sl]
                    e = s % 2
                    b.wait("pe", "act", stt["vrz"][e])
                    b.wait("pe", "dve", stt["vtail"][e])
                    stt["vpb"][e] = b.inc("pe", nc.tensor.matmul(
                        prz[e][:, :], consts["ones_f32"][:, :], rzr[e][:, :],
                        start=True, stop=True))
                    b.wait("act", "dve", stt["vtail"][e])
                    b.wait("act", "pe", stt["vzav"][s])
                    va = stt["vavs"][e] = b.pre("act").inc(
                        "act", nc.scalar.copy(avs[e][:, :], pav[e][:, :]))
                    b.wait("dve", "act", va)
                    b.wait("dve", "pe", stt["vpb"][e])
                    b.dwait("dve", "aax")
                    b.pre("dve").inc("dve", nc.vector.tensor_mul(
                        an[:, :], avs[e][:, :], prz[e][:, :]))
                    xsl = xf[:, h * SEG + par:(h + 1) * SEG:PAR]
                    stt["vtail"][e] = stt["tail"] = b.pre("dve").inc(
                        "dve", nc.vector.tensor_add(xsl, xsl, an[:, :]))

                stage1(0)
                stage1(1)
                for s in range(16):
                    zav_recip(s)
                    if s + 2 < 16:
                        stage1(s + 2)
                    if s >= 1:
                        tail(s - 1)
                tail(15)
            b.wait("sy", "dve")
            for dc in range(NCH):
                b.dma("sy", "aar", xres[dc * 128:(dc + 1) * 128, c0:c0 + SEG],
                      xf[:, dc * SEG:(dc + 1) * SEG])
            b.dwait("sy", "aar")
        b.bar()


def emit_attn_b(b, xres, wqt, wkt, wvt, wot, consts):
    nc = b.nc
    with ExitStack() as st:
        ao = b.sb("ab_ao", [128, NCH * S], BF16, st)
        with ExitStack() as hst:
            xn = b.sb("ab_xn", [128, NCH * S], BF16, hst)
            vxn = emit_ln(b, xres, xn, 0, S, consts)
            b.wait("pe", "dve", vxn)
            b.wait("gp", "dve", vxn)
            b.wait("gp", "act")
            qw = [b.sb(f"ab_qw{i}", [128, NCH * 128], BF16, hst) for i in range(2)]
            kw = [b.sb(f"ab_kw{i}", [128, NCH * 128], BF16, hst) for i in range(2)]
            vw = [b.sb(f"ab_vw{i}", [128, NCH * 128], BF16, hst) for i in range(2)]
            qh = [b.sb(f"ab_qh{i}", [128, S], BF16, hst) for i in range(2)]
            kh = [b.sb(f"ab_kh{i}", [128, S], BF16, hst) for i in range(2)]
            vh = [b.sb(f"ab_vh{i}", [128, S], BF16, hst) for i in range(2)]
            vt = [b.sb(f"ab_vt{i}", [128, S], BF16, hst) for i in range(2)]
            et = [b.sb(f"ab_et{i}", [128, 512], BF16, hst) for i in range(2)]
            avs = [b.sb(f"ab_avs{i}", [128, 512], F32, hst) for i in range(2)]
            rzr = [b.sb(f"ab_rz{i}", [1, 512], F32, hst) for i in range(2)]

            def load_head_w(h):
                for wt, dst in ((wqt, qw), (wkt, kw), (wvt, vw)):
                    b.dma("gp", f"abw{h % 2}", dst[h % 2][:, :].rearrange(
                        "dp (dc o) -> dp dc o", dc=NCH),
                        wt[:, h * 128:(h + 1) * 128].rearrange(
                            "(dc dp) o -> dp dc o", dp=128))

            load_head_w(0)
            load_head_w(1)
            with ExitStack() as pst:
                pq = [b.ps("ab_pq", [128, 512], F32, pst)]
                psc = [b.ps(f"ab_psc{i}", [128, 512], F32, pst) for i in range(2)]
                pz = [b.ps(f"ab_pz{i}", [1, 512], F32, pst) for i in range(2)]
                pav = [b.ps(f"ab_pav{i}", [128, 512], F32, pst) for i in range(2)]
                prz = b.ps("ab_prz", [128, 512], F32, pst)
                stt = {"pq_prev": [], "vtr": 0, "qv": 0, "pe_proj": {},
                       "ready": {}, "vhva": {}}

                def qkv_units(h):
                    k = h % 2

                    def mk_proj(di, tg):
                        def emit():
                            dst = (qh[k], kh[k], vh[k])[di]
                            wsb = (qw[k], kw[k], vw[k])[di]
                            q = 0
                            b.dwait("pe", f"abw{k}")
                            if stt["pq_prev"]:
                                b.wait("pe", "dve", stt["pq_prev"][-1])
                            for dc in range(NCH):
                                vp = b.inc("pe", nc.tensor.matmul(
                                    pq[q][:, :], wsb[:, dc * 128:(dc + 1) * 128],
                                    xn[:, dc * S + tg * 512:dc * S + tg * 512 + 512],
                                    start=(dc == 0), stop=(dc == NCH - 1)))
                            if di == 2 and tg == 3:
                                stt["pe_proj"][h] = vp
                            b.wait("dve", "pe", vp)
                            va = b.pre("dve").inc("dve", nc.vector.tensor_copy(
                                dst[:, tg * 512:(tg + 1) * 512], pq[q][:, :]))
                            stt["pq_prev"].append(va)
                            if di == 2:
                                stt["vhva"][(h, tg)] = va
                        return emit

                    def mk_vt(j):
                        def emit():
                            b.wait("sy", "dve", stt["vhva"][(h, j // 4)])
                            if j == 0:
                                b.wait("sy", "pe")
                            b.dma("sy", f"vtr{k}", vt[k][:, j * 128:(j + 1) * 128],
                                  vh[k][:, j * 128:(j + 1) * 128], transpose=True)
                            if j == 15:
                                stt["ready"][h] = stt["pq_prev"][-1]
                        return emit

                    return ([mk_proj(di, tg) for di in range(3) for tg in range(4)]
                            + [mk_vt(j) for j in range(16)])

                for u in qkv_units(0):
                    u()
                vd_tail = 0
                vd_tails = [0, 0]
                vpb_prev = 0
                vrz_prev = [0, 0]
                vavs_prev = [0, 0]
                for h in range(NHEADS):
                    k = h % 2
                    units = qkv_units(h + 1) if h + 1 < NHEADS else []
                    nslots = 44
                    slot = [0, 0]

                    def tick():
                        slot[0] += 1
                        tgt = (slot[0] * len(units)) // nslots
                        while slot[1] < min(tgt, len(units)):
                            units[slot[1]]()
                            slot[1] += 1

                    if h + 2 < NHEADS:
                        b.wait("gp", "pe", stt["pe_proj"][h])
                        load_head_w(h + 2)
                    b.dwait("pe", f"vtr{k}")
                    b.wait("pe", "dve", stt["ready"][h])
                    for ig in range(4):
                        jmax = 4 * ig + 3
                        vas = {}
                        vzav = {}

                        e = (h * 4 + ig) % 2

                        def emit_zav(j, start, stop):
                            q = j % 2
                            osj = max(0, j * 128 - ig * 512)
                            b.wait("pe", "act", vas[j])
                            b.inc("pe", nc.tensor.matmul(
                                pz[e][:, osj:512], consts["ones_bf"][:, :],
                                et[q][:, osj:512], start=start, stop=stop))
                            vzav[j] = b.inc("pe", nc.tensor.matmul(
                                pav[e][:, osj:512], vt[k][:, j * 128:(j + 1) * 128],
                                et[q][:, osj:512], start=start, stop=stop))

                        b.wait("pe", "act", vrz_prev[e])
                        b.wait("pe", "act", vavs_prev[e])
                        for j in range(jmax + 1):
                            osj = max(0, j * 128 - ig * 512)
                            q = j % 2
                            if j >= 2:
                                b.wait("pe", "act", vas[j - 2])
                            vp = b.inc("pe", nc.tensor.matmul(
                                psc[q][:, osj:512], kh[k][:, j * 128:(j + 1) * 128],
                                qh[k][:, ig * 512 + osj:(ig + 1) * 512],
                                start=True, stop=True))
                            vdm = 0
                            if j * 128 >= ig * 512:
                                b.wait("dve", "pe", vp)
                                vdm = b.pre("dve").inc("dve", nc.vector.tensor_add(
                                    psc[q][:, osj:osj + 128],
                                    psc[q][:, osj:osj + 128], consts["dmask"][:, :]))
                            if j >= 1:
                                emit_zav(j - 1, j - 1 == 0, False)
                            tick()
                            b.wait("act", "pe", vp)
                            if vdm:
                                b.wait("act", "dve", vdm)
                            if j >= 2:
                                b.wait("act", "pe", vzav[j - 2])
                            vas[j] = b.pre("act").inc("act", nc.scalar.activation(
                                et[q][:, osj:512], psc[q][:, osj:512], AF.Exp,
                                bias=consts["shift"][:, :], scale=INV_SQRT_HD))
                        emit_zav(jmax, jmax == 0, True)
                        vp = vzav[jmax]
                        b.wait("act", "pe", vp)
                        if vpb_prev:
                            b.wait("act", "pe", vpb_prev)
                        b.pre("act").inc("act", nc.scalar.activation(
                            rzr[e][:, :], pz[e][:, :], AF.Ln))
                        vrz = vrz_prev[e] = b.pre("act").inc(
                            "act", nc.scalar.activation(
                                rzr[e][:, :], rzr[e][:, :], AF.Exp, scale=-1.0))
                        tick()
                        b.wait("pe", "act", vrz)
                        b.wait("pe", "dve", vd_tail)
                        vpb_prev = b.inc("pe", nc.tensor.matmul(
                            prz[:, :], consts["ones_f32"][:, :], rzr[e][:, :],
                            start=True, stop=True))
                        b.wait("act", "dve", vd_tails[e])
                        b.wait("act", "pe", vp)
                        va = vavs_prev[e] = b.pre("act").inc("act", nc.scalar.copy(
                            avs[e][:, :], pav[e][:, :]))
                        b.wait("dve", "act", va)
                        b.wait("dve", "pe", vpb_prev)
                        vd_tail = vd_tails[e] = b.pre("dve").inc(
                            "dve", nc.vector.tensor_mul(
                                ao[:, h * S + ig * 512:h * S + (ig + 1) * 512],
                                avs[e][:, :], prz[:, :]))
                    for u in units[slot[1]:]:
                        u()
                    slot[1] = len(units)
                b.wait("pe", "dve", vd_tail)
        b.bar()
        with ExitStack() as wst:
            wob = [b.sb(f"ab_wo{i}", [128, NCH * 128], BF16, wst) for i in range(3)]
            xr = [b.sb(f"ab_xr{i}", [128, 512], F32, wst) for i in range(3)]
            oxt = [b.sb(f"ab_ox{i}", [128, 512], F32, wst) for i in range(3)]
            pwo = [b.ps(f"ab_pw{i}", [128, 512], F32, wst) for i in range(3)]

            def load_wo(do):
                return b.dma(
                    "gp", f"awo{do % 3}",
                    wob[do % 3][:, :].rearrange("dp (dc o) -> dp dc o", dc=NCH),
                    wot[:, do * 128:(do + 1) * 128].rearrange(
                        "(dc dp) o -> dp dc o", dp=128))

            load_wo(0)
            load_wo(1)
            pv_wo = {}
            seq = []
            for do in range(NCH):
                k = do % 3
                b.dwait("pe", f"awo{do % 3}")
                if do + 2 < NCH:
                    if do >= 1:
                        b.wait("gp", "pe", pv_wo[do - 1])
                    load_wo(do + 2)
                for tg in range(4):
                    idx = do * 4 + tg
                    q = idx % 3
                    if idx >= 3:
                        b.wait("pe", "dve", seq[idx - 3][0])
                    for dc in range(NCH):
                        vp = b.inc("pe", nc.tensor.matmul(
                            pwo[q][:, :], wob[k][:, dc * 128:(dc + 1) * 128],
                            ao[:, dc * S + tg * 512:dc * S + tg * 512 + 512],
                            start=(dc == 0), stop=(dc == NCH - 1)))
                    if idx >= 3:
                        b.wait("ac", "dve", seq[idx - 3][0])
                    b.dma("ac", f"abx{q}", xr[q][:, :],
                          xres[do * 128:(do + 1) * 128, tg * 512:(tg + 1) * 512])
                    if idx >= 3:
                        b.dwait("dve", f"abo{q}")
                    b.dwait("dve", f"abx{q}")
                    b.wait("dve", "pe", vp)
                    vd = b.pre("dve").inc("dve", nc.vector.tensor_add(
                        oxt[q][:, :], pwo[q][:, :], xr[q][:, :]))
                    b.wait("sy", "dve", vd)
                    vo = b.dma("sy", f"abo{q}", xres[do * 128:(do + 1) * 128,
                               tg * 512:(tg + 1) * 512], oxt[q][:, :])
                    pv_wo[do] = vp
                    seq.append((vd, vo))
            b.dwait("sy", "abo0")
            b.dwait("sy", "abo1")
            b.dwait("sy", "abo2")
        b.bar()


def build_nc(phases=("attn_a", "prep", "ffn1", "attn_b", "ffn2")):
    nc = bass.Bass("TRN2", target_bir_lowering=False)
    x_in = nc.dram_tensor("x", [D, S], F32, kind="ExternalInput").ap()
    w1t = nc.dram_tensor("w1t", [D, HID], F32, kind="ExternalInput").ap()
    w3t = nc.dram_tensor("w3t", [D, HID], F32, kind="ExternalInput").ap()
    w2t = nc.dram_tensor("w2t", [HID, D], F32, kind="ExternalInput").ap()
    wqt = nc.dram_tensor("wqt", [D, D], F32, kind="ExternalInput").ap()
    wkt = nc.dram_tensor("wkt", [D, D], F32, kind="ExternalInput").ap()
    wvt = nc.dram_tensor("wvt", [D, D], F32, kind="ExternalInput").ap()
    wot = nc.dram_tensor("wot", [D, D], F32, kind="ExternalInput").ap()
    out = nc.dram_tensor("out", [D, S], F32, kind="ExternalOutput").ap()
    xres = nc.dram_tensor("xres", [D, S], F32, kind="Internal").ap()
    w2tmp = nc.dram_tensor("w2tmp", [HID, D], BF16, kind="Internal").ap()
    w2dd = nc.dram_tensor("w2dd", [HID, D], BF16, kind="Internal").ap()
    hid = nc.dram_tensor("hid", [2, NHID, 128, 1024], BF16, kind="Internal")

    with ExitStack() as st:
        b = Bld(nc, st)
        consts = emit_consts(b)
        if "attn_a" in phases:
            emit_attn_a(b, x_in, xres, consts)
        if "prep" in phases and "ffn1" not in phases:
            with ExitStack() as pst:
                for _ in w2_prep_steps(b, w2t, w2tmp, w2dd, consts, pst):
                    pass
                b.dwait("sy", "wpd0")
                b.dwait("sy", "wpd1")
            b.bar()
        if "ffn1" in phases:
            pa = (w2t, w2tmp) if "prep" in phases else None
            emit_ffn(b, xres, xres, w1t, w3t, w2dd, hid, consts, prep_args=pa)
        if "attn_b" in phases:
            emit_attn_b(b, xres, wqt, wkt, wvt, wot, consts)
        if "ffn2" in phases:
            emit_ffn(b, xres, out, w1t, w3t, w2dd, hid, consts)
        b.dwait("sy", "fbo0")
        b.dwait("sy", "fbo1")
    return nc


def kernel(L0_attn_a_x, L0_attn_b_wq, L0_attn_b_wk, L0_attn_b_wv, L0_attn_b_wo,
           L0_ffn_W1, L0_ffn_W2, L0_ffn_W3):
    x = np.asarray(L0_attn_a_x, dtype=np.float32)
    wshared = {
        "w1t": np.ascontiguousarray(np.asarray(L0_ffn_W1, np.float32).T),
        "w3t": np.ascontiguousarray(np.asarray(L0_ffn_W3, np.float32).T),
        "w2t": np.ascontiguousarray(np.asarray(L0_ffn_W2, np.float32).T),
        "wqt": np.ascontiguousarray(np.asarray(L0_attn_b_wq, np.float32).T),
        "wkt": np.ascontiguousarray(np.asarray(L0_attn_b_wk, np.float32).T),
        "wvt": np.ascontiguousarray(np.asarray(L0_attn_b_wv, np.float32).T),
        "wot": np.ascontiguousarray(np.asarray(L0_attn_b_wo, np.float32).T),
    }
    nc = build_nc()
    in_maps = [dict(wshared, x=np.ascontiguousarray(x[i].T)) for i in range(8)]
    res = run_bass_kernel_spmd(nc, in_maps, core_ids=list(range(8)))
    outs = [np.ascontiguousarray(np.asarray(r["out"]).T) for r in res.results]
    return np.stack(outs, axis=0)


if __name__ == "__main__":
    nc = build_nc()
    print("instructions:", nc.n_instructions())


# version 14
# speedup vs baseline: 1.0633x; 1.0633x over previous
import numpy as np
from contextlib import ExitStack

import concourse.bass as bass
import concourse.mybir as mybir
from concourse.bass_utils import run_bass_kernel_spmd
from concourse import dve_ops

F32 = mybir.dt.float32
BF16 = mybir.dt.bfloat16
AF = mybir.ActivationFunctionType
ALU = mybir.AluOpType

D = 2048
S = 2048
HID = 8192
NCH = D // 128
NHID = HID // 128
NHEADS = 16
SEG = 512
NSEGS = S // SEG
PAR = 2
SEGH = SEG // PAR
INV_SQRT_HD = 0.08838834764831845
EXP_SHIFT = -15.0
LN_EPS = 1e-5
FWHT_SCALE = float(1.0 / np.sqrt(HID))


def _hadamard(n):
    h = np.array([[1.0]])
    while h.shape[0] < n:
        h = np.block([[h, h], [h, -h]])
    return h


def _bf16_u16(a):
    import ml_dtypes
    return np.ascontiguousarray(np.asarray(a, dtype=ml_dtypes.bfloat16).view(np.uint16))


class Bld:

    def __init__(self, nc, st):
        self.nc = nc
        self.st = st
        self.sems = {}
        self.val = {}
        self.waited = {}
        for name in ("pe", "act", "dve", "bar"):
            self.sems[name] = st.enter_context(nc.semaphore(name))
            self.val[name] = 0
        self.streams = {}
        self.engs = {
            "pe": nc.tensor, "act": nc.scalar, "dve": nc.vector,
            "gp": nc.gpsimd, "sy": nc.sync,
        }
        self.alias = {"ac": "act"}

    def inc(self, name, ins, n=1):
        ins.then_inc(self.sems[name], n)
        self.val[name] += n
        return self.val[name]

    def pre(self, name):
        self.wait(name, name)
        return self

    def wait(self, eng_name, sem_name, v=None):
        v = self.val[sem_name] if v is None else v
        if v <= 0:
            return
        eng_name = self.alias.get(eng_name, eng_name)
        key = (eng_name, sem_name)
        if self.waited.get(key, -1) < v:
            self.engs[eng_name].wait_ge(self.sems[sem_name], v)
            self.waited[key] = v

    def ds(self, name):
        if name not in self.streams:
            self.streams[name] = [self.st.enter_context(self.nc.semaphore(name)), 0]
        return self.streams[name]

    def dma(self, q, stream, out, in_, **kw):
        s = self.ds(stream)
        ins = self.engs[self.alias.get(q, q)].dma_start(out, in_, **kw)
        ins.then_inc(s[0], 16)
        s[1] += 16
        return s[1]

    def dwait(self, eng_name, stream, v=None):
        s = self.ds(stream)
        v = s[1] if v is None else v
        if v <= 0:
            return
        eng_name = self.alias.get(eng_name, eng_name)
        key = (eng_name, "D" + stream)
        if self.waited.get(key, -1) < v:
            self.engs[eng_name].wait_ge(s[0], v)
            self.waited[key] = v

    def bar(self):
        self.val["bar"] += len(self.engs)
        tgt = self.val["bar"]
        for e in self.engs.values():
            e.sem_inc(self.sems["bar"], 1)
        for name, e in self.engs.items():
            e.wait_ge(self.sems["bar"], tgt)
            self.waited[(name, "bar")] = tgt
            for s in ("pe", "act", "dve"):
                k = (name, s)
                self.waited[k] = max(self.waited.get(k, -1), self.val[s])

    _uid = 0

    def sb(self, name, shape, dt, st=None):
        Bld._uid += 1
        return (st or self.st).enter_context(
            self.nc.sbuf_tensor(f"{name}_{Bld._uid}", shape, dt))

    def ps(self, name, shape, dt=F32, st=None):
        Bld._uid += 1
        return (st or self.st).enter_context(
            self.nc.psum_tensor(f"{name}_{Bld._uid}", shape, dt))


def emit_consts(b):
    nc = b.nc
    h128 = _hadamard(128)
    h64 = _hadamard(64)
    h64blk = np.zeros((128, 128))
    h64blk[:64, :64] = h64
    h64blk[64:, 64:] = h64
    ident = np.eye(128)
    jj, ii = np.meshgrid(np.arange(128), np.arange(128), indexing="ij")
    dmask = np.where(jj <= ii, 0.0, -1e30).astype(np.float32)

    consts = {
        "h128": b.sb("c_h128", [128, 128], BF16),
        "h64blk": b.sb("c_h64b", [128, 128], BF16),
        "ident": b.sb("c_id", [128, 128], BF16),
        "dmask": b.sb("c_dmask", [128, 128], F32),
        "ones_bf": b.sb("c_onesb", [128, 1], BF16),
        "ones_f32": b.sb("c_onesf", [1, 128], F32),
        "eps": b.sb("c_eps", [128, 1], F32),
        "shift": b.sb("c_shift", [128, 1], F32),
    }
    d_h128 = nc.inline_tensor(_bf16_u16(h128), "i_h128")
    d_h64b = nc.inline_tensor(_bf16_u16(h64blk), "i_h64b")
    d_id = nc.inline_tensor(_bf16_u16(ident), "i_id")
    d_mask = nc.inline_tensor(dmask, "i_dmask")
    b.dma("sy", "cst", consts["h128"][:, :], d_h128.ap().bitcast(BF16))
    b.dma("sy", "cst", consts["h64blk"][:, :], d_h64b.ap().bitcast(BF16))
    b.dma("sy", "cst", consts["ident"][:, :], d_id.ap().bitcast(BF16))
    b.dma("sy", "cst", consts["dmask"][:, :], d_mask.ap())
    b.pre("dve").inc("dve", nc.vector.memset(consts["ones_bf"][:, :], 1.0))
    b.pre("dve").inc("dve", nc.vector.memset(consts["ones_f32"][:, :], 1.0))
    b.pre("dve").inc("dve", nc.vector.memset(consts["eps"][:, :], LN_EPS))
    b.pre("dve").inc("dve", nc.vector.memset(consts["shift"][:, :], EXP_SHIFT))
    b.dwait("pe", "cst")
    b.wait("pe", "dve")
    b.dwait("dve", "cst")
    b.dwait("act", "cst")
    return consts


def emit_ln(b, src_hbm, xn, cols0, ncols, consts, par_split=None, src_sbuf=None):
    nc = b.nc
    onesb, onesf = consts["ones_bf"], consts["ones_f32"]
    ngr = (ncols + 511) // 512
    with ExitStack() as st:
        xbf = xn if par_split is None else b.sb("ln_xbf", [128, NCH * ncols], BF16, st)
        rows = b.sb("ln_rows", [1, 4 * ncols], F32, st)
        ab = b.sb("ln_ab", [128, 2 * ncols], F32, st)
        scr = ExitStack()
        xf = [b.sb(f"ln_xf{i}", [128, ncols], F32, scr) for i in range(2)]
        m_r = rows[:, 0:ncols]
        rs_r = rows[:, ncols:2 * ncols]
        bm_r = rows[:, 2 * ncols:3 * ncols]
        sc_r = rows[:, 3 * ncols:4 * ncols]
        with ExitStack() as pst:
            zx = b.ps("ln_zx", [1, ncols], F32, pst)
            vd = 0
            vp = 0
            for dc in range(NCH):
                if src_sbuf is not None:
                    sb_t, sb_stream = src_sbuf
                    b.dwait("dve", sb_stream)
                    buf = sb_t[:, dc * ncols:(dc + 1) * ncols]
                else:
                    buf = xf[dc % 2]
                    if dc >= 2:
                        b.wait("sy", "dve", ld2[0])
                    b.dma("sy", f"lnx{dc % 2}", buf[:, :],
                          src_hbm[dc * 128:(dc + 1) * 128, cols0:cols0 + ncols])
                    b.dwait("dve", f"lnx{dc % 2}")
                    buf = buf[:, :]
                    ld2 = (vd,)
                vd = b.pre("dve").inc("dve", nc.vector.tensor_copy(
                    xbf[:, dc * ncols:(dc + 1) * ncols], buf))
                b.wait("pe", "dve", vd)
                for g in range(ngr):
                    w = min(512, ncols - g * 512)
                    vp = b.inc("pe", nc.tensor.matmul(
                        zx[:, g * 512:g * 512 + w], onesb[:, :],
                        xbf[:, dc * ncols + g * 512: dc * ncols + g * 512 + w],
                        start=(dc == 0), stop=(dc == NCH - 1)))
            b.wait("dve", "pe", vp)
            vd = b.pre("dve").inc("dve", nc.vector.tensor_scalar_mul(m_r, zx[:, :], 1.0 / D))
            scr.close()
            scr = ExitStack()
            x2b = [b.sb(f"ln_x2b{i}", [128, ncols], BF16, scr) for i in range(2)]
            b.wait("pe", "dve", vd)
            b.wait("act", "dve", vd)
            vsq = {}
            for dc in range(NCH):
                if dc >= 2:
                    b.wait("act", "pe", vsq[dc - 2])
                va = b.pre("act").inc("act", nc.scalar.activation(
                    x2b[dc % 2][:, :], xbf[:, dc * ncols:(dc + 1) * ncols],
                    AF.Square))
                b.wait("pe", "act", va)
                for g in range(ngr):
                    w = min(512, ncols - g * 512)
                    vp = b.inc("pe", nc.tensor.matmul(
                        zx[:, g * 512:g * 512 + w], onesb[:, :],
                        x2b[dc % 2][:, g * 512:g * 512 + w],
                        start=(dc == 0), stop=(dc == NCH - 1)))
                vsq[dc] = vp
            b.wait("dve", "pe", vp)
            b.pre("dve").inc("dve", nc.vector.tensor_scalar_mul(sc_r, zx[:, :], 1.0 / D))
            b.pre("dve").inc("dve", nc.vector.tensor_mul(rs_r, m_r, m_r))
            vd = b.pre("dve").inc("dve", nc.vector.tensor_sub(sc_r, sc_r, rs_r))
            b.wait("act", "dve", vd)
            b.pre("act").inc("act", nc.scalar.activation(sc_r, sc_r, AF.Ln, bias=consts["eps"][0:1, 0:1]))
            va = b.pre("act").inc("act", nc.scalar.activation(rs_r, sc_r, AF.Exp, scale=-0.5))
            b.wait("dve", "act", va)
            vd = b.pre("dve").inc("dve", nc.vector.tensor_mul(bm_r, m_r, rs_r))
        with ExitStack() as pst:
            nbc = min(ncols, 1024)
            pa = b.ps("ln_pa", [128, nbc], F32, pst)
            pb = b.ps("ln_pb", [128, nbc], F32, pst)
            b.wait("pe", "act", va)
            b.wait("pe", "dve", vd)
            for c0 in range(0, ncols, nbc):
                if c0 > 0:
                    b.wait("pe", "act", va)
                for g in range(0, nbc, 512):
                    sl = slice(c0 + g, c0 + g + min(512, nbc - g))
                    psl = slice(g, g + min(512, nbc - g))
                    b.inc("pe", nc.tensor.matmul(pa[:, psl], onesf[:, :],
                                                 rs_r[:, sl], start=True, stop=True))
                    vp = b.inc("pe", nc.tensor.matmul(pb[:, psl], onesf[:, :],
                                                      bm_r[:, sl], start=True, stop=True))
                b.wait("act", "pe", vp)
                b.pre("act").inc("act", nc.scalar.copy(
                    ab[:, c0:c0 + nbc], pa[:, :]))
                va = b.pre("act").inc("act", nc.scalar.copy(
                    ab[:, ncols + c0:ncols + c0 + nbc], pb[:, :]))
        scr.close()
        scr = ExitStack()
        t1 = b.sb("ln_t1", [128, ncols], F32, scr)
        A = ab[:, 0:ncols]
        B = ab[:, ncols:2 * ncols]
        b.wait("dve", "act", va)
        if par_split is None:
            for dc in range(NCH):
                xsl = xbf[:, dc * ncols:(dc + 1) * ncols]
                b.pre("dve").inc("dve", nc.vector.tensor_mul(t1[:, :], xsl, A))
                vd = b.pre("dve").inc("dve", nc.vector.tensor_sub(
                    xn[:, dc * ncols:(dc + 1) * ncols], t1[:, :], B))
        else:
            xnp, chlists = par_split
            for par in range(PAR):
                for slot, dc in enumerate(chlists[par]):
                    xsl = xbf[:, dc * ncols + par:(dc + 1) * ncols:PAR]
                    b.pre("dve").inc("dve", nc.vector.tensor_mul(
                        t1[:, 0:ncols // PAR], xsl, A[:, par::PAR]))
                    vd = b.pre("dve").inc("dve", nc.vector.tensor_sub(
                        xnp[par][:, slot * (ncols // PAR):(slot + 1) * (ncols // PAR)],
                        t1[:, 0:ncols // PAR], B[:, par::PAR]))
        scr.close()
    return vd


def emit_w2_prep(b, w2t, w2tmp, w2dd, consts):
    nc = b.nc
    with ExitStack() as st:
        blk = [b.sb(f"wp_blk{i}", [128, D], BF16, st) for i in range(3)]
        ev = [b.sb(f"wp_ev{i}", [128, D], BF16, st) for i in range(2)]
        pp = [b.ps(f"wp_p{i}", [128, 1024], F32, st) for i in range(2)]
        vcopy = {}
        vstore = {}
        vmm = {}

        def p1_load(j1):
            b.dma("gp", f"wpg{j1 % 3}", blk[j1 % 3][:, :],
                  w2t[j1 * 128:(j1 + 1) * 128, :])

        for j in range(3):
            p1_load(j)
        for j1 in range(64):
            k = j1 % 3
            e = j1 % 2
            b.dwait("pe", f"wpg{k}")
            for half in range(2):
                b.wait("pe", "act", vcopy.get((j1 - 1, half), 0))
                for g in range(2):
                    o = half * 1024 + g * 512
                    vp = b.inc("pe", nc.tensor.matmul(
                        pp[half][:, g * 512:(g + 1) * 512], consts["h128"][:, :],
                        blk[k][:, o:o + 512], start=True, stop=True))
                vmm[j1] = vp
                b.wait("act", "pe", vp)
                if half == 0 and j1 >= 2:
                    b.dwait("act", f"wpo{e}")
                vcopy[(j1, half)] = b.pre("act").inc("act", nc.scalar.copy(
                    ev[e][:, half * 1024:(half + 1) * 1024], pp[half][:, :]))
            b.wait("sy", "act", vcopy[(j1, 1)])
            b.dma("sy", f"wpo{e}", w2tmp[j1 * 128:(j1 + 1) * 128, :], ev[e][:, :])
            if j1 + 3 < 64:
                b.wait("gp", "pe", vmm[j1])
                p1_load(j1 + 3)
        b.dwait("sy", "wpo0")
        b.dwait("sy", "wpo1")
        b.wait("gp", "pe")
        tmp_r = w2tmp.rearrange("(j1 k2) d -> k2 j1 d", k2=128)
        dd_r = w2dd.rearrange("(k1 k2) d -> k2 k1 d", k2=128)
        b.dwait("gp", "wpo0")
        b.dwait("gp", "wpo1")

        def p2_load(i):
            b.dma("gp", f"wps{i % 3}", blk[i % 3][:, :], tmp_r[2 * i:2 * i + 2])

        for i in range(3):
            p2_load(i)
        vcopy = {}
        vmm = {}
        for i in range(64):
            k = i % 3
            e = i % 2
            b.dwait("pe", f"wps{k}")
            for half in range(2):
                b.wait("pe", "act", vcopy.get((i - 1, half), 0))
                for g in range(2):
                    o = half * 1024 + g * 512
                    vp = b.inc("pe", nc.tensor.matmul(
                        pp[half][:, g * 512:(g + 1) * 512], consts["h64blk"][:, :],
                        blk[k][:, o:o + 512], start=True, stop=True))
                vmm[i] = vp
                b.wait("act", "pe", vp)
                if half == 0 and i >= 2:
                    b.dwait("act", f"wpd{e}")
                vcopy[(i, half)] = b.pre("act").inc("act", nc.scalar.activation(
                    ev[e][:, half * 1024:(half + 1) * 1024], pp[half][:, :],
                    AF.Copy, scale=FWHT_SCALE))
            b.wait("sy", "act", vcopy[(i, 1)])
            b.dma("sy", f"wpd{e}", dd_r[2 * i:2 * i + 2], ev[e][:, :])
            if i + 3 < 64:
                b.wait("gp", "pe", vmm[i])
                p2_load(i + 3)
        b.dwait("sy", "wpd0")
        b.dwait("sy", "wpd1")


def _load_band(b, w1t, w3t, d1, d3, band):
    sl = slice(band * 256, (band + 1) * 256)
    ds = f"fab{band % 2}"
    b.dma("gp", ds, d1[:, :].rearrange("dp (dc h) -> dp dc h", dc=NCH),
          w1t[:, sl].rearrange("(dc dp) h -> dp dc h", dp=128))
    return b.dma("gp", ds, d3[:, :].rearrange("dp (dc h) -> dp dc h", dc=NCH),
                 w3t[:, sl].rearrange("(dc dp) h -> dp dc h", dp=128))


def emit_ffn(b, xres_src, dst_hbm, w1t, w3t, w2dd, hid, consts, prep_args=None):
    nc = b.nc
    with ExitStack() as st:
        with ExitStack() as ast:
            xn = b.sb("fa_xn", [128, NCH * S], BF16, ast)
            prep_gen = None
            if prep_args is not None:
                prep_gen = w2_prep_steps(b, prep_args[0], prep_args[1],
                                         w2dd, consts, ast)
            w1b = [b.sb(f"fa_w1{i}", [128, NCH * 256], BF16, ast) for i in range(2)]
            w3b = [b.sb(f"fa_w3{i}", [128, NCH * 256], BF16, ast) for i in range(2)]
            sg = [b.sb(f"fa_sg{i}", [128, 1024], F32, ast) for i in range(2)]
            hbuf = [b.sb(f"fa_hb{i}", [128, 1024], BF16, ast) for i in range(2)]
            wload = [_load_band(b, w1t, w3t, w1b[0], w3b[0], 0),
                     _load_band(b, w1t, w3t, w1b[1], w3b[1], 1)]
            vxn = emit_ln(b, xres_src, xn, 0, S, consts)
            b.wait("pe", "dve", vxn)
            pg = [b.ps(f"fa_pg{i}", [128, 1024], F32, ast) for i in range(2)]
            pl = [b.ps("fa_pl", [128, 1024], F32, ast)]
            hseq = []
            for band in range(32):
                k = band % 2
                b.dwait("pe", f"fab{band % 2}")
                for hcl in range(2):
                    hc = band * 2 + hcl
                    for th in range(2):
                        idx = hc * 2 + th
                        q = idx % 2
                        if len(hseq) >= 2:
                            b.wait("pe", "act", hseq[-2][0])
                        xo0 = th * 1024
                        for dc in range(NCH):
                            l1 = w1b[k][:, dc * 256 + hcl * 128:
                                        dc * 256 + hcl * 128 + 128]
                            xb = dc * S + xo0
                            b.inc("pe", nc.tensor.matmul(
                                pg[q][:, 0:512], l1, xn[:, xb:xb + 512],
                                start=(dc == 0), stop=(dc == NCH - 1)))
                            vpg = b.inc("pe", nc.tensor.matmul(
                                pg[q][:, 512:1024], l1, xn[:, xb + 512:xb + 1024],
                                start=(dc == 0), stop=(dc == NCH - 1)))
                        if len(hseq) >= 2:
                            b.wait("act", "dve", hseq[-2][1])
                        b.wait("act", "pe", vpg)
                        va = b.pre("act").inc("act", nc.scalar.activation(
                            sg[q][:, :], pg[q][:, :], AF.Silu))
                        if hseq:
                            b.wait("pe", "dve", hseq[-1][1])
                        for dc in range(NCH):
                            l3 = w3b[k][:, dc * 256 + hcl * 128:
                                        dc * 256 + hcl * 128 + 128]
                            xb = dc * S + xo0
                            b.inc("pe", nc.tensor.matmul(
                                pl[0][:, 0:512], l3, xn[:, xb:xb + 512],
                                start=(dc == 0), stop=(dc == NCH - 1)))
                            vp = b.inc("pe", nc.tensor.matmul(
                                pl[0][:, 512:1024], l3, xn[:, xb + 512:xb + 1024],
                                start=(dc == 0), stop=(dc == NCH - 1)))
                        b.dwait("dve", f"fah{q}")
                        b.wait("dve", "act", va)
                        b.wait("dve", "pe", vp)
                        vd = b.pre("dve").inc("dve", nc.vector.tensor_mul(
                            hbuf[q][:, :], sg[q][:, :], pl[0][:, :]))
                        b.wait("sy", "dve", vd)
                        vo = b.dma("sy", f"fah{q}", hid.ap()[th, hc], hbuf[q][:, :])
                        if prep_gen is not None:
                            next(prep_gen, None)
                        hseq.append((va, vd, vo))
                if band < 30:
                    b.wait("gp", "pe")
                    wload.append(_load_band(b, w1t, w3t, w1b[k], w3b[k], band + 2))
            b.dwait("sy", "fah0")
            b.dwait("sy", "fah1")
            if prep_gen is not None:
                for _ in prep_gen:
                    pass
                b.dwait("sy", "wpd0")
                b.dwait("sy", "wpd1")
        b.bar()
        with ExitStack() as bst:
            hstr = b.sb("fb_h", [128, NHID * 1024], BF16, bst)
            w2s = [b.sb(f"fb_w2{i}", [128, 256], BF16, bst) for i in range(8)]
            xr = [b.sb(f"fb_xr{i}", [128, 1024], F32, bst) for i in range(2)]
            ox = [b.sb(f"fb_ox{i}", [128, 1024], F32, bst) for i in range(2)]
            pb = [[b.ps(f"fb_p{i}{j}", [128, 1024], F32, bst) for j in range(2)]
                  for i in range(2)]
            hs_r = hstr[:, :].rearrange("dp (hc t) -> dp hc t", hc=NHID)
            for ts in range(2):
                if ts > 0:
                    b.wait("sy", "pe")
                    b.wait("ac", "pe")
                for part in range(8):
                    qn = "sy" if part % 2 == 0 else "ac"
                    b.dma(qn, "fbh" + qn, hs_r[:, part * 8:(part + 1) * 8],
                          hid.ap()[ts, part * 8:(part + 1) * 8].rearrange(
                              "hc dp t -> dp hc t"))
                pv = {}
                b.wait("ac", "pe")
                for pre in range(6):
                    b.dma("ac", f"fbw{pre}", w2s[pre][:, :],
                          w2dd[pre * 128:(pre + 1) * 128, 0:256])
                b.dwait("pe", "fbhsy")
                b.dwait("pe", "fbhac")
                ev = []
                evx = []
                for db in range(8):
                    q = db % 2
                    if db >= 2:
                        b.wait("pe", "dve", ev[db - 2][0])
                    for h in range(NHID):
                        idx = db * NHID + h
                        b.dwait("pe", f"fbw{idx % 8}")
                        ws = w2s[idx % 8]
                        for ds in range(2):
                            lh = ws[:, ds * 128:(ds + 1) * 128]
                            for th in range(2):
                                vp = b.inc("pe", nc.tensor.matmul(
                                    pb[q][ds][:, th * 512:(th + 1) * 512], lh,
                                    hstr[:, h * 1024 + th * 512:h * 1024 + th * 512 + 512],
                                    start=(h == 0), stop=(h == NHID - 1)))
                        pv[idx] = vp
                        nidx = idx + 6
                        if nidx < 8 * NHID:
                            ndb, nh = divmod(nidx, NHID)
                            if nidx >= 8:
                                b.wait("ac", "pe", pv[nidx - 8])
                            b.dma("ac", f"fbw{nidx % 8}", w2s[nidx % 8][:, :],
                                  w2dd[nh * 128:(nh + 1) * 128,
                                       ndb * 256:(ndb + 1) * 256])
                    for ds in range(2):
                        r0 = db * 256 + ds * 128
                        kx = (db * 2 + ds) % 2
                        if len(evx) >= 2:
                            b.wait("sy", "dve", evx[-2][0])
                        b.dma("sy", f"fbx{kx}", xr[kx][:, :],
                              xres_src[r0:r0 + 128, ts * 1024:(ts + 1) * 1024])
                        if len(evx) >= 2:
                            b.dwait("dve", f"fbo{kx}")
                        b.dwait("dve", f"fbx{kx}")
                        b.wait("dve", "pe", vp)
                        vd = b.pre("dve").inc("dve", nc.vector.tensor_add(
                            ox[kx][:, :], pb[q][ds][:, :], xr[kx][:, :]))
                        b.wait("sy", "dve", vd)
                        vo = b.dma("sy", f"fbo{kx}", dst_hbm[r0:r0 + 128,
                                   ts * 1024:(ts + 1) * 1024], ox[kx][:, :])
                        evx.append((vd, vo))
                    ev.append((vd, vo))
                b.dwait("sy", "fbo0")
                b.dwait("sy", "fbo1")
        b.bar()


def w2_prep_steps(b, w2t, w2tmp, w2dd, consts, st):
    nc = b.nc
    blk = [b.sb(f"wp_blk{i}", [128, D], BF16, st) for i in range(3)]
    ev = [b.sb(f"wp_ev{i}", [128, D], BF16, st) for i in range(2)]
    pp = [b.ps(f"wp_p{i}", [128, 512], F32, st) for i in range(2)]

    def p1_load(j1):
        b.dma("gp", f"wpg{j1 % 3}", blk[j1 % 3][:, :],
              w2t[j1 * 128:(j1 + 1) * 128, :])

    for j in range(3):
        p1_load(j)
    return _w2_gen(b, w2t, w2tmp, w2dd, consts, blk, ev, pp, p1_load)


def _w2_gen(b, w2t, w2tmp, w2dd, consts, blk, ev, pp, p1_load):
    nc = b.nc

    def one_pass(load_stream, out_stream, hmat, scale, dst_r, loader):
        vq = {}
        vmm = {}
        for it in range(64):
            k = it % 3
            e = it % 2
            b.dwait("pe", f"{load_stream}{k}")
            for qr in range(4):
                p = qr % 2
                eng = "act" if p == 0 else "dve"
                b.wait("pe", eng, vq.get(qr - 2 if qr >= 2 else (2 + qr, it - 1), 0)
                       if False else vq.get((it, qr - 2), vq.get((it - 1, qr + 2), 0)))
                vp = b.inc("pe", nc.tensor.matmul(
                    pp[p][:, :], hmat[:, :],
                    blk[k][:, qr * 512:(qr + 1) * 512], start=True, stop=True))
                if qr == 3:
                    vmm[it] = vp
                b.wait(eng, "pe", vp)
                if qr < 2 and it >= 2:
                    b.dwait(eng, f"{out_stream}{e}")
                dsl = ev[e][:, qr * 512:(qr + 1) * 512]
                if eng == "act":
                    vq[(it, qr)] = b.pre("act").inc("act", nc.scalar.activation(
                        dsl, pp[p][:, :], AF.Copy, scale=scale))
                else:
                    if scale != 1.0:
                        vq[(it, qr)] = b.pre("dve").inc(
                            "dve", nc.vector.tensor_scalar_mul(dsl, pp[p][:, :], scale))
                    else:
                        vq[(it, qr)] = b.pre("dve").inc(
                            "dve", nc.vector.tensor_copy(dsl, pp[p][:, :]))
            b.wait("sy", "act", vq[(it, 2)])
            b.wait("sy", "dve", vq[(it, 3)])
            b.dma("sy", f"{out_stream}{e}", dst_r(it), ev[e][:, :])
            if it + 3 < 64:
                b.wait("gp", "pe", vmm[it])
                loader(it + 3)
            yield

    yield from one_pass("wpg", "wpo", consts["h128"], 1.0,
                        lambda it: w2tmp[it * 128:(it + 1) * 128, :], p1_load)
    tmp_r = w2tmp.rearrange("(j1 k2) d -> k2 j1 d", k2=128)
    dd_r = w2dd.rearrange("(k1 k2) d -> k2 k1 d", k2=128)
    b.dwait("gp", "wpo0")
    b.dwait("gp", "wpo1")
    b.wait("gp", "pe")

    def p2_load(i):
        b.dma("gp", f"wps{i % 3}", blk[i % 3][:, :], tmp_r[2 * i:2 * i + 2])

    for i in range(3):
        p2_load(i)
    yield from one_pass("wps", "wpd", consts["h64blk"], FWHT_SCALE,
                        lambda it: dd_r[2 * it:2 * it + 2], p2_load)


def emit_attn_a(b, x_in, xres, consts, prep_gen=None):
    nc = b.nc
    chl = [[h for h in range(NHEADS) if h % 2 == p] for p in range(PAR)]
    slots = [(par, sl) for par in range(PAR) for sl in range(8)]
    for seg in range(NSEGS):
        c0 = seg * SEG
        with ExitStack() as st:
            xf = b.sb("aa_xf", [128, NCH * SEG], F32, st)
            xnp = [b.sb(f"aa_xnp{p}", [128, 8 * SEGH], BF16, st) for p in range(PAR)]
            gt = [b.sb(f"aa_gt{i}", [128, 2 * SEGH], BF16, st) for i in range(2)]
            et = [[b.sb(f"aa_et{i}{jb}", [128, SEGH], BF16, st) for jb in range(2)]
                  for i in range(2)]
            avs = [b.sb(f"aa_avs{i}", [128, SEGH], F32, st) for i in range(2)]
            rzr = [b.sb(f"aa_rz{i}", [1, SEGH], F32, st) for i in range(2)]
            an = b.sb("aa_an", [128, SEGH], F32, st)
            for dc in range(NCH):
                b.dma("sy", "aax", xf[:, dc * SEG:(dc + 1) * SEG],
                      x_in[dc * 128:(dc + 1) * 128, c0:c0 + SEG])
            vxn = emit_ln(b, x_in, None, c0, SEG, consts,
                          par_split=(xnp, chl), src_sbuf=(xf, "aax"))
            b.wait("pe", "dve", vxn)
            b.wait("sy", "dve", vxn)
            with ExitStack() as pst:
                psc = [b.ps(f"aa_psc{i}", [128, SEGH], F32, pst) for i in range(2)]
                pz = [b.ps(f"aa_pz{i}", [1, SEGH], F32, pst) for i in range(2)]
                pav = [b.ps(f"aa_pav{i}", [128, SEGH], F32, pst) for i in range(2)]
                prz = [b.ps(f"aa_prz{i}", [128, SEGH], F32, pst) for i in range(2)]
                stt = {"vas": {}, "vzav": {}, "vrz": [0, 0], "vavs": [0, 0],
                       "vpb": [0, 0], "vtail": [0, 0], "tail": 0}

                def stage1(s):
                    par, sl = slots[s]
                    g = xnp[par][:, sl * SEGH:(sl + 1) * SEGH]
                    sb = s % 2
                    if s >= 2:
                        b.wait("sy", "pe", stt["vzav"].get(s - 2, 0))
                    for jb in range(2):
                        b.dma("sy", f"agt{sb}", gt[sb][:, jb * 128:(jb + 1) * 128],
                              g[:, jb * 128:(jb + 1) * 128], transpose=True)
                    for jb in range(2):
                        b.wait("pe", "act", stt["vas"].get((s - 1, jb), 0))
                        vp = b.inc("pe", nc.tensor.matmul(
                            psc[jb][:, :], g[:, jb * 128:(jb + 1) * 128], g,
                            start=True, stop=True))
                        b.wait("act", "pe", vp)
                        if s >= 2:
                            b.wait("act", "pe", stt["vzav"].get(s - 2, 0))
                        stt["vas"][(s, jb)] = b.pre("act").inc(
                            "act", nc.scalar.activation(
                                et[sb][jb][:, :], psc[jb][:, :], AF.Exp,
                                bias=consts["shift"][:, :], scale=INV_SQRT_HD))

                def zav_recip(s):
                    par, sl = slots[s]
                    sb = s % 2
                    e = s % 2
                    b.wait("pe", "act", stt["vrz"][e])
                    b.wait("pe", "act", stt["vavs"][e])
                    b.dwait("pe", f"agt{sb}")
                    for jb in range(2):
                        b.wait("pe", "act", stt["vas"][(s, jb)])
                        b.inc("pe", nc.tensor.matmul(
                            pz[e][:, :], consts["ones_bf"][:, :], et[sb][jb][:, :],
                            start=(jb == 0), stop=(jb == 1)))
                        vp = b.inc("pe", nc.tensor.matmul(
                            pav[e][:, :], gt[sb][:, jb * 128:(jb + 1) * 128],
                            et[sb][jb][:, :], start=(jb == 0), stop=(jb == 1)))
                    stt["vzav"][s] = vp
                    b.wait("act", "pe", vp)
                    if stt["vpb"][e]:
                        b.wait("act", "pe", stt["vpb"][e])
                    b.pre("act").inc("act", nc.scalar.activation(
                        rzr[e][:, :], pz[e][:, :], AF.Ln))
                    stt["vrz"][e] = b.pre("act").inc("act", nc.scalar.activation(
                        rzr[e][:, :], rzr[e][:, :], AF.Exp, scale=-1.0))

                def tail(s):
                    par, sl = slots[s]
                    h = chl[par][sl]
                    e = s % 2
                    b.wait("pe", "act", stt["vrz"][e])
                    b.wait("pe", "dve", stt["vtail"][e])
                    stt["vpb"][e] = b.inc("pe", nc.tensor.matmul(
                        prz[e][:, :], consts["ones_f32"][:, :], rzr[e][:, :],
                        start=True, stop=True))
                    b.wait("act", "dve", stt["vtail"][e])
                    b.wait("act", "pe", stt["vzav"][s])
                    va = stt["vavs"][e] = b.pre("act").inc(
                        "act", nc.scalar.copy(avs[e][:, :], pav[e][:, :]))
                    b.wait("dve", "act", va)
                    b.wait("dve", "pe", stt["vpb"][e])
                    b.dwait("dve", "aax")
                    b.pre("dve").inc("dve", nc.vector.tensor_mul(
                        an[:, :], avs[e][:, :], prz[e][:, :]))
                    xsl = xf[:, h * SEG + par:(h + 1) * SEG:PAR]
                    stt["vtail"][e] = stt["tail"] = b.pre("dve").inc(
                        "dve", nc.vector.tensor_add(xsl, xsl, an[:, :]))

                stage1(0)
                stage1(1)
                for s in range(16):
                    zav_recip(s)
                    if s + 2 < 16:
                        stage1(s + 2)
                    if s >= 1:
                        tail(s - 1)
                tail(15)
            b.wait("sy", "dve")
            for dc in range(NCH):
                b.dma("sy", "aar", xres[dc * 128:(dc + 1) * 128, c0:c0 + SEG],
                      xf[:, dc * SEG:(dc + 1) * SEG])
            b.dwait("sy", "aar")
        b.bar()


def emit_attn_b(b, xres, wqt, wkt, wvt, wot, consts):
    nc = b.nc
    with ExitStack() as st:
        ao = b.sb("ab_ao", [128, NCH * S], BF16, st)
        with ExitStack() as hst:
            xn = b.sb("ab_xn", [128, NCH * S], BF16, hst)
            vxn = emit_ln(b, xres, xn, 0, S, consts)
            b.wait("pe", "dve", vxn)
            b.wait("gp", "dve", vxn)
            b.wait("gp", "act")
            qw = [b.sb(f"ab_qw{i}", [128, NCH * 128], BF16, hst) for i in range(2)]
            kw = [b.sb(f"ab_kw{i}", [128, NCH * 128], BF16, hst) for i in range(2)]
            vw = [b.sb(f"ab_vw{i}", [128, NCH * 128], BF16, hst) for i in range(2)]
            qh = [b.sb(f"ab_qh{i}", [128, S], BF16, hst) for i in range(2)]
            kh = [b.sb(f"ab_kh{i}", [128, S], BF16, hst) for i in range(2)]
            vh = [b.sb(f"ab_vh{i}", [128, S], BF16, hst) for i in range(2)]
            vt = [b.sb(f"ab_vt{i}", [128, S], BF16, hst) for i in range(2)]
            et = [b.sb(f"ab_et{i}", [128, 512], BF16, hst) for i in range(2)]
            avs = [b.sb(f"ab_avs{i}", [128, 512], F32, hst) for i in range(2)]
            rzr = [b.sb(f"ab_rz{i}", [1, 512], F32, hst) for i in range(2)]

            def load_head_w(h):
                for wt, dst in ((wqt, qw), (wkt, kw), (wvt, vw)):
                    b.dma("gp", f"abw{h % 2}", dst[h % 2][:, :].rearrange(
                        "dp (dc o) -> dp dc o", dc=NCH),
                        wt[:, h * 128:(h + 1) * 128].rearrange(
                            "(dc dp) o -> dp dc o", dp=128))

            load_head_w(0)
            load_head_w(1)
            with ExitStack() as pst:
                pq = [b.ps("ab_pq", [128, 512], F32, pst)]
                psc = [b.ps(f"ab_psc{i}", [128, 512], F32, pst) for i in range(2)]
                pz = [b.ps(f"ab_pz{i}", [1, 512], F32, pst) for i in range(2)]
                pav = [b.ps(f"ab_pav{i}", [128, 512], F32, pst) for i in range(2)]
                prz = b.ps("ab_prz", [128, 512], F32, pst)
                stt = {"pq_prev": [], "vtr": 0, "qv": 0, "pe_proj": {},
                       "ready": {}, "vhva": {}}

                def qkv_units(h):
                    k = h % 2

                    def mk_proj(di, tg):
                        def emit():
                            dst = (qh[k], kh[k], vh[k])[di]
                            wsb = (qw[k], kw[k], vw[k])[di]
                            q = 0
                            b.dwait("pe", f"abw{k}")
                            if stt["pq_prev"]:
                                b.wait("pe", "dve", stt["pq_prev"][-1])
                            for dc in range(NCH):
                                vp = b.inc("pe", nc.tensor.matmul(
                                    pq[q][:, :], wsb[:, dc * 128:(dc + 1) * 128],
                                    xn[:, dc * S + tg * 512:dc * S + tg * 512 + 512],
                                    start=(dc == 0), stop=(dc == NCH - 1)))
                            if di == 2 and tg == 3:
                                stt["pe_proj"][h] = vp
                            b.wait("dve", "pe", vp)
                            va = b.pre("dve").inc("dve", nc.vector.tensor_copy(
                                dst[:, tg * 512:(tg + 1) * 512], pq[q][:, :]))
                            stt["pq_prev"].append(va)
                            if di == 2:
                                stt["vhva"][(h, tg)] = va
                        return emit

                    def mk_vt(j):
                        def emit():
                            b.wait("sy", "dve", stt["vhva"][(h, j // 4)])
                            if j == 0:
                                b.wait("sy", "pe")
                            b.dma("sy", f"vtr{k}", vt[k][:, j * 128:(j + 1) * 128],
                                  vh[k][:, j * 128:(j + 1) * 128], transpose=True)
                            if j == 15:
                                stt["ready"][h] = stt["pq_prev"][-1]
                        return emit

                    return ([mk_proj(di, tg) for di in range(3) for tg in range(4)]
                            + [mk_vt(j) for j in range(16)])

                for u in qkv_units(0):
                    u()
                vd_tail = 0
                vd_tails = [0, 0]
                vpb_prev = 0
                vrz_prev = [0, 0]
                vavs_prev = [0, 0]
                for h in range(NHEADS):
                    k = h % 2
                    units = qkv_units(h + 1) if h + 1 < NHEADS else []
                    nslots = 44
                    slot = [0, 0]

                    def tick():
                        slot[0] += 1
                        tgt = (slot[0] * len(units)) // nslots
                        while slot[1] < min(tgt, len(units)):
                            units[slot[1]]()
                            slot[1] += 1

                    if h + 2 < NHEADS:
                        b.wait("gp", "pe", stt["pe_proj"][h])
                        load_head_w(h + 2)
                    b.dwait("pe", f"vtr{k}")
                    b.wait("pe", "dve", stt["ready"][h])
                    for ig in range(4):
                        jmax = 4 * ig + 3
                        vas = {}
                        vzav = {}

                        e = (h * 4 + ig) % 2

                        def emit_zav(j, start, stop):
                            q = j % 2
                            osj = max(0, j * 128 - ig * 512)
                            b.wait("pe", "act", vas[j])
                            b.inc("pe", nc.tensor.matmul(
                                pz[e][:, osj:512], consts["ones_bf"][:, :],
                                et[q][:, osj:512], start=start, stop=stop))
                            vzav[j] = b.inc("pe", nc.tensor.matmul(
                                pav[e][:, osj:512], vt[k][:, j * 128:(j + 1) * 128],
                                et[q][:, osj:512], start=start, stop=stop))

                        b.wait("pe", "dve", vrz_prev[e])
                        b.wait("pe", "act", vavs_prev[e])
                        for j in range(jmax + 1):
                            osj = max(0, j * 128 - ig * 512)
                            q = j % 2
                            if j >= 2:
                                b.wait("pe", "act", vas[j - 2])
                            vp = b.inc("pe", nc.tensor.matmul(
                                psc[q][:, osj:512], kh[k][:, j * 128:(j + 1) * 128],
                                qh[k][:, ig * 512 + osj:(ig + 1) * 512],
                                start=True, stop=True))
                            vdm = 0
                            if j * 128 >= ig * 512:
                                b.wait("dve", "pe", vp)
                                vdm = b.pre("dve").inc("dve", nc.vector.tensor_add(
                                    psc[q][:, osj:osj + 128],
                                    psc[q][:, osj:osj + 128], consts["dmask"][:, :]))
                            if j >= 1:
                                emit_zav(j - 1, j - 1 == 0, False)
                            tick()
                            b.wait("act", "pe", vp)
                            if vdm:
                                b.wait("act", "dve", vdm)
                            if j >= 2:
                                b.wait("act", "pe", vzav[j - 2])
                            vas[j] = b.pre("act").inc("act", nc.scalar.activation(
                                et[q][:, osj:512], psc[q][:, osj:512], AF.Exp,
                                bias=consts["shift"][:, :], scale=INV_SQRT_HD))
                        emit_zav(jmax, jmax == 0, True)
                        vp = vzav[jmax]
                        b.wait("dve", "pe", vp)
                        if vpb_prev:
                            b.wait("dve", "pe", vpb_prev)
                        vrz = vrz_prev[e] = b.pre("dve").inc(
                            "dve", nc.vector.reciprocal(rzr[e][:, :], pz[e][:, :]))
                        tick()
                        b.wait("pe", "dve", vrz)
                        b.wait("pe", "dve", vd_tail)
                        vpb_prev = b.inc("pe", nc.tensor.matmul(
                            prz[:, :], consts["ones_f32"][:, :], rzr[e][:, :],
                            start=True, stop=True))
                        b.wait("act", "dve", vd_tails[e])
                        b.wait("act", "pe", vp)
                        va = vavs_prev[e] = b.pre("act").inc("act", nc.scalar.copy(
                            avs[e][:, :], pav[e][:, :]))
                        b.wait("dve", "act", va)
                        b.wait("dve", "pe", vpb_prev)
                        vd_tail = vd_tails[e] = b.pre("dve").inc(
                            "dve", nc.vector.tensor_mul(
                                ao[:, h * S + ig * 512:h * S + (ig + 1) * 512],
                                avs[e][:, :], prz[:, :]))
                    for u in units[slot[1]:]:
                        u()
                    slot[1] = len(units)
                b.wait("pe", "dve", vd_tail)
        b.bar()
        with ExitStack() as wst:
            wob = [b.sb(f"ab_wo{i}", [128, NCH * 128], BF16, wst) for i in range(3)]
            xr = [b.sb(f"ab_xr{i}", [128, 512], F32, wst) for i in range(3)]
            oxt = [b.sb(f"ab_ox{i}", [128, 512], F32, wst) for i in range(3)]
            pwo = [b.ps(f"ab_pw{i}", [128, 512], F32, wst) for i in range(3)]

            def load_wo(do):
                return b.dma(
                    "gp", f"awo{do % 3}",
                    wob[do % 3][:, :].rearrange("dp (dc o) -> dp dc o", dc=NCH),
                    wot[:, do * 128:(do + 1) * 128].rearrange(
                        "(dc dp) o -> dp dc o", dp=128))

            load_wo(0)
            load_wo(1)
            pv_wo = {}
            seq = []
            for do in range(NCH):
                k = do % 3
                b.dwait("pe", f"awo{do % 3}")
                if do + 2 < NCH:
                    if do >= 1:
                        b.wait("gp", "pe", pv_wo[do - 1])
                    load_wo(do + 2)
                for tg in range(4):
                    idx = do * 4 + tg
                    q = idx % 3
                    if idx >= 3:
                        b.wait("pe", "dve", seq[idx - 3][0])
                    for dc in range(NCH):
                        vp = b.inc("pe", nc.tensor.matmul(
                            pwo[q][:, :], wob[k][:, dc * 128:(dc + 1) * 128],
                            ao[:, dc * S + tg * 512:dc * S + tg * 512 + 512],
                            start=(dc == 0), stop=(dc == NCH - 1)))
                    if idx >= 3:
                        b.wait("ac", "dve", seq[idx - 3][0])
                    b.dma("ac", f"abx{q}", xr[q][:, :],
                          xres[do * 128:(do + 1) * 128, tg * 512:(tg + 1) * 512])
                    if idx >= 3:
                        b.dwait("dve", f"abo{q}")
                    b.dwait("dve", f"abx{q}")
                    b.wait("dve", "pe", vp)
                    vd = b.pre("dve").inc("dve", nc.vector.tensor_add(
                        oxt[q][:, :], pwo[q][:, :], xr[q][:, :]))
                    b.wait("sy", "dve", vd)
                    vo = b.dma("sy", f"abo{q}", xres[do * 128:(do + 1) * 128,
                               tg * 512:(tg + 1) * 512], oxt[q][:, :])
                    pv_wo[do] = vp
                    seq.append((vd, vo))
            b.dwait("sy", "abo0")
            b.dwait("sy", "abo1")
            b.dwait("sy", "abo2")
        b.bar()


def build_nc(phases=("attn_a", "prep", "ffn1", "attn_b", "ffn2")):
    nc = bass.Bass("TRN2", target_bir_lowering=False)
    x_in = nc.dram_tensor("x", [D, S], F32, kind="ExternalInput").ap()
    w1t = nc.dram_tensor("w1t", [D, HID], F32, kind="ExternalInput").ap()
    w3t = nc.dram_tensor("w3t", [D, HID], F32, kind="ExternalInput").ap()
    w2t = nc.dram_tensor("w2t", [HID, D], F32, kind="ExternalInput").ap()
    wqt = nc.dram_tensor("wqt", [D, D], F32, kind="ExternalInput").ap()
    wkt = nc.dram_tensor("wkt", [D, D], F32, kind="ExternalInput").ap()
    wvt = nc.dram_tensor("wvt", [D, D], F32, kind="ExternalInput").ap()
    wot = nc.dram_tensor("wot", [D, D], F32, kind="ExternalInput").ap()
    out = nc.dram_tensor("out", [D, S], F32, kind="ExternalOutput").ap()
    xres = nc.dram_tensor("xres", [D, S], F32, kind="Internal").ap()
    w2tmp = nc.dram_tensor("w2tmp", [HID, D], BF16, kind="Internal").ap()
    w2dd = nc.dram_tensor("w2dd", [HID, D], BF16, kind="Internal").ap()
    hid = nc.dram_tensor("hid", [2, NHID, 128, 1024], BF16, kind="Internal")

    with ExitStack() as st:
        b = Bld(nc, st)
        consts = emit_consts(b)
        if "attn_a" in phases:
            emit_attn_a(b, x_in, xres, consts)
        if "prep" in phases and "ffn1" not in phases:
            with ExitStack() as pst:
                for _ in w2_prep_steps(b, w2t, w2tmp, w2dd, consts, pst):
                    pass
                b.dwait("sy", "wpd0")
                b.dwait("sy", "wpd1")
            b.bar()
        if "ffn1" in phases:
            pa = (w2t, w2tmp) if "prep" in phases else None
            emit_ffn(b, xres, xres, w1t, w3t, w2dd, hid, consts, prep_args=pa)
        if "attn_b" in phases:
            emit_attn_b(b, xres, wqt, wkt, wvt, wot, consts)
        if "ffn2" in phases:
            emit_ffn(b, xres, out, w1t, w3t, w2dd, hid, consts)
        b.dwait("sy", "fbo0")
        b.dwait("sy", "fbo1")
    return nc


def kernel(L0_attn_a_x, L0_attn_b_wq, L0_attn_b_wk, L0_attn_b_wv, L0_attn_b_wo,
           L0_ffn_W1, L0_ffn_W2, L0_ffn_W3):
    x = np.asarray(L0_attn_a_x, dtype=np.float32)
    wshared = {
        "w1t": np.ascontiguousarray(np.asarray(L0_ffn_W1, np.float32).T),
        "w3t": np.ascontiguousarray(np.asarray(L0_ffn_W3, np.float32).T),
        "w2t": np.ascontiguousarray(np.asarray(L0_ffn_W2, np.float32).T),
        "wqt": np.ascontiguousarray(np.asarray(L0_attn_b_wq, np.float32).T),
        "wkt": np.ascontiguousarray(np.asarray(L0_attn_b_wk, np.float32).T),
        "wvt": np.ascontiguousarray(np.asarray(L0_attn_b_wv, np.float32).T),
        "wot": np.ascontiguousarray(np.asarray(L0_attn_b_wo, np.float32).T),
    }
    nc = build_nc()
    in_maps = [dict(wshared, x=np.ascontiguousarray(x[i].T)) for i in range(8)]
    res = run_bass_kernel_spmd(nc, in_maps, core_ids=list(range(8)))
    outs = [np.ascontiguousarray(np.asarray(r["out"]).T) for r in res.results]
    return np.stack(outs, axis=0)


if __name__ == "__main__":
    nc = build_nc()
    print("instructions:", nc.n_instructions())


# version 15
# speedup vs baseline: 1.0667x; 1.0031x over previous
import numpy as np
from contextlib import ExitStack

import concourse.bass as bass
import concourse.mybir as mybir
from concourse.bass_utils import run_bass_kernel_spmd
from concourse import dve_ops

F32 = mybir.dt.float32
BF16 = mybir.dt.bfloat16
AF = mybir.ActivationFunctionType
ALU = mybir.AluOpType

D = 2048
S = 2048
HID = 8192
NCH = D // 128
NHID = HID // 128
NHEADS = 16
SEG = 512
NSEGS = S // SEG
PAR = 2
SEGH = SEG // PAR
INV_SQRT_HD = 0.08838834764831845
EXP_SHIFT = -15.0
LN_EPS = 1e-5
FWHT_SCALE = float(1.0 / np.sqrt(HID))


def _hadamard(n):
    h = np.array([[1.0]])
    while h.shape[0] < n:
        h = np.block([[h, h], [h, -h]])
    return h


def _bf16_u16(a):
    import ml_dtypes
    return np.ascontiguousarray(np.asarray(a, dtype=ml_dtypes.bfloat16).view(np.uint16))


class Bld:

    def __init__(self, nc, st):
        self.nc = nc
        self.st = st
        self.sems = {}
        self.val = {}
        self.waited = {}
        for name in ("pe", "act", "dve", "bar"):
            self.sems[name] = st.enter_context(nc.semaphore(name))
            self.val[name] = 0
        self.streams = {}
        self.engs = {
            "pe": nc.tensor, "act": nc.scalar, "dve": nc.vector,
            "gp": nc.gpsimd, "sy": nc.sync,
        }
        self.alias = {"ac": "act"}

    def inc(self, name, ins, n=1):
        ins.then_inc(self.sems[name], n)
        self.val[name] += n
        return self.val[name]

    def pre(self, name):
        self.wait(name, name)
        return self

    def wait(self, eng_name, sem_name, v=None):
        v = self.val[sem_name] if v is None else v
        if v <= 0:
            return
        eng_name = self.alias.get(eng_name, eng_name)
        key = (eng_name, sem_name)
        if self.waited.get(key, -1) < v:
            self.engs[eng_name].wait_ge(self.sems[sem_name], v)
            self.waited[key] = v

    def ds(self, name):
        if name not in self.streams:
            self.streams[name] = [self.st.enter_context(self.nc.semaphore(name)), 0]
        return self.streams[name]

    def dma(self, q, stream, out, in_, **kw):
        s = self.ds(stream)
        ins = self.engs[self.alias.get(q, q)].dma_start(out, in_, **kw)
        ins.then_inc(s[0], 16)
        s[1] += 16
        return s[1]

    def dwait(self, eng_name, stream, v=None):
        s = self.ds(stream)
        v = s[1] if v is None else v
        if v <= 0:
            return
        eng_name = self.alias.get(eng_name, eng_name)
        key = (eng_name, "D" + stream)
        if self.waited.get(key, -1) < v:
            self.engs[eng_name].wait_ge(s[0], v)
            self.waited[key] = v

    def bar(self):
        self.val["bar"] += len(self.engs)
        tgt = self.val["bar"]
        for e in self.engs.values():
            e.sem_inc(self.sems["bar"], 1)
        for name, e in self.engs.items():
            e.wait_ge(self.sems["bar"], tgt)
            self.waited[(name, "bar")] = tgt
            for s in ("pe", "act", "dve"):
                k = (name, s)
                self.waited[k] = max(self.waited.get(k, -1), self.val[s])

    _uid = 0

    def sb(self, name, shape, dt, st=None):
        Bld._uid += 1
        return (st or self.st).enter_context(
            self.nc.sbuf_tensor(f"{name}_{Bld._uid}", shape, dt))

    def ps(self, name, shape, dt=F32, st=None):
        Bld._uid += 1
        return (st or self.st).enter_context(
            self.nc.psum_tensor(f"{name}_{Bld._uid}", shape, dt))


def emit_consts(b):
    nc = b.nc
    h128 = _hadamard(128)
    h64 = _hadamard(64)
    h64blk = np.zeros((128, 128))
    h64blk[:64, :64] = h64
    h64blk[64:, 64:] = h64
    ident = np.eye(128)
    jj, ii = np.meshgrid(np.arange(128), np.arange(128), indexing="ij")
    dmask = np.where(jj <= ii, 0.0, -1e30).astype(np.float32)

    consts = {
        "h128": b.sb("c_h128", [128, 128], BF16),
        "h64blk": b.sb("c_h64b", [128, 128], BF16),
        "ident": b.sb("c_id", [128, 128], BF16),
        "dmask": b.sb("c_dmask", [128, 128], F32),
        "ones_bf": b.sb("c_onesb", [128, 1], BF16),
        "ones_f32": b.sb("c_onesf", [1, 128], F32),
        "eps": b.sb("c_eps", [128, 1], F32),
        "shift": b.sb("c_shift", [128, 1], F32),
    }
    d_h128 = nc.inline_tensor(_bf16_u16(h128), "i_h128")
    d_h64b = nc.inline_tensor(_bf16_u16(h64blk), "i_h64b")
    d_id = nc.inline_tensor(_bf16_u16(ident), "i_id")
    d_mask = nc.inline_tensor(dmask, "i_dmask")
    b.dma("sy", "cst", consts["h128"][:, :], d_h128.ap().bitcast(BF16))
    b.dma("sy", "cst", consts["h64blk"][:, :], d_h64b.ap().bitcast(BF16))
    b.dma("sy", "cst", consts["ident"][:, :], d_id.ap().bitcast(BF16))
    b.dma("sy", "cst", consts["dmask"][:, :], d_mask.ap())
    b.pre("dve").inc("dve", nc.vector.memset(consts["ones_bf"][:, :], 1.0))
    b.pre("dve").inc("dve", nc.vector.memset(consts["ones_f32"][:, :], 1.0))
    b.pre("dve").inc("dve", nc.vector.memset(consts["eps"][:, :], LN_EPS))
    b.pre("dve").inc("dve", nc.vector.memset(consts["shift"][:, :], EXP_SHIFT))
    b.dwait("pe", "cst")
    b.wait("pe", "dve")
    b.dwait("dve", "cst")
    b.dwait("act", "cst")
    return consts


def emit_ln(b, src_hbm, xn, cols0, ncols, consts, par_split=None, src_sbuf=None):
    nc = b.nc
    onesb, onesf = consts["ones_bf"], consts["ones_f32"]
    ngr = (ncols + 511) // 512
    with ExitStack() as st:
        xbf = xn if par_split is None else b.sb("ln_xbf", [128, NCH * ncols], BF16, st)
        rows = b.sb("ln_rows", [1, 4 * ncols], F32, st)
        ab = b.sb("ln_ab", [128, 2 * ncols], F32, st)
        scr = ExitStack()
        xf = [b.sb(f"ln_xf{i}", [128, ncols], F32, scr) for i in range(2)]
        m_r = rows[:, 0:ncols]
        rs_r = rows[:, ncols:2 * ncols]
        bm_r = rows[:, 2 * ncols:3 * ncols]
        sc_r = rows[:, 3 * ncols:4 * ncols]
        with ExitStack() as pst:
            zx = b.ps("ln_zx", [1, ncols], F32, pst)
            vd = 0
            vp = 0
            for dc in range(NCH):
                if src_sbuf is not None:
                    sb_t, sb_stream = src_sbuf
                    b.dwait("dve", sb_stream)
                    buf = sb_t[:, dc * ncols:(dc + 1) * ncols]
                else:
                    buf = xf[dc % 2]
                    if dc >= 2:
                        b.wait("sy", "dve", ld2[0])
                    b.dma("sy", f"lnx{dc % 2}", buf[:, :],
                          src_hbm[dc * 128:(dc + 1) * 128, cols0:cols0 + ncols])
                    b.dwait("dve", f"lnx{dc % 2}")
                    buf = buf[:, :]
                    ld2 = (vd,)
                vd = b.pre("dve").inc("dve", nc.vector.tensor_copy(
                    xbf[:, dc * ncols:(dc + 1) * ncols], buf))
                b.wait("pe", "dve", vd)
                for g in range(ngr):
                    w = min(512, ncols - g * 512)
                    vp = b.inc("pe", nc.tensor.matmul(
                        zx[:, g * 512:g * 512 + w], onesb[:, :],
                        xbf[:, dc * ncols + g * 512: dc * ncols + g * 512 + w],
                        start=(dc == 0), stop=(dc == NCH - 1)))
            b.wait("dve", "pe", vp)
            vd = b.pre("dve").inc("dve", nc.vector.tensor_scalar_mul(m_r, zx[:, :], 1.0 / D))
            scr.close()
            scr = ExitStack()
            x2b = [b.sb(f"ln_x2b{i}", [128, ncols], BF16, scr) for i in range(2)]
            b.wait("pe", "dve", vd)
            b.wait("act", "dve", vd)
            vsq = {}
            for dc in range(NCH):
                if dc >= 2:
                    b.wait("act", "pe", vsq[dc - 2])
                va = b.pre("act").inc("act", nc.scalar.activation(
                    x2b[dc % 2][:, :], xbf[:, dc * ncols:(dc + 1) * ncols],
                    AF.Square))
                b.wait("pe", "act", va)
                for g in range(ngr):
                    w = min(512, ncols - g * 512)
                    vp = b.inc("pe", nc.tensor.matmul(
                        zx[:, g * 512:g * 512 + w], onesb[:, :],
                        x2b[dc % 2][:, g * 512:g * 512 + w],
                        start=(dc == 0), stop=(dc == NCH - 1)))
                vsq[dc] = vp
            b.wait("dve", "pe", vp)
            b.pre("dve").inc("dve", nc.vector.tensor_scalar_mul(sc_r, zx[:, :], 1.0 / D))
            b.pre("dve").inc("dve", nc.vector.tensor_mul(rs_r, m_r, m_r))
            vd = b.pre("dve").inc("dve", nc.vector.tensor_sub(sc_r, sc_r, rs_r))
            b.wait("act", "dve", vd)
            b.pre("act").inc("act", nc.scalar.activation(sc_r, sc_r, AF.Ln, bias=consts["eps"][0:1, 0:1]))
            va = b.pre("act").inc("act", nc.scalar.activation(rs_r, sc_r, AF.Exp, scale=-0.5))
            b.wait("dve", "act", va)
            vd = b.pre("dve").inc("dve", nc.vector.tensor_mul(bm_r, m_r, rs_r))
        with ExitStack() as pst:
            nbc = min(ncols, 1024)
            pa = b.ps("ln_pa", [128, nbc], F32, pst)
            pb = b.ps("ln_pb", [128, nbc], F32, pst)
            b.wait("pe", "act", va)
            b.wait("pe", "dve", vd)
            for c0 in range(0, ncols, nbc):
                if c0 > 0:
                    b.wait("pe", "act", va)
                for g in range(0, nbc, 512):
                    sl = slice(c0 + g, c0 + g + min(512, nbc - g))
                    psl = slice(g, g + min(512, nbc - g))
                    b.inc("pe", nc.tensor.matmul(pa[:, psl], onesf[:, :],
                                                 rs_r[:, sl], start=True, stop=True))
                    vp = b.inc("pe", nc.tensor.matmul(pb[:, psl], onesf[:, :],
                                                      bm_r[:, sl], start=True, stop=True))
                b.wait("act", "pe", vp)
                b.pre("act").inc("act", nc.scalar.copy(
                    ab[:, c0:c0 + nbc], pa[:, :]))
                va = b.pre("act").inc("act", nc.scalar.copy(
                    ab[:, ncols + c0:ncols + c0 + nbc], pb[:, :]))
        scr.close()
        scr = ExitStack()
        t1 = b.sb("ln_t1", [128, ncols], F32, scr)
        A = ab[:, 0:ncols]
        B = ab[:, ncols:2 * ncols]
        b.wait("dve", "act", va)
        if par_split is None:
            for dc in range(NCH):
                xsl = xbf[:, dc * ncols:(dc + 1) * ncols]
                b.pre("dve").inc("dve", nc.vector.tensor_mul(t1[:, :], xsl, A))
                vd = b.pre("dve").inc("dve", nc.vector.tensor_sub(
                    xn[:, dc * ncols:(dc + 1) * ncols], t1[:, :], B))
        else:
            xnp, chlists = par_split
            for par in range(PAR):
                for slot, dc in enumerate(chlists[par]):
                    xsl = xbf[:, dc * ncols + par:(dc + 1) * ncols:PAR]
                    b.pre("dve").inc("dve", nc.vector.tensor_mul(
                        t1[:, 0:ncols // PAR], xsl, A[:, par::PAR]))
                    vd = b.pre("dve").inc("dve", nc.vector.tensor_sub(
                        xnp[par][:, slot * (ncols // PAR):(slot + 1) * (ncols // PAR)],
                        t1[:, 0:ncols // PAR], B[:, par::PAR]))
        scr.close()
    return vd


def emit_w2_prep(b, w2t, w2tmp, w2dd, consts):
    nc = b.nc
    with ExitStack() as st:
        blk = [b.sb(f"wp_blk{i}", [128, D], BF16, st) for i in range(3)]
        ev = [b.sb(f"wp_ev{i}", [128, D], BF16, st) for i in range(2)]
        pp = [b.ps(f"wp_p{i}", [128, 1024], F32, st) for i in range(2)]
        vcopy = {}
        vstore = {}
        vmm = {}

        def p1_load(j1):
            b.dma("gp", f"wpg{j1 % 3}", blk[j1 % 3][:, :],
                  w2t[j1 * 128:(j1 + 1) * 128, :])

        for j in range(3):
            p1_load(j)
        for j1 in range(64):
            k = j1 % 3
            e = j1 % 2
            b.dwait("pe", f"wpg{k}")
            for half in range(2):
                b.wait("pe", "act", vcopy.get((j1 - 1, half), 0))
                for g in range(2):
                    o = half * 1024 + g * 512
                    vp = b.inc("pe", nc.tensor.matmul(
                        pp[half][:, g * 512:(g + 1) * 512], consts["h128"][:, :],
                        blk[k][:, o:o + 512], start=True, stop=True))
                vmm[j1] = vp
                b.wait("act", "pe", vp)
                if half == 0 and j1 >= 2:
                    b.dwait("act", f"wpo{e}")
                vcopy[(j1, half)] = b.pre("act").inc("act", nc.scalar.copy(
                    ev[e][:, half * 1024:(half + 1) * 1024], pp[half][:, :]))
            b.wait("sy", "act", vcopy[(j1, 1)])
            b.dma("sy", f"wpo{e}", w2tmp[j1 * 128:(j1 + 1) * 128, :], ev[e][:, :])
            if j1 + 3 < 64:
                b.wait("gp", "pe", vmm[j1])
                p1_load(j1 + 3)
        b.dwait("sy", "wpo0")
        b.dwait("sy", "wpo1")
        b.wait("gp", "pe")
        tmp_r = w2tmp.rearrange("(j1 k2) d -> k2 j1 d", k2=128)
        dd_r = w2dd.rearrange("(k1 k2) d -> k2 k1 d", k2=128)
        b.dwait("gp", "wpo0")
        b.dwait("gp", "wpo1")

        def p2_load(i):
            b.dma("gp", f"wps{i % 3}", blk[i % 3][:, :], tmp_r[2 * i:2 * i + 2])

        for i in range(3):
            p2_load(i)
        vcopy = {}
        vmm = {}
        for i in range(64):
            k = i % 3
            e = i % 2
            b.dwait("pe", f"wps{k}")
            for half in range(2):
                b.wait("pe", "act", vcopy.get((i - 1, half), 0))
                for g in range(2):
                    o = half * 1024 + g * 512
                    vp = b.inc("pe", nc.tensor.matmul(
                        pp[half][:, g * 512:(g + 1) * 512], consts["h64blk"][:, :],
                        blk[k][:, o:o + 512], start=True, stop=True))
                vmm[i] = vp
                b.wait("act", "pe", vp)
                if half == 0 and i >= 2:
                    b.dwait("act", f"wpd{e}")
                vcopy[(i, half)] = b.pre("act").inc("act", nc.scalar.activation(
                    ev[e][:, half * 1024:(half + 1) * 1024], pp[half][:, :],
                    AF.Copy, scale=FWHT_SCALE))
            b.wait("sy", "act", vcopy[(i, 1)])
            b.dma("sy", f"wpd{e}", dd_r[2 * i:2 * i + 2], ev[e][:, :])
            if i + 3 < 64:
                b.wait("gp", "pe", vmm[i])
                p2_load(i + 3)
        b.dwait("sy", "wpd0")
        b.dwait("sy", "wpd1")


def _load_band(b, w1t, w3t, d1, d3, band):
    sl = slice(band * 256, (band + 1) * 256)
    ds = f"fab{band % 2}"
    b.dma("gp", ds, d1[:, :].rearrange("dp (dc h) -> dp dc h", dc=NCH),
          w1t[:, sl].rearrange("(dc dp) h -> dp dc h", dp=128))
    return b.dma("gp", ds, d3[:, :].rearrange("dp (dc h) -> dp dc h", dc=NCH),
                 w3t[:, sl].rearrange("(dc dp) h -> dp dc h", dp=128))


def emit_ffn(b, xres_src, dst_hbm, w1t, w3t, w2dd, hid, consts, prep_args=None):
    nc = b.nc
    with ExitStack() as st:
        with ExitStack() as ast:
            xn = b.sb("fa_xn", [128, NCH * S], BF16, ast)
            prep_gen = None
            if prep_args is not None:
                prep_gen = w2_prep_steps(b, prep_args[0], prep_args[1],
                                         w2dd, consts, ast)
            w1b = [b.sb(f"fa_w1{i}", [128, NCH * 256], BF16, ast) for i in range(2)]
            w3b = [b.sb(f"fa_w3{i}", [128, NCH * 256], BF16, ast) for i in range(2)]
            sg = [b.sb(f"fa_sg{i}", [128, 1024], F32, ast) for i in range(2)]
            hbuf = [b.sb(f"fa_hb{i}", [128, 1024], BF16, ast) for i in range(2)]
            wload = [_load_band(b, w1t, w3t, w1b[0], w3b[0], 0),
                     _load_band(b, w1t, w3t, w1b[1], w3b[1], 1)]
            vxn = emit_ln(b, xres_src, xn, 0, S, consts)
            b.wait("pe", "dve", vxn)
            if prep_gen is not None:
                pg = [b.ps(f"fa_pg{i}", [128, 1024], F32, ast) for i in range(2)]
                pl = [b.ps("fa_pl", [128, 1024], F32, ast)]
            hseq = []
            if prep_gen is None:
                with ExitStack() as wps:
                    pgw = b.ps("fa_pgw", [128, 2048], F32, wps)
                    plw = b.ps("fa_plw", [128, 2048], F32, wps)
                    sgw = [b.sb(f"fa_sgw{i}", [128, 2048], F32, wps)
                           for i in range(2)]
                    hbw = [b.sb(f"fa_hbw{i}", [128, 2048], BF16, wps)
                           for i in range(2)]
                    vsil = {}
                    vmul = {}
                    for hc in range(NHID):
                        band, hcl = hc // 2, hc % 2
                        k = band % 2
                        b.dwait("pe", f"fab{k}")
                        q = hc % 2
                        if hc >= 1:
                            b.wait("pe", "act", vsil[hc - 1])
                        for dc in range(NCH):
                            l1 = w1b[k][:, dc * 256 + hcl * 128:
                                        dc * 256 + hcl * 128 + 128]
                            for g in range(4):
                                vpg = b.inc("pe", nc.tensor.matmul(
                                    pgw[:, g * 512:(g + 1) * 512], l1,
                                    xn[:, dc * S + g * 512:dc * S + g * 512 + 512],
                                    start=(dc == 0), stop=(dc == NCH - 1)))
                        if hc >= 2:
                            b.wait("act", "dve", vmul[hc - 2])
                        b.wait("act", "pe", vpg)
                        vsil[hc] = b.pre("act").inc("act", nc.scalar.activation(
                            sgw[q][:, :], pgw[:, :], AF.Silu))
                        if hc >= 1:
                            b.wait("pe", "dve", vmul[hc - 1])
                        for dc in range(NCH):
                            l3 = w3b[k][:, dc * 256 + hcl * 128:
                                        dc * 256 + hcl * 128 + 128]
                            for g in range(4):
                                vp = b.inc("pe", nc.tensor.matmul(
                                    plw[:, g * 512:(g + 1) * 512], l3,
                                    xn[:, dc * S + g * 512:dc * S + g * 512 + 512],
                                    start=(dc == 0), stop=(dc == NCH - 1)))
                        b.dwait("dve", f"fah{q}")
                        b.wait("dve", "act", vsil[hc])
                        b.wait("dve", "pe", vp)
                        vmul[hc] = b.pre("dve").inc("dve", nc.vector.tensor_mul(
                            hbw[q][:, :], sgw[q][:, :], plw[:, :]))
                        b.wait("sy", "dve", vmul[hc])
                        b.dma("sy", f"fah{q}", hid.ap()[0, hc], hbw[q][:, 0:1024])
                        b.dma("sy", f"fah{q}", hid.ap()[1, hc], hbw[q][:, 1024:2048])
                        if hc % 2 == 1 and band + 2 < 32:
                            b.wait("gp", "pe")
                            wload.append(_load_band(b, w1t, w3t, w1b[k], w3b[k],
                                                    band + 2))
                b.dwait("sy", "fah0")
                b.dwait("sy", "fah1")
            else:
              for band in range(32):
                k = band % 2
                b.dwait("pe", f"fab{band % 2}")
                for hcl in range(2):
                    hc = band * 2 + hcl
                    for th in range(2):
                        idx = hc * 2 + th
                        q = idx % 2
                        if len(hseq) >= 2:
                            b.wait("pe", "act", hseq[-2][0])
                        xo0 = th * 1024
                        for dc in range(NCH):
                            l1 = w1b[k][:, dc * 256 + hcl * 128:
                                        dc * 256 + hcl * 128 + 128]
                            xb = dc * S + xo0
                            b.inc("pe", nc.tensor.matmul(
                                pg[q][:, 0:512], l1, xn[:, xb:xb + 512],
                                start=(dc == 0), stop=(dc == NCH - 1)))
                            vpg = b.inc("pe", nc.tensor.matmul(
                                pg[q][:, 512:1024], l1, xn[:, xb + 512:xb + 1024],
                                start=(dc == 0), stop=(dc == NCH - 1)))
                        if len(hseq) >= 2:
                            b.wait("act", "dve", hseq[-2][1])
                        b.wait("act", "pe", vpg)
                        va = b.pre("act").inc("act", nc.scalar.activation(
                            sg[q][:, :], pg[q][:, :], AF.Silu))
                        if hseq:
                            b.wait("pe", "dve", hseq[-1][1])
                        for dc in range(NCH):
                            l3 = w3b[k][:, dc * 256 + hcl * 128:
                                        dc * 256 + hcl * 128 + 128]
                            xb = dc * S + xo0
                            b.inc("pe", nc.tensor.matmul(
                                pl[0][:, 0:512], l3, xn[:, xb:xb + 512],
                                start=(dc == 0), stop=(dc == NCH - 1)))
                            vp = b.inc("pe", nc.tensor.matmul(
                                pl[0][:, 512:1024], l3, xn[:, xb + 512:xb + 1024],
                                start=(dc == 0), stop=(dc == NCH - 1)))
                        b.dwait("dve", f"fah{q}")
                        b.wait("dve", "act", va)
                        b.wait("dve", "pe", vp)
                        vd = b.pre("dve").inc("dve", nc.vector.tensor_mul(
                            hbuf[q][:, :], sg[q][:, :], pl[0][:, :]))
                        b.wait("sy", "dve", vd)
                        vo = b.dma("sy", f"fah{q}", hid.ap()[th, hc], hbuf[q][:, :])
                        if prep_gen is not None:
                            next(prep_gen, None)
                        hseq.append((va, vd, vo))
                if band < 30:
                    b.wait("gp", "pe")
                    wload.append(_load_band(b, w1t, w3t, w1b[k], w3b[k], band + 2))
            b.dwait("sy", "fah0")
            b.dwait("sy", "fah1")
            if prep_gen is not None:
                for _ in prep_gen:
                    pass
                b.dwait("sy", "wpd0")
                b.dwait("sy", "wpd1")
        b.bar()
        with ExitStack() as bst:
            hstr = b.sb("fb_h", [128, NHID * 1024], BF16, bst)
            w2s = [b.sb(f"fb_w2{i}", [128, 256], BF16, bst) for i in range(8)]
            xr = [b.sb(f"fb_xr{i}", [128, 1024], F32, bst) for i in range(2)]
            ox = [b.sb(f"fb_ox{i}", [128, 1024], F32, bst) for i in range(2)]
            pb = [[b.ps(f"fb_p{i}{j}", [128, 1024], F32, bst) for j in range(2)]
                  for i in range(2)]
            hs_r = hstr[:, :].rearrange("dp (hc t) -> dp hc t", hc=NHID)
            for ts in range(2):
                if ts > 0:
                    b.wait("sy", "pe")
                    b.wait("ac", "pe")
                for part in range(8):
                    qn = "sy" if part % 2 == 0 else "ac"
                    b.dma(qn, "fbh" + qn, hs_r[:, part * 8:(part + 1) * 8],
                          hid.ap()[ts, part * 8:(part + 1) * 8].rearrange(
                              "hc dp t -> dp hc t"))
                pv = {}
                b.wait("ac", "pe")
                for pre in range(6):
                    b.dma("ac", f"fbw{pre}", w2s[pre][:, :],
                          w2dd[pre * 128:(pre + 1) * 128, 0:256])
                b.dwait("pe", "fbhsy")
                b.dwait("pe", "fbhac")
                ev = []
                evx = []
                for db in range(8):
                    q = db % 2
                    if db >= 2:
                        b.wait("pe", "dve", ev[db - 2][0])
                    for h in range(NHID):
                        idx = db * NHID + h
                        b.dwait("pe", f"fbw{idx % 8}")
                        ws = w2s[idx % 8]
                        for ds in range(2):
                            lh = ws[:, ds * 128:(ds + 1) * 128]
                            for th in range(2):
                                vp = b.inc("pe", nc.tensor.matmul(
                                    pb[q][ds][:, th * 512:(th + 1) * 512], lh,
                                    hstr[:, h * 1024 + th * 512:h * 1024 + th * 512 + 512],
                                    start=(h == 0), stop=(h == NHID - 1)))
                        pv[idx] = vp
                        nidx = idx + 6
                        if nidx < 8 * NHID:
                            ndb, nh = divmod(nidx, NHID)
                            if nidx >= 8:
                                b.wait("ac", "pe", pv[nidx - 8])
                            b.dma("ac", f"fbw{nidx % 8}", w2s[nidx % 8][:, :],
                                  w2dd[nh * 128:(nh + 1) * 128,
                                       ndb * 256:(ndb + 1) * 256])
                    for ds in range(2):
                        r0 = db * 256 + ds * 128
                        kx = (db * 2 + ds) % 2
                        if len(evx) >= 2:
                            b.wait("sy", "dve", evx[-2][0])
                        b.dma("sy", f"fbx{kx}", xr[kx][:, :],
                              xres_src[r0:r0 + 128, ts * 1024:(ts + 1) * 1024])
                        if len(evx) >= 2:
                            b.dwait("dve", f"fbo{kx}")
                        b.dwait("dve", f"fbx{kx}")
                        b.wait("dve", "pe", vp)
                        vd = b.pre("dve").inc("dve", nc.vector.tensor_add(
                            ox[kx][:, :], pb[q][ds][:, :], xr[kx][:, :]))
                        b.wait("sy", "dve", vd)
                        vo = b.dma("sy", f"fbo{kx}", dst_hbm[r0:r0 + 128,
                                   ts * 1024:(ts + 1) * 1024], ox[kx][:, :])
                        evx.append((vd, vo))
                    ev.append((vd, vo))
                b.dwait("sy", "fbo0")
                b.dwait("sy", "fbo1")
        b.bar()


def w2_prep_steps(b, w2t, w2tmp, w2dd, consts, st):
    nc = b.nc
    blk = [b.sb(f"wp_blk{i}", [128, D], BF16, st) for i in range(3)]
    ev = [b.sb(f"wp_ev{i}", [128, D], BF16, st) for i in range(2)]
    pp = [b.ps(f"wp_p{i}", [128, 512], F32, st) for i in range(2)]

    def p1_load(j1):
        b.dma("gp", f"wpg{j1 % 3}", blk[j1 % 3][:, :],
              w2t[j1 * 128:(j1 + 1) * 128, :])

    for j in range(3):
        p1_load(j)
    return _w2_gen(b, w2t, w2tmp, w2dd, consts, blk, ev, pp, p1_load)


def _w2_gen(b, w2t, w2tmp, w2dd, consts, blk, ev, pp, p1_load):
    nc = b.nc

    def one_pass(load_stream, out_stream, hmat, scale, dst_r, loader):
        vq = {}
        vmm = {}

        def quarter(it, qr):
            e = it % 2
            p = qr % 2
            eng = "act" if p == 0 else "dve"
            b.dwait("pe", f"{load_stream}{it % 3}")
            b.wait("pe", eng, vq.get((it, qr - 2), vq.get((it - 1, qr + 2), 0)))
            vp = b.inc("pe", nc.tensor.matmul(
                pp[p][:, :], hmat[:, :],
                blk[it % 3][:, qr * 512:(qr + 1) * 512], start=True, stop=True))
            if qr == 3:
                vmm[it] = vp
            b.wait(eng, "pe", vp)
            if qr < 2 and it >= 2:
                b.dwait(eng, f"{out_stream}{e}")
            dsl = ev[e][:, qr * 512:(qr + 1) * 512]
            if eng == "act":
                vq[(it, qr)] = b.pre("act").inc("act", nc.scalar.activation(
                    dsl, pp[p][:, :], AF.Copy, scale=scale))
            elif scale != 1.0:
                vq[(it, qr)] = b.pre("dve").inc(
                    "dve", nc.vector.tensor_scalar_mul(dsl, pp[p][:, :], scale))
            else:
                vq[(it, qr)] = b.pre("dve").inc(
                    "dve", nc.vector.tensor_copy(dsl, pp[p][:, :]))

        quarter(0, 0)
        quarter(0, 1)
        yield
        for it in range(64):
            quarter(it, 2)
            quarter(it, 3)
            e = it % 2
            b.wait("sy", "act", vq[(it, 2)])
            b.wait("sy", "dve", vq[(it, 3)])
            b.dma("sy", f"{out_stream}{e}", dst_r(it), ev[e][:, :])
            if it + 3 < 64:
                b.wait("gp", "pe", vmm[it])
                loader(it + 3)
            if it + 1 < 64:
                quarter(it + 1, 0)
                quarter(it + 1, 1)
            yield

    yield from one_pass("wpg", "wpo", consts["h128"], 1.0,
                        lambda it: w2tmp[it * 128:(it + 1) * 128, :], p1_load)
    tmp_r = w2tmp.rearrange("(j1 k2) d -> k2 j1 d", k2=128)
    dd_r = w2dd.rearrange("(k1 k2) d -> k2 k1 d", k2=128)
    b.dwait("gp", "wpo0")
    b.dwait("gp", "wpo1")
    b.wait("gp", "pe")

    def p2_load(i):
        b.dma("gp", f"wps{i % 3}", blk[i % 3][:, :], tmp_r[2 * i:2 * i + 2])

    for i in range(3):
        p2_load(i)
    yield from one_pass("wps", "wpd", consts["h64blk"], FWHT_SCALE,
                        lambda it: dd_r[2 * it:2 * it + 2], p2_load)


def emit_attn_a(b, x_in, xres, consts, prep_gen=None):
    nc = b.nc
    chl = [[h for h in range(NHEADS) if h % 2 == p] for p in range(PAR)]
    slots = [(par, sl) for par in range(PAR) for sl in range(8)]
    for seg in range(NSEGS):
        c0 = seg * SEG
        with ExitStack() as st:
            xf = b.sb("aa_xf", [128, NCH * SEG], F32, st)
            xnp = [b.sb(f"aa_xnp{p}", [128, 8 * SEGH], BF16, st) for p in range(PAR)]
            gt = [b.sb(f"aa_gt{i}", [128, 2 * SEGH], BF16, st) for i in range(2)]
            et = [[b.sb(f"aa_et{i}{jb}", [128, SEGH], BF16, st) for jb in range(2)]
                  for i in range(2)]
            avs = [b.sb(f"aa_avs{i}", [128, SEGH], F32, st) for i in range(2)]
            rzr = [b.sb(f"aa_rz{i}", [1, SEGH], F32, st) for i in range(2)]
            an = b.sb("aa_an", [128, SEGH], F32, st)
            for dc in range(NCH):
                b.dma("sy", "aax", xf[:, dc * SEG:(dc + 1) * SEG],
                      x_in[dc * 128:(dc + 1) * 128, c0:c0 + SEG])
            vxn = emit_ln(b, x_in, None, c0, SEG, consts,
                          par_split=(xnp, chl), src_sbuf=(xf, "aax"))
            b.wait("pe", "dve", vxn)
            b.wait("sy", "dve", vxn)
            with ExitStack() as pst:
                psc = [b.ps(f"aa_psc{i}", [128, SEGH], F32, pst) for i in range(2)]
                pz = [b.ps(f"aa_pz{i}", [1, SEGH], F32, pst) for i in range(2)]
                pav = [b.ps(f"aa_pav{i}", [128, SEGH], F32, pst) for i in range(2)]
                prz = [b.ps(f"aa_prz{i}", [128, SEGH], F32, pst) for i in range(2)]
                stt = {"vas": {}, "vzav": {}, "vrz": [0, 0], "vavs": [0, 0],
                       "vpb": [0, 0], "vtail": [0, 0], "tail": 0}

                def stage1(s):
                    par, sl = slots[s]
                    g = xnp[par][:, sl * SEGH:(sl + 1) * SEGH]
                    sb = s % 2
                    if s >= 2:
                        b.wait("sy", "pe", stt["vzav"].get(s - 2, 0))
                    for jb in range(2):
                        b.dma("sy", f"agt{sb}", gt[sb][:, jb * 128:(jb + 1) * 128],
                              g[:, jb * 128:(jb + 1) * 128], transpose=True)
                    for jb in range(2):
                        b.wait("pe", "act", stt["vas"].get((s - 1, jb), 0))
                        vp = b.inc("pe", nc.tensor.matmul(
                            psc[jb][:, :], g[:, jb * 128:(jb + 1) * 128], g,
                            start=True, stop=True))
                        b.wait("act", "pe", vp)
                        if s >= 2:
                            b.wait("act", "pe", stt["vzav"].get(s - 2, 0))
                        stt["vas"][(s, jb)] = b.pre("act").inc(
                            "act", nc.scalar.activation(
                                et[sb][jb][:, :], psc[jb][:, :], AF.Exp,
                                bias=consts["shift"][:, :], scale=INV_SQRT_HD))

                def zav_recip(s):
                    par, sl = slots[s]
                    sb = s % 2
                    e = s % 2
                    b.wait("pe", "act", stt["vrz"][e])
                    b.wait("pe", "act", stt["vavs"][e])
                    b.dwait("pe", f"agt{sb}")
                    for jb in range(2):
                        b.wait("pe", "act", stt["vas"][(s, jb)])
                        b.inc("pe", nc.tensor.matmul(
                            pz[e][:, :], consts["ones_bf"][:, :], et[sb][jb][:, :],
                            start=(jb == 0), stop=(jb == 1)))
                        vp = b.inc("pe", nc.tensor.matmul(
                            pav[e][:, :], gt[sb][:, jb * 128:(jb + 1) * 128],
                            et[sb][jb][:, :], start=(jb == 0), stop=(jb == 1)))
                    stt["vzav"][s] = vp
                    b.wait("act", "pe", vp)
                    if stt["vpb"][e]:
                        b.wait("act", "pe", stt["vpb"][e])
                    b.pre("act").inc("act", nc.scalar.activation(
                        rzr[e][:, :], pz[e][:, :], AF.Ln))
                    stt["vrz"][e] = b.pre("act").inc("act", nc.scalar.activation(
                        rzr[e][:, :], rzr[e][:, :], AF.Exp, scale=-1.0))

                def tail(s):
                    par, sl = slots[s]
                    h = chl[par][sl]
                    e = s % 2
                    b.wait("pe", "act", stt["vrz"][e])
                    b.wait("pe", "dve", stt["vtail"][e])
                    stt["vpb"][e] = b.inc("pe", nc.tensor.matmul(
                        prz[e][:, :], consts["ones_f32"][:, :], rzr[e][:, :],
                        start=True, stop=True))
                    b.wait("act", "dve", stt["vtail"][e])
                    b.wait("act", "pe", stt["vzav"][s])
                    va = stt["vavs"][e] = b.pre("act").inc(
                        "act", nc.scalar.copy(avs[e][:, :], pav[e][:, :]))
                    b.wait("dve", "act", va)
                    b.wait("dve", "pe", stt["vpb"][e])
                    b.dwait("dve", "aax")
                    b.pre("dve").inc("dve", nc.vector.tensor_mul(
                        an[:, :], avs[e][:, :], prz[e][:, :]))
                    xsl = xf[:, h * SEG + par:(h + 1) * SEG:PAR]
                    stt["vtail"][e] = stt["tail"] = b.pre("dve").inc(
                        "dve", nc.vector.tensor_add(xsl, xsl, an[:, :]))

                stage1(0)
                stage1(1)
                for s in range(16):
                    zav_recip(s)
                    if s + 2 < 16:
                        stage1(s + 2)
                    if s >= 1:
                        tail(s - 1)
                tail(15)
            b.wait("sy", "dve")
            for dc in range(NCH):
                b.dma("sy", "aar", xres[dc * 128:(dc + 1) * 128, c0:c0 + SEG],
                      xf[:, dc * SEG:(dc + 1) * SEG])
            b.dwait("sy", "aar")
        b.bar()


def emit_attn_b(b, xres, wqt, wkt, wvt, wot, consts):
    nc = b.nc
    with ExitStack() as st:
        ao = b.sb("ab_ao", [128, NCH * S], BF16, st)
        with ExitStack() as hst:
            xn = b.sb("ab_xn", [128, NCH * S], BF16, hst)
            vxn = emit_ln(b, xres, xn, 0, S, consts)
            b.wait("pe", "dve", vxn)
            b.wait("gp", "dve", vxn)
            b.wait("gp", "act")
            qw = [b.sb(f"ab_qw{i}", [128, NCH * 128], BF16, hst) for i in range(2)]
            kw = [b.sb(f"ab_kw{i}", [128, NCH * 128], BF16, hst) for i in range(2)]
            vw = [b.sb(f"ab_vw{i}", [128, NCH * 128], BF16, hst) for i in range(2)]
            qh = [b.sb(f"ab_qh{i}", [128, S], BF16, hst) for i in range(2)]
            kh = [b.sb(f"ab_kh{i}", [128, S], BF16, hst) for i in range(2)]
            vh = [b.sb(f"ab_vh{i}", [128, S], BF16, hst) for i in range(2)]
            vt = [b.sb(f"ab_vt{i}", [128, S], BF16, hst) for i in range(2)]
            et = [b.sb(f"ab_et{i}", [128, 512], BF16, hst) for i in range(2)]
            avs = [b.sb(f"ab_avs{i}", [128, 512], F32, hst) for i in range(2)]
            rzr = [b.sb(f"ab_rz{i}", [1, 512], F32, hst) for i in range(2)]

            def load_head_w(h):
                for wt, dst in ((wqt, qw), (wkt, kw), (wvt, vw)):
                    b.dma("gp", f"abw{h % 2}", dst[h % 2][:, :].rearrange(
                        "dp (dc o) -> dp dc o", dc=NCH),
                        wt[:, h * 128:(h + 1) * 128].rearrange(
                            "(dc dp) o -> dp dc o", dp=128))

            load_head_w(0)
            load_head_w(1)
            with ExitStack() as pst:
                pq = [b.ps("ab_pq", [128, 512], F32, pst)]
                psc = [b.ps(f"ab_psc{i}", [128, 512], F32, pst) for i in range(2)]
                pz = [b.ps(f"ab_pz{i}", [1, 512], F32, pst) for i in range(2)]
                pav = [b.ps(f"ab_pav{i}", [128, 512], F32, pst) for i in range(2)]
                prz = b.ps("ab_prz", [128, 512], F32, pst)
                stt = {"pq_prev": [], "vtr": 0, "qv": 0, "pe_proj": {},
                       "ready": {}, "vhva": {}}

                def qkv_units(h):
                    k = h % 2

                    def mk_proj(di, tg):
                        def emit():
                            dst = (qh[k], kh[k], vh[k])[di]
                            wsb = (qw[k], kw[k], vw[k])[di]
                            q = 0
                            b.dwait("pe", f"abw{k}")
                            if stt["pq_prev"]:
                                b.wait("pe", "dve", stt["pq_prev"][-1])
                            for dc in range(NCH):
                                vp = b.inc("pe", nc.tensor.matmul(
                                    pq[q][:, :], wsb[:, dc * 128:(dc + 1) * 128],
                                    xn[:, dc * S + tg * 512:dc * S + tg * 512 + 512],
                                    start=(dc == 0), stop=(dc == NCH - 1)))
                            if di == 2 and tg == 3:
                                stt["pe_proj"][h] = vp
                            b.wait("dve", "pe", vp)
                            va = b.pre("dve").inc("dve", nc.vector.tensor_copy(
                                dst[:, tg * 512:(tg + 1) * 512], pq[q][:, :]))
                            stt["pq_prev"].append(va)
                            if di == 2:
                                stt["vhva"][(h, tg)] = va
                        return emit

                    def mk_vt(j):
                        def emit():
                            b.wait("sy", "dve", stt["vhva"][(h, j // 4)])
                            if j == 0:
                                b.wait("sy", "pe")
                            b.dma("sy", f"vtr{k}", vt[k][:, j * 128:(j + 1) * 128],
                                  vh[k][:, j * 128:(j + 1) * 128], transpose=True)
                            if j == 15:
                                stt["ready"][h] = stt["pq_prev"][-1]
                        return emit

                    return ([mk_proj(di, tg) for di in range(3) for tg in range(4)]
                            + [mk_vt(j) for j in range(16)])

                for u in qkv_units(0):
                    u()
                vd_tail = 0
                vd_tails = [0, 0]
                vpb_prev = 0
                vrz_prev = [0, 0]
                vavs_prev = [0, 0]
                for h in range(NHEADS):
                    k = h % 2
                    units = qkv_units(h + 1) if h + 1 < NHEADS else []
                    nslots = 44
                    slot = [0, 0]

                    def tick():
                        slot[0] += 1
                        tgt = (slot[0] * len(units)) // nslots
                        while slot[1] < min(tgt, len(units)):
                            units[slot[1]]()
                            slot[1] += 1

                    if h + 2 < NHEADS:
                        b.wait("gp", "pe", stt["pe_proj"][h])
                        load_head_w(h + 2)
                    b.dwait("pe", f"vtr{k}")
                    b.wait("pe", "dve", stt["ready"][h])
                    for ig in range(4):
                        jmax = 4 * ig + 3
                        vas = {}
                        vzav = {}

                        e = (h * 4 + ig) % 2

                        def emit_zav(j, start, stop):
                            q = j % 2
                            osj = max(0, j * 128 - ig * 512)
                            b.wait("pe", "act", vas[j])
                            b.inc("pe", nc.tensor.matmul(
                                pz[e][:, osj:512], consts["ones_bf"][:, :],
                                et[q][:, osj:512], start=start, stop=stop))
                            vzav[j] = b.inc("pe", nc.tensor.matmul(
                                pav[e][:, osj:512], vt[k][:, j * 128:(j + 1) * 128],
                                et[q][:, osj:512], start=start, stop=stop))

                        b.wait("pe", "act", vrz_prev[e])
                        b.wait("pe", "act", vavs_prev[e])
                        for j in range(jmax + 1):
                            osj = max(0, j * 128 - ig * 512)
                            q = j % 2
                            if j >= 2:
                                b.wait("pe", "act", vas[j - 2])
                            vp = b.inc("pe", nc.tensor.matmul(
                                psc[q][:, osj:512], kh[k][:, j * 128:(j + 1) * 128],
                                qh[k][:, ig * 512 + osj:(ig + 1) * 512],
                                start=True, stop=True))
                            vdm = 0
                            if j * 128 >= ig * 512:
                                b.wait("dve", "pe", vp)
                                vdm = b.pre("dve").inc("dve", nc.vector.tensor_add(
                                    psc[q][:, osj:osj + 128],
                                    psc[q][:, osj:osj + 128], consts["dmask"][:, :]))
                            if j >= 1:
                                emit_zav(j - 1, j - 1 == 0, False)
                            tick()
                            b.wait("act", "pe", vp)
                            if vdm:
                                b.wait("act", "dve", vdm)
                            if j >= 2:
                                b.wait("act", "pe", vzav[j - 2])
                            vas[j] = b.pre("act").inc("act", nc.scalar.activation(
                                et[q][:, osj:512], psc[q][:, osj:512], AF.Exp,
                                bias=consts["shift"][:, :], scale=INV_SQRT_HD))
                        emit_zav(jmax, jmax == 0, True)
                        vp = vzav[jmax]
                        b.wait("act", "pe", vp)
                        if vpb_prev:
                            b.wait("act", "pe", vpb_prev)
                        b.pre("act").inc("act", nc.scalar.activation(
                            rzr[e][:, :], pz[e][:, :], AF.Ln))
                        vrz = vrz_prev[e] = b.pre("act").inc(
                            "act", nc.scalar.activation(
                                rzr[e][:, :], rzr[e][:, :], AF.Exp, scale=-1.0))
                        tick()
                        b.wait("pe", "act", vrz)
                        b.wait("pe", "dve", vd_tail)
                        vpb_prev = b.inc("pe", nc.tensor.matmul(
                            prz[:, :], consts["ones_f32"][:, :], rzr[e][:, :],
                            start=True, stop=True))
                        b.wait("act", "dve", vd_tails[e])
                        b.wait("act", "pe", vp)
                        va = vavs_prev[e] = b.pre("act").inc("act", nc.scalar.copy(
                            avs[e][:, :], pav[e][:, :]))
                        b.wait("dve", "act", va)
                        b.wait("dve", "pe", vpb_prev)
                        vd_tail = vd_tails[e] = b.pre("dve").inc(
                            "dve", nc.vector.tensor_mul(
                                ao[:, h * S + ig * 512:h * S + (ig + 1) * 512],
                                avs[e][:, :], prz[:, :]))
                    for u in units[slot[1]:]:
                        u()
                    slot[1] = len(units)
                b.wait("pe", "dve", vd_tail)
        b.bar()
        with ExitStack() as wst:
            wob = [b.sb(f"ab_wo{i}", [128, NCH * 128], BF16, wst) for i in range(3)]
            xr = [b.sb(f"ab_xr{i}", [128, 512], F32, wst) for i in range(3)]
            oxt = [b.sb(f"ab_ox{i}", [128, 512], F32, wst) for i in range(3)]
            pwo = [b.ps(f"ab_pw{i}", [128, 512], F32, wst) for i in range(3)]

            def load_wo(do):
                return b.dma(
                    "gp", f"awo{do % 3}",
                    wob[do % 3][:, :].rearrange("dp (dc o) -> dp dc o", dc=NCH),
                    wot[:, do * 128:(do + 1) * 128].rearrange(
                        "(dc dp) o -> dp dc o", dp=128))

            load_wo(0)
            load_wo(1)
            pv_wo = {}
            seq = []
            for do in range(NCH):
                k = do % 3
                b.dwait("pe", f"awo{do % 3}")
                if do + 2 < NCH:
                    if do >= 1:
                        b.wait("gp", "pe", pv_wo[do - 1])
                    load_wo(do + 2)
                for tg in range(4):
                    idx = do * 4 + tg
                    q = idx % 3
                    if idx >= 3:
                        b.wait("pe", "dve", seq[idx - 3][0])
                    for dc in range(NCH):
                        vp = b.inc("pe", nc.tensor.matmul(
                            pwo[q][:, :], wob[k][:, dc * 128:(dc + 1) * 128],
                            ao[:, dc * S + tg * 512:dc * S + tg * 512 + 512],
                            start=(dc == 0), stop=(dc == NCH - 1)))
                    if idx >= 3:
                        b.wait("ac", "dve", seq[idx - 3][0])
                    b.dma("ac", f"abx{q}", xr[q][:, :],
                          xres[do * 128:(do + 1) * 128, tg * 512:(tg + 1) * 512])
                    if idx >= 3:
                        b.dwait("dve", f"abo{q}")
                    b.dwait("dve", f"abx{q}")
                    b.wait("dve", "pe", vp)
                    vd = b.pre("dve").inc("dve", nc.vector.tensor_add(
                        oxt[q][:, :], pwo[q][:, :], xr[q][:, :]))
                    b.wait("sy", "dve", vd)
                    vo = b.dma("sy", f"abo{q}", xres[do * 128:(do + 1) * 128,
                               tg * 512:(tg + 1) * 512], oxt[q][:, :])
                    pv_wo[do] = vp
                    seq.append((vd, vo))
            b.dwait("sy", "abo0")
            b.dwait("sy", "abo1")
            b.dwait("sy", "abo2")
        b.bar()


def build_nc(phases=("attn_a", "prep", "ffn1", "attn_b", "ffn2")):
    nc = bass.Bass("TRN2", target_bir_lowering=False)
    x_in = nc.dram_tensor("x", [D, S], F32, kind="ExternalInput").ap()
    w1t = nc.dram_tensor("w1t", [D, HID], F32, kind="ExternalInput").ap()
    w3t = nc.dram_tensor("w3t", [D, HID], F32, kind="ExternalInput").ap()
    w2t = nc.dram_tensor("w2t", [HID, D], F32, kind="ExternalInput").ap()
    wqt = nc.dram_tensor("wqt", [D, D], F32, kind="ExternalInput").ap()
    wkt = nc.dram_tensor("wkt", [D, D], F32, kind="ExternalInput").ap()
    wvt = nc.dram_tensor("wvt", [D, D], F32, kind="ExternalInput").ap()
    wot = nc.dram_tensor("wot", [D, D], F32, kind="ExternalInput").ap()
    out = nc.dram_tensor("out", [D, S], F32, kind="ExternalOutput").ap()
    xres = nc.dram_tensor("xres", [D, S], F32, kind="Internal").ap()
    w2tmp = nc.dram_tensor("w2tmp", [HID, D], BF16, kind="Internal").ap()
    w2dd = nc.dram_tensor("w2dd", [HID, D], BF16, kind="Internal").ap()
    hid = nc.dram_tensor("hid", [2, NHID, 128, 1024], BF16, kind="Internal")

    with ExitStack() as st:
        b = Bld(nc, st)
        consts = emit_consts(b)
        if "attn_a" in phases:
            emit_attn_a(b, x_in, xres, consts)
        if "prep" in phases and "ffn1" not in phases:
            with ExitStack() as pst:
                for _ in w2_prep_steps(b, w2t, w2tmp, w2dd, consts, pst):
                    pass
                b.dwait("sy", "wpd0")
                b.dwait("sy", "wpd1")
            b.bar()
        if "ffn1" in phases:
            pa = (w2t, w2tmp) if "prep" in phases else None
            emit_ffn(b, xres, xres, w1t, w3t, w2dd, hid, consts, prep_args=pa)
        if "attn_b" in phases:
            emit_attn_b(b, xres, wqt, wkt, wvt, wot, consts)
        if "ffn2" in phases:
            emit_ffn(b, xres, out, w1t, w3t, w2dd, hid, consts)
        b.dwait("sy", "fbo0")
        b.dwait("sy", "fbo1")
    return nc


def kernel(L0_attn_a_x, L0_attn_b_wq, L0_attn_b_wk, L0_attn_b_wv, L0_attn_b_wo,
           L0_ffn_W1, L0_ffn_W2, L0_ffn_W3):
    x = np.asarray(L0_attn_a_x, dtype=np.float32)
    wshared = {
        "w1t": np.ascontiguousarray(np.asarray(L0_ffn_W1, np.float32).T),
        "w3t": np.ascontiguousarray(np.asarray(L0_ffn_W3, np.float32).T),
        "w2t": np.ascontiguousarray(np.asarray(L0_ffn_W2, np.float32).T),
        "wqt": np.ascontiguousarray(np.asarray(L0_attn_b_wq, np.float32).T),
        "wkt": np.ascontiguousarray(np.asarray(L0_attn_b_wk, np.float32).T),
        "wvt": np.ascontiguousarray(np.asarray(L0_attn_b_wv, np.float32).T),
        "wot": np.ascontiguousarray(np.asarray(L0_attn_b_wo, np.float32).T),
    }
    nc = build_nc()
    in_maps = [dict(wshared, x=np.ascontiguousarray(x[i].T)) for i in range(8)]
    res = run_bass_kernel_spmd(nc, in_maps, core_ids=list(range(8)))
    outs = [np.ascontiguousarray(np.asarray(r["out"]).T) for r in res.results]
    return np.stack(outs, axis=0)


if __name__ == "__main__":
    nc = build_nc()
    print("instructions:", nc.n_instructions())


# version 16
# speedup vs baseline: 1.1058x; 1.0367x over previous
import numpy as np
from contextlib import ExitStack

import concourse.bass as bass
import concourse.mybir as mybir
from concourse.bass_utils import run_bass_kernel_spmd
from concourse import dve_ops

F32 = mybir.dt.float32
BF16 = mybir.dt.bfloat16
AF = mybir.ActivationFunctionType
ALU = mybir.AluOpType

D = 2048
S = 2048
HID = 8192
NCH = D // 128
NHID = HID // 128
NHEADS = 16
SEG = 512
NSEGS = S // SEG
PAR = 2
SEGH = SEG // PAR
INV_SQRT_HD = 0.08838834764831845
EXP_SHIFT = -15.0
LN_EPS = 1e-5
FWHT_SCALE = float(1.0 / np.sqrt(HID))


def _hadamard(n):
    h = np.array([[1.0]])
    while h.shape[0] < n:
        h = np.block([[h, h], [h, -h]])
    return h


def _bf16_u16(a):
    import ml_dtypes
    return np.ascontiguousarray(np.asarray(a, dtype=ml_dtypes.bfloat16).view(np.uint16))


class Bld:

    def __init__(self, nc, st):
        self.nc = nc
        self.st = st
        self.sems = {}
        self.val = {}
        self.waited = {}
        for name in ("pe", "act", "dve", "bar"):
            self.sems[name] = st.enter_context(nc.semaphore(name))
            self.val[name] = 0
        self.streams = {}
        self.engs = {
            "pe": nc.tensor, "act": nc.scalar, "dve": nc.vector,
            "gp": nc.gpsimd, "sy": nc.sync,
        }
        self.alias = {"ac": "act"}

    def inc(self, name, ins, n=1):
        ins.then_inc(self.sems[name], n)
        self.val[name] += n
        return self.val[name]

    def pre(self, name):
        self.wait(name, name)
        return self

    def wait(self, eng_name, sem_name, v=None):
        v = self.val[sem_name] if v is None else v
        if v <= 0:
            return
        eng_name = self.alias.get(eng_name, eng_name)
        key = (eng_name, sem_name)
        if self.waited.get(key, -1) < v:
            self.engs[eng_name].wait_ge(self.sems[sem_name], v)
            self.waited[key] = v

    def ds(self, name):
        if name not in self.streams:
            self.streams[name] = [self.st.enter_context(self.nc.semaphore(name)), 0]
        return self.streams[name]

    def dma(self, q, stream, out, in_, **kw):
        s = self.ds(stream)
        ins = self.engs[self.alias.get(q, q)].dma_start(out, in_, **kw)
        ins.then_inc(s[0], 16)
        s[1] += 16
        return s[1]

    def dwait(self, eng_name, stream, v=None):
        s = self.ds(stream)
        v = s[1] if v is None else v
        if v <= 0:
            return
        eng_name = self.alias.get(eng_name, eng_name)
        key = (eng_name, "D" + stream)
        if self.waited.get(key, -1) < v:
            self.engs[eng_name].wait_ge(s[0], v)
            self.waited[key] = v

    def bar(self):
        self.val["bar"] += len(self.engs)
        tgt = self.val["bar"]
        for e in self.engs.values():
            e.sem_inc(self.sems["bar"], 1)
        for name, e in self.engs.items():
            e.wait_ge(self.sems["bar"], tgt)
            self.waited[(name, "bar")] = tgt
            for s in ("pe", "act", "dve"):
                k = (name, s)
                self.waited[k] = max(self.waited.get(k, -1), self.val[s])

    _uid = 0

    def sb(self, name, shape, dt, st=None):
        Bld._uid += 1
        return (st or self.st).enter_context(
            self.nc.sbuf_tensor(f"{name}_{Bld._uid}", shape, dt))

    def ps(self, name, shape, dt=F32, st=None):
        Bld._uid += 1
        return (st or self.st).enter_context(
            self.nc.psum_tensor(f"{name}_{Bld._uid}", shape, dt))


def emit_consts(b):
    nc = b.nc
    h128 = _hadamard(128)
    h64 = _hadamard(64)
    h64blk = np.zeros((128, 128))
    h64blk[:64, :64] = h64
    h64blk[64:, 64:] = h64
    ident = np.eye(128)
    jj, ii = np.meshgrid(np.arange(128), np.arange(128), indexing="ij")
    dmask = np.where(jj <= ii, 0.0, -1e30).astype(np.float32)

    consts = {
        "h128": b.sb("c_h128", [128, 128], BF16),
        "h64blk": b.sb("c_h64b", [128, 128], BF16),
        "ident": b.sb("c_id", [128, 128], BF16),
        "dmask": b.sb("c_dmask", [128, 128], F32),
        "ones_bf": b.sb("c_onesb", [128, 1], BF16),
        "ones_f32": b.sb("c_onesf", [1, 128], F32),
        "eps": b.sb("c_eps", [128, 1], F32),
        "shift": b.sb("c_shift", [128, 1], F32),
    }
    d_h128 = nc.inline_tensor(_bf16_u16(h128), "i_h128")
    d_h64b = nc.inline_tensor(_bf16_u16(h64blk), "i_h64b")
    d_id = nc.inline_tensor(_bf16_u16(ident), "i_id")
    d_mask = nc.inline_tensor(dmask, "i_dmask")
    b.dma("sy", "cst", consts["h128"][:, :], d_h128.ap().bitcast(BF16))
    b.dma("sy", "cst", consts["h64blk"][:, :], d_h64b.ap().bitcast(BF16))
    b.dma("sy", "cst", consts["ident"][:, :], d_id.ap().bitcast(BF16))
    b.dma("sy", "cst", consts["dmask"][:, :], d_mask.ap())
    b.pre("dve").inc("dve", nc.vector.memset(consts["ones_bf"][:, :], 1.0))
    b.pre("dve").inc("dve", nc.vector.memset(consts["ones_f32"][:, :], 1.0))
    b.pre("dve").inc("dve", nc.vector.memset(consts["eps"][:, :], LN_EPS))
    b.pre("dve").inc("dve", nc.vector.memset(consts["shift"][:, :], EXP_SHIFT))
    b.dwait("pe", "cst")
    b.wait("pe", "dve")
    b.dwait("dve", "cst")
    b.dwait("act", "cst")
    return consts


def emit_ln(b, src_hbm, xn, cols0, ncols, consts, par_split=None, src_sbuf=None):
    nc = b.nc
    onesb, onesf = consts["ones_bf"], consts["ones_f32"]
    ngr = (ncols + 511) // 512
    with ExitStack() as st:
        xbf = xn if par_split is None else b.sb("ln_xbf", [128, NCH * ncols], BF16, st)
        rows = b.sb("ln_rows", [1, 4 * ncols], F32, st)
        ab = b.sb("ln_ab", [128, 2 * ncols], F32, st)
        scr = ExitStack()
        xf = [b.sb(f"ln_xf{i}", [128, ncols], F32, scr) for i in range(2)]
        m_r = rows[:, 0:ncols]
        rs_r = rows[:, ncols:2 * ncols]
        bm_r = rows[:, 2 * ncols:3 * ncols]
        sc_r = rows[:, 3 * ncols:4 * ncols]
        with ExitStack() as pst:
            zx = b.ps("ln_zx", [1, ncols], F32, pst)
            vd = 0
            vp = 0
            for dc in range(NCH):
                if src_sbuf is not None:
                    sb_t, sb_stream = src_sbuf
                    b.dwait("dve", sb_stream)
                    buf = sb_t[:, dc * ncols:(dc + 1) * ncols]
                else:
                    buf = xf[dc % 2]
                    if dc >= 2:
                        b.wait("sy", "dve", ld2[0])
                    b.dma("sy", f"lnx{dc % 2}", buf[:, :],
                          src_hbm[dc * 128:(dc + 1) * 128, cols0:cols0 + ncols])
                    b.dwait("dve", f"lnx{dc % 2}")
                    buf = buf[:, :]
                    ld2 = (vd,)
                vd = b.pre("dve").inc("dve", nc.vector.tensor_copy(
                    xbf[:, dc * ncols:(dc + 1) * ncols], buf))
                b.wait("pe", "dve", vd)
                for g in range(ngr):
                    w = min(512, ncols - g * 512)
                    vp = b.inc("pe", nc.tensor.matmul(
                        zx[:, g * 512:g * 512 + w], onesb[:, :],
                        xbf[:, dc * ncols + g * 512: dc * ncols + g * 512 + w],
                        start=(dc == 0), stop=(dc == NCH - 1)))
            b.wait("dve", "pe", vp)
            vd = b.pre("dve").inc("dve", nc.vector.tensor_scalar_mul(m_r, zx[:, :], 1.0 / D))
            scr.close()
            scr = ExitStack()
            x2b = [b.sb(f"ln_x2b{i}", [128, ncols], BF16, scr) for i in range(2)]
            b.wait("pe", "dve", vd)
            b.wait("act", "dve", vd)
            vsq = {}
            for dc in range(NCH):
                if dc >= 2:
                    b.wait("act", "pe", vsq[dc - 2])
                va = b.pre("act").inc("act", nc.scalar.activation(
                    x2b[dc % 2][:, :], xbf[:, dc * ncols:(dc + 1) * ncols],
                    AF.Square))
                b.wait("pe", "act", va)
                for g in range(ngr):
                    w = min(512, ncols - g * 512)
                    vp = b.inc("pe", nc.tensor.matmul(
                        zx[:, g * 512:g * 512 + w], onesb[:, :],
                        x2b[dc % 2][:, g * 512:g * 512 + w],
                        start=(dc == 0), stop=(dc == NCH - 1)))
                vsq[dc] = vp
            b.wait("dve", "pe", vp)
            b.pre("dve").inc("dve", nc.vector.tensor_scalar_mul(sc_r, zx[:, :], 1.0 / D))
            b.pre("dve").inc("dve", nc.vector.tensor_mul(rs_r, m_r, m_r))
            vd = b.pre("dve").inc("dve", nc.vector.tensor_sub(sc_r, sc_r, rs_r))
            b.wait("act", "dve", vd)
            b.pre("act").inc("act", nc.scalar.activation(sc_r, sc_r, AF.Ln, bias=consts["eps"][0:1, 0:1]))
            va = b.pre("act").inc("act", nc.scalar.activation(rs_r, sc_r, AF.Exp, scale=-0.5))
            b.wait("dve", "act", va)
            vd = b.pre("dve").inc("dve", nc.vector.tensor_mul(bm_r, m_r, rs_r))
        with ExitStack() as pst:
            nbc = min(ncols, 1024)
            pa = b.ps("ln_pa", [128, nbc], F32, pst)
            pb = b.ps("ln_pb", [128, nbc], F32, pst)
            b.wait("pe", "act", va)
            b.wait("pe", "dve", vd)
            for c0 in range(0, ncols, nbc):
                if c0 > 0:
                    b.wait("pe", "act", va)
                for g in range(0, nbc, 512):
                    sl = slice(c0 + g, c0 + g + min(512, nbc - g))
                    psl = slice(g, g + min(512, nbc - g))
                    b.inc("pe", nc.tensor.matmul(pa[:, psl], onesf[:, :],
                                                 rs_r[:, sl], start=True, stop=True))
                    vp = b.inc("pe", nc.tensor.matmul(pb[:, psl], onesf[:, :],
                                                      bm_r[:, sl], start=True, stop=True))
                b.wait("act", "pe", vp)
                b.pre("act").inc("act", nc.scalar.copy(
                    ab[:, c0:c0 + nbc], pa[:, :]))
                va = b.pre("act").inc("act", nc.scalar.copy(
                    ab[:, ncols + c0:ncols + c0 + nbc], pb[:, :]))
        scr.close()
        scr = ExitStack()
        t1 = b.sb("ln_t1", [128, ncols], F32, scr)
        A = ab[:, 0:ncols]
        B = ab[:, ncols:2 * ncols]
        b.wait("dve", "act", va)
        if par_split is None:
            for dc in range(NCH):
                xsl = xbf[:, dc * ncols:(dc + 1) * ncols]
                b.pre("dve").inc("dve", nc.vector.tensor_mul(t1[:, :], xsl, A))
                vd = b.pre("dve").inc("dve", nc.vector.tensor_sub(
                    xn[:, dc * ncols:(dc + 1) * ncols], t1[:, :], B))
        else:
            xnp, chlists = par_split
            for par in range(PAR):
                for slot, dc in enumerate(chlists[par]):
                    xsl = xbf[:, dc * ncols + par:(dc + 1) * ncols:PAR]
                    b.pre("dve").inc("dve", nc.vector.tensor_mul(
                        t1[:, 0:ncols // PAR], xsl, A[:, par::PAR]))
                    vd = b.pre("dve").inc("dve", nc.vector.tensor_sub(
                        xnp[par][:, slot * (ncols // PAR):(slot + 1) * (ncols // PAR)],
                        t1[:, 0:ncols // PAR], B[:, par::PAR]))
        scr.close()
    return vd


def emit_w2_prep(b, w2t, w2tmp, w2dd, consts):
    nc = b.nc
    with ExitStack() as st:
        blk = [b.sb(f"wp_blk{i}", [128, D], BF16, st) for i in range(3)]
        ev = [b.sb(f"wp_ev{i}", [128, D], BF16, st) for i in range(2)]
        pp = [b.ps(f"wp_p{i}", [128, 1024], F32, st) for i in range(2)]
        vcopy = {}
        vstore = {}
        vmm = {}

        def p1_load(j1):
            b.dma("gp", f"wpg{j1 % 3}", blk[j1 % 3][:, :],
                  w2t[j1 * 128:(j1 + 1) * 128, :])

        for j in range(3):
            p1_load(j)
        for j1 in range(64):
            k = j1 % 3
            e = j1 % 2
            b.dwait("pe", f"wpg{k}")
            for half in range(2):
                b.wait("pe", "act", vcopy.get((j1 - 1, half), 0))
                for g in range(2):
                    o = half * 1024 + g * 512
                    vp = b.inc("pe", nc.tensor.matmul(
                        pp[half][:, g * 512:(g + 1) * 512], consts["h128"][:, :],
                        blk[k][:, o:o + 512], start=True, stop=True))
                vmm[j1] = vp
                b.wait("act", "pe", vp)
                if half == 0 and j1 >= 2:
                    b.dwait("act", f"wpo{e}")
                vcopy[(j1, half)] = b.pre("act").inc("act", nc.scalar.copy(
                    ev[e][:, half * 1024:(half + 1) * 1024], pp[half][:, :]))
            b.wait("sy", "act", vcopy[(j1, 1)])
            b.dma("sy", f"wpo{e}", w2tmp[j1 * 128:(j1 + 1) * 128, :], ev[e][:, :])
            if j1 + 3 < 64:
                b.wait("gp", "pe", vmm[j1])
                p1_load(j1 + 3)
        b.dwait("sy", "wpo0")
        b.dwait("sy", "wpo1")
        b.wait("gp", "pe")
        tmp_r = w2tmp.rearrange("(j1 k2) d -> k2 j1 d", k2=128)
        dd_r = w2dd.rearrange("(k1 k2) d -> k2 k1 d", k2=128)
        b.dwait("gp", "wpo0")
        b.dwait("gp", "wpo1")

        def p2_load(i):
            b.dma("gp", f"wps{i % 3}", blk[i % 3][:, :], tmp_r[2 * i:2 * i + 2])

        for i in range(3):
            p2_load(i)
        vcopy = {}
        vmm = {}
        for i in range(64):
            k = i % 3
            e = i % 2
            b.dwait("pe", f"wps{k}")
            for half in range(2):
                b.wait("pe", "act", vcopy.get((i - 1, half), 0))
                for g in range(2):
                    o = half * 1024 + g * 512
                    vp = b.inc("pe", nc.tensor.matmul(
                        pp[half][:, g * 512:(g + 1) * 512], consts["h64blk"][:, :],
                        blk[k][:, o:o + 512], start=True, stop=True))
                vmm[i] = vp
                b.wait("act", "pe", vp)
                if half == 0 and i >= 2:
                    b.dwait("act", f"wpd{e}")
                vcopy[(i, half)] = b.pre("act").inc("act", nc.scalar.activation(
                    ev[e][:, half * 1024:(half + 1) * 1024], pp[half][:, :],
                    AF.Copy, scale=FWHT_SCALE))
            b.wait("sy", "act", vcopy[(i, 1)])
            b.dma("sy", f"wpd{e}", dd_r[2 * i:2 * i + 2], ev[e][:, :])
            if i + 3 < 64:
                b.wait("gp", "pe", vmm[i])
                p2_load(i + 3)
        b.dwait("sy", "wpd0")
        b.dwait("sy", "wpd1")


def _load_band(b, w1t, w3t, d1, d3, band):
    sl = slice(band * 256, (band + 1) * 256)
    ds = f"fab{band % 2}"
    b.dma("gp", ds, d1[:, :].rearrange("dp (dc h) -> dp dc h", dc=NCH),
          w1t[:, sl].rearrange("(dc dp) h -> dp dc h", dp=128))
    return b.dma("gp", ds, d3[:, :].rearrange("dp (dc h) -> dp dc h", dc=NCH),
                 w3t[:, sl].rearrange("(dc dp) h -> dp dc h", dp=128))


def emit_ffn(b, xres_src, dst_hbm, w1t, w3t, w2dd, hid, consts, prep_args=None):
    nc = b.nc
    with ExitStack() as st:
        with ExitStack() as ast:
            xn = b.sb("fa_xn", [128, NCH * S], BF16, ast)
            prep_gen = None
            if prep_args is not None:
                prep_gen = w2_prep_steps(b, prep_args[0], prep_args[1],
                                         w2dd, consts, ast)
            w1b = [b.sb(f"fa_w1{i}", [128, NCH * 256], BF16, ast) for i in range(2)]
            w3b = [b.sb(f"fa_w3{i}", [128, NCH * 256], BF16, ast) for i in range(2)]
            sg = [b.sb(f"fa_sg{i}", [128, 1024], F32, ast) for i in range(2)]
            hbuf = [b.sb(f"fa_hb{i}", [128, 1024], BF16, ast) for i in range(2)]
            wload = [_load_band(b, w1t, w3t, w1b[0], w3b[0], 0),
                     _load_band(b, w1t, w3t, w1b[1], w3b[1], 1)]
            vxn = emit_ln(b, xres_src, xn, 0, S, consts)
            b.wait("pe", "dve", vxn)
            if prep_gen is not None:
                pg = [b.ps(f"fa_pg{i}", [128, 1024], F32, ast) for i in range(2)]
                pl = [b.ps("fa_pl", [128, 1024], F32, ast)]
            hseq = []
            if prep_gen is None:
                with ExitStack() as wps:
                    pgw = b.ps("fa_pgw", [128, 2048], F32, wps)
                    plw = b.ps("fa_plw", [128, 2048], F32, wps)
                    sgw = [b.sb(f"fa_sgw{i}", [128, 2048], F32, wps)
                           for i in range(2)]
                    hbw = [b.sb(f"fa_hbw{i}", [128, 2048], BF16, wps)
                           for i in range(2)]
                    vsil = {}
                    vmul = {}
                    for hc in range(NHID):
                        band, hcl = hc // 2, hc % 2
                        k = band % 2
                        b.dwait("pe", f"fab{k}")
                        q = hc % 2
                        if hc >= 1:
                            b.wait("pe", "act", vsil[hc - 1])
                        for dc in range(NCH):
                            l1 = w1b[k][:, dc * 256 + hcl * 128:
                                        dc * 256 + hcl * 128 + 128]
                            for g in range(4):
                                vpg = b.inc("pe", nc.tensor.matmul(
                                    pgw[:, g * 512:(g + 1) * 512], l1,
                                    xn[:, dc * S + g * 512:dc * S + g * 512 + 512],
                                    start=(dc == 0), stop=(dc == NCH - 1)))
                        if hc >= 2:
                            b.wait("act", "dve", vmul[hc - 2])
                        b.wait("act", "pe", vpg)
                        vsil[hc] = b.pre("act").inc("act", nc.scalar.activation(
                            sgw[q][:, :], pgw[:, :], AF.Silu))
                        if hc >= 1:
                            b.wait("pe", "dve", vmul[hc - 1])
                        for dc in range(NCH):
                            l3 = w3b[k][:, dc * 256 + hcl * 128:
                                        dc * 256 + hcl * 128 + 128]
                            for g in range(4):
                                vp = b.inc("pe", nc.tensor.matmul(
                                    plw[:, g * 512:(g + 1) * 512], l3,
                                    xn[:, dc * S + g * 512:dc * S + g * 512 + 512],
                                    start=(dc == 0), stop=(dc == NCH - 1)))
                        b.dwait("dve", f"fah{q}")
                        b.wait("dve", "act", vsil[hc])
                        b.wait("dve", "pe", vp)
                        vmul[hc] = b.pre("dve").inc("dve", nc.vector.tensor_mul(
                            hbw[q][:, :], sgw[q][:, :], plw[:, :]))
                        b.wait("sy", "dve", vmul[hc])
                        b.dma("sy", f"fah{q}", hid.ap()[0, hc], hbw[q][:, 0:1024])
                        b.dma("sy", f"fah{q}", hid.ap()[1, hc], hbw[q][:, 1024:2048])
                        if hc % 2 == 1 and band + 2 < 32:
                            b.wait("gp", "pe")
                            wload.append(_load_band(b, w1t, w3t, w1b[k], w3b[k],
                                                    band + 2))
                b.dwait("sy", "fah0")
                b.dwait("sy", "fah1")
            else:
              for band in range(32):
                k = band % 2
                b.dwait("pe", f"fab{band % 2}")
                for hcl in range(2):
                    hc = band * 2 + hcl
                    for th in range(2):
                        idx = hc * 2 + th
                        q = idx % 2
                        if len(hseq) >= 2:
                            b.wait("pe", "act", hseq[-2][0])
                        xo0 = th * 1024
                        for dc in range(NCH):
                            l1 = w1b[k][:, dc * 256 + hcl * 128:
                                        dc * 256 + hcl * 128 + 128]
                            xb = dc * S + xo0
                            b.inc("pe", nc.tensor.matmul(
                                pg[q][:, 0:512], l1, xn[:, xb:xb + 512],
                                start=(dc == 0), stop=(dc == NCH - 1)))
                            vpg = b.inc("pe", nc.tensor.matmul(
                                pg[q][:, 512:1024], l1, xn[:, xb + 512:xb + 1024],
                                start=(dc == 0), stop=(dc == NCH - 1)))
                        if len(hseq) >= 2:
                            b.wait("act", "dve", hseq[-2][1])
                        b.wait("act", "pe", vpg)
                        va = b.pre("act").inc("act", nc.scalar.activation(
                            sg[q][:, :], pg[q][:, :], AF.Silu))
                        if hseq:
                            b.wait("pe", "dve", hseq[-1][1])
                        for dc in range(NCH):
                            l3 = w3b[k][:, dc * 256 + hcl * 128:
                                        dc * 256 + hcl * 128 + 128]
                            xb = dc * S + xo0
                            b.inc("pe", nc.tensor.matmul(
                                pl[0][:, 0:512], l3, xn[:, xb:xb + 512],
                                start=(dc == 0), stop=(dc == NCH - 1)))
                            vp = b.inc("pe", nc.tensor.matmul(
                                pl[0][:, 512:1024], l3, xn[:, xb + 512:xb + 1024],
                                start=(dc == 0), stop=(dc == NCH - 1)))
                        b.dwait("dve", f"fah{q}")
                        b.wait("dve", "act", va)
                        b.wait("dve", "pe", vp)
                        vd = b.pre("dve").inc("dve", nc.vector.tensor_mul(
                            hbuf[q][:, :], sg[q][:, :], pl[0][:, :]))
                        b.wait("sy", "dve", vd)
                        vo = b.dma("sy", f"fah{q}", hid.ap()[th, hc], hbuf[q][:, :])
                        if prep_gen is not None:
                            next(prep_gen, None)
                        hseq.append((va, vd, vo))
                if band < 30:
                    b.wait("gp", "pe")
                    wload.append(_load_band(b, w1t, w3t, w1b[k], w3b[k], band + 2))
            b.dwait("sy", "fah0")
            b.dwait("sy", "fah1")
            if prep_gen is not None:
                for _ in prep_gen:
                    pass
                b.dwait("sy", "wpd0")
                b.dwait("sy", "wpd1")
        b.bar()
        with ExitStack() as bst:
            hstr = b.sb("fb_h", [128, NHID * 1024], BF16, bst)
            w2s = [b.sb(f"fb_w2{i}", [128, 256], BF16, bst) for i in range(8)]
            xr = [b.sb(f"fb_xr{i}", [128, 1024], F32, bst) for i in range(2)]
            ox = [b.sb(f"fb_ox{i}", [128, 1024], F32, bst) for i in range(2)]
            pb = [[b.ps(f"fb_p{i}{j}", [128, 1024], F32, bst) for j in range(2)]
                  for i in range(2)]
            hs_r = hstr[:, :].rearrange("dp (hc t) -> dp hc t", hc=NHID)
            for ts in range(2):
                if ts > 0:
                    b.wait("sy", "pe")
                    b.wait("ac", "pe")
                for part in range(8):
                    qn = "sy" if part % 2 == 0 else "ac"
                    b.dma(qn, "fbh" + qn, hs_r[:, part * 8:(part + 1) * 8],
                          hid.ap()[ts, part * 8:(part + 1) * 8].rearrange(
                              "hc dp t -> dp hc t"))
                pv = {}
                b.wait("ac", "pe")
                for pre in range(6):
                    b.dma("ac", f"fbw{pre}", w2s[pre][:, :],
                          w2dd[pre * 128:(pre + 1) * 128, 0:256])
                b.dwait("pe", "fbhsy")
                b.dwait("pe", "fbhac")
                ev = []
                evx = []
                for db in range(8):
                    q = db % 2
                    if db >= 2:
                        b.wait("pe", "dve", ev[db - 2][0])
                    for h in range(NHID):
                        idx = db * NHID + h
                        b.dwait("pe", f"fbw{idx % 8}")
                        ws = w2s[idx % 8]
                        for ds in range(2):
                            lh = ws[:, ds * 128:(ds + 1) * 128]
                            for th in range(2):
                                vp = b.inc("pe", nc.tensor.matmul(
                                    pb[q][ds][:, th * 512:(th + 1) * 512], lh,
                                    hstr[:, h * 1024 + th * 512:h * 1024 + th * 512 + 512],
                                    start=(h == 0), stop=(h == NHID - 1)))
                        pv[idx] = vp
                        nidx = idx + 6
                        if nidx < 8 * NHID:
                            ndb, nh = divmod(nidx, NHID)
                            if nidx >= 8:
                                b.wait("ac", "pe", pv[nidx - 8])
                            b.dma("ac", f"fbw{nidx % 8}", w2s[nidx % 8][:, :],
                                  w2dd[nh * 128:(nh + 1) * 128,
                                       ndb * 256:(ndb + 1) * 256])
                    for ds in range(2):
                        r0 = db * 256 + ds * 128
                        kx = (db * 2 + ds) % 2
                        if len(evx) >= 2:
                            b.wait("sy", "dve", evx[-2][0])
                        b.dma("sy", f"fbx{kx}", xr[kx][:, :],
                              xres_src[r0:r0 + 128, ts * 1024:(ts + 1) * 1024])
                        if len(evx) >= 2:
                            b.dwait("dve", f"fbo{kx}")
                        b.dwait("dve", f"fbx{kx}")
                        b.wait("dve", "pe", vp)
                        vd = b.pre("dve").inc("dve", nc.vector.tensor_add(
                            ox[kx][:, :], pb[q][ds][:, :], xr[kx][:, :]))
                        b.wait("sy", "dve", vd)
                        vo = b.dma("sy", f"fbo{kx}", dst_hbm[r0:r0 + 128,
                                   ts * 1024:(ts + 1) * 1024], ox[kx][:, :])
                        evx.append((vd, vo))
                    ev.append((vd, vo))
                b.dwait("sy", "fbo0")
                b.dwait("sy", "fbo1")
        b.bar()


def w2_prep_steps(b, w2t, w2tmp, w2dd, consts, st):
    nc = b.nc
    blk = [b.sb(f"wp_blk{i}", [128, D], BF16, st) for i in range(3)]
    ev = [b.sb(f"wp_ev{i}", [128, D], BF16, st) for i in range(2)]
    pp = [b.ps(f"wp_p{i}", [128, 512], F32, st) for i in range(2)]

    def p1_load(j1):
        b.dma("gp", f"wpg{j1 % 3}", blk[j1 % 3][:, :],
              w2t[j1 * 128:(j1 + 1) * 128, :])

    for j in range(3):
        p1_load(j)
    return _w2_gen(b, w2t, w2tmp, w2dd, consts, blk, ev, pp, p1_load)


def _w2_gen(b, w2t, w2tmp, w2dd, consts, blk, ev, pp, p1_load):
    nc = b.nc

    def one_pass(load_stream, out_stream, hmat, scale, dst_r, loader):
        vq = {}
        vmm = {}

        def quarter(it, qr):
            e = it % 2
            p = qr % 2
            eng = "act" if p == 0 else "dve"
            b.dwait("pe", f"{load_stream}{it % 3}")
            b.wait("pe", eng, vq.get((it, qr - 2), vq.get((it - 1, qr + 2), 0)))
            vp = b.inc("pe", nc.tensor.matmul(
                pp[p][:, :], hmat[:, :],
                blk[it % 3][:, qr * 512:(qr + 1) * 512], start=True, stop=True))
            if qr == 3:
                vmm[it] = vp
            b.wait(eng, "pe", vp)
            if qr < 2 and it >= 2:
                b.dwait(eng, f"{out_stream}{e}")
            dsl = ev[e][:, qr * 512:(qr + 1) * 512]
            if eng == "act":
                vq[(it, qr)] = b.pre("act").inc("act", nc.scalar.activation(
                    dsl, pp[p][:, :], AF.Copy, scale=scale))
            elif scale != 1.0:
                vq[(it, qr)] = b.pre("dve").inc(
                    "dve", nc.vector.tensor_scalar_mul(dsl, pp[p][:, :], scale))
            else:
                vq[(it, qr)] = b.pre("dve").inc(
                    "dve", nc.vector.tensor_copy(dsl, pp[p][:, :]))

        quarter(0, 0)
        quarter(0, 1)
        yield
        for it in range(64):
            quarter(it, 2)
            quarter(it, 3)
            e = it % 2
            b.wait("sy", "act", vq[(it, 2)])
            b.wait("sy", "dve", vq[(it, 3)])
            b.dma("sy", f"{out_stream}{e}", dst_r(it), ev[e][:, :])
            if it + 3 < 64:
                b.wait("gp", "pe", vmm[it])
                loader(it + 3)
            if it + 1 < 64:
                quarter(it + 1, 0)
                quarter(it + 1, 1)
            yield

    yield from one_pass("wpg", "wpo", consts["h128"], 1.0,
                        lambda it: w2tmp[it * 128:(it + 1) * 128, :], p1_load)
    tmp_r = w2tmp.rearrange("(j1 k2) d -> k2 j1 d", k2=128)
    dd_r = w2dd.rearrange("(k1 k2) d -> k2 k1 d", k2=128)
    b.dwait("gp", "wpo0")
    b.dwait("gp", "wpo1")
    b.wait("gp", "pe")

    def p2_load(i):
        b.dma("gp", f"wps{i % 3}", blk[i % 3][:, :], tmp_r[2 * i:2 * i + 2])

    for i in range(3):
        p2_load(i)
    yield from one_pass("wps", "wpd", consts["h64blk"], FWHT_SCALE,
                        lambda it: dd_r[2 * it:2 * it + 2], p2_load)


def emit_attn_a(b, x_in, xres, consts, prep_gen=None):
    nc = b.nc
    chl = [[h for h in range(NHEADS) if h % 2 == p] for p in range(PAR)]
    slots = [(par, sl) for par in range(PAR) for sl in range(8)]
    for seg in range(NSEGS):
        c0 = seg * SEG
        with ExitStack() as st:
            xf = b.sb("aa_xf", [128, NCH * SEG], F32, st)
            xnp = [b.sb(f"aa_xnp{p}", [128, 8 * SEGH], BF16, st) for p in range(PAR)]
            gt = [b.sb(f"aa_gt{i}", [128, 2 * SEGH], BF16, st) for i in range(2)]
            et = [[b.sb(f"aa_et{i}{jb}", [128, SEGH], BF16, st) for jb in range(2)]
                  for i in range(2)]
            avs = [b.sb(f"aa_avs{i}", [128, SEGH], F32, st) for i in range(2)]
            rzr = [b.sb(f"aa_rz{i}", [1, SEGH], F32, st) for i in range(2)]
            an = b.sb("aa_an", [128, SEGH], F32, st)
            for dc in range(NCH):
                b.dma("sy", "aax", xf[:, dc * SEG:(dc + 1) * SEG],
                      x_in[dc * 128:(dc + 1) * 128, c0:c0 + SEG])
            vxn = emit_ln(b, x_in, None, c0, SEG, consts,
                          par_split=(xnp, chl), src_sbuf=(xf, "aax"))
            b.wait("pe", "dve", vxn)
            b.wait("sy", "dve", vxn)
            with ExitStack() as pst:
                psc = [b.ps(f"aa_psc{i}", [128, SEGH], F32, pst) for i in range(2)]
                pz = [b.ps(f"aa_pz{i}", [1, SEGH], F32, pst) for i in range(2)]
                pav = [b.ps(f"aa_pav{i}", [128, SEGH], F32, pst) for i in range(2)]
                prz = [b.ps(f"aa_prz{i}", [128, SEGH], F32, pst) for i in range(2)]
                stt = {"vas": {}, "vzav": {}, "vrz": [0, 0], "vavs": [0, 0],
                       "vpb": [0, 0], "vtail": [0, 0], "tail": 0}

                def stage1(s):
                    par, sl = slots[s]
                    g = xnp[par][:, sl * SEGH:(sl + 1) * SEGH]
                    sb = s % 2
                    if s >= 2:
                        b.wait("sy", "pe", stt["vzav"].get(s - 2, 0))
                    for jb in range(2):
                        b.dma("sy", f"agt{sb}", gt[sb][:, jb * 128:(jb + 1) * 128],
                              g[:, jb * 128:(jb + 1) * 128], transpose=True)
                    for jb in range(2):
                        b.wait("pe", "act", stt["vas"].get((s - 1, jb), 0))
                        vp = b.inc("pe", nc.tensor.matmul(
                            psc[jb][:, :], g[:, jb * 128:(jb + 1) * 128], g,
                            start=True, stop=True))
                        b.wait("act", "pe", vp)
                        if s >= 2:
                            b.wait("act", "pe", stt["vzav"].get(s - 2, 0))
                        stt["vas"][(s, jb)] = b.pre("act").inc(
                            "act", nc.scalar.activation(
                                et[sb][jb][:, :], psc[jb][:, :], AF.Exp,
                                bias=consts["shift"][:, :], scale=INV_SQRT_HD))

                def zav_recip(s):
                    par, sl = slots[s]
                    sb = s % 2
                    e = s % 2
                    b.wait("pe", "act", stt["vrz"][e])
                    b.wait("pe", "act", stt["vavs"][e])
                    b.dwait("pe", f"agt{sb}")
                    for jb in range(2):
                        b.wait("pe", "act", stt["vas"][(s, jb)])
                        b.inc("pe", nc.tensor.matmul(
                            pz[e][:, :], consts["ones_bf"][:, :], et[sb][jb][:, :],
                            start=(jb == 0), stop=(jb == 1)))
                        vp = b.inc("pe", nc.tensor.matmul(
                            pav[e][:, :], gt[sb][:, jb * 128:(jb + 1) * 128],
                            et[sb][jb][:, :], start=(jb == 0), stop=(jb == 1)))
                    stt["vzav"][s] = vp
                    b.wait("act", "pe", vp)
                    if stt["vpb"][e]:
                        b.wait("act", "pe", stt["vpb"][e])
                    b.pre("act").inc("act", nc.scalar.activation(
                        rzr[e][:, :], pz[e][:, :], AF.Ln))
                    stt["vrz"][e] = b.pre("act").inc("act", nc.scalar.activation(
                        rzr[e][:, :], rzr[e][:, :], AF.Exp, scale=-1.0))

                def tail(s):
                    par, sl = slots[s]
                    h = chl[par][sl]
                    e = s % 2
                    b.wait("pe", "act", stt["vrz"][e])
                    b.wait("pe", "dve", stt["vtail"][e])
                    stt["vpb"][e] = b.inc("pe", nc.tensor.matmul(
                        prz[e][:, :], consts["ones_f32"][:, :], rzr[e][:, :],
                        start=True, stop=True))
                    b.wait("act", "dve", stt["vtail"][e])
                    b.wait("act", "pe", stt["vzav"][s])
                    va = stt["vavs"][e] = b.pre("act").inc(
                        "act", nc.scalar.copy(avs[e][:, :], pav[e][:, :]))
                    b.wait("dve", "act", va)
                    b.wait("dve", "pe", stt["vpb"][e])
                    b.dwait("dve", "aax")
                    b.pre("dve").inc("dve", nc.vector.tensor_mul(
                        an[:, :], avs[e][:, :], prz[e][:, :]))
                    xsl = xf[:, h * SEG + par:(h + 1) * SEG:PAR]
                    stt["vtail"][e] = stt["tail"] = b.pre("dve").inc(
                        "dve", nc.vector.tensor_add(xsl, xsl, an[:, :]))

                stage1(0)
                stage1(1)
                for s in range(16):
                    zav_recip(s)
                    if s + 2 < 16:
                        stage1(s + 2)
                    if s >= 1:
                        tail(s - 1)
                tail(15)
            b.wait("sy", "dve")
            for dc in range(NCH):
                b.dma("sy", "aar", xres[dc * 128:(dc + 1) * 128, c0:c0 + SEG],
                      xf[:, dc * SEG:(dc + 1) * SEG])
            b.dwait("sy", "aar")
        b.bar()


def emit_attn_b(b, xres, wqt, wkt, wvt, wot, consts):
    nc = b.nc
    with ExitStack() as st:
        ao = b.sb("ab_ao", [128, NCH * S], BF16, st)
        with ExitStack() as hst:
            xn = b.sb("ab_xn", [128, NCH * S], BF16, hst)
            vxn = emit_ln(b, xres, xn, 0, S, consts)
            b.wait("pe", "dve", vxn)
            b.wait("gp", "dve", vxn)
            b.wait("gp", "act")
            qw = [b.sb(f"ab_qw{i}", [128, NCH * 128], BF16, hst) for i in range(2)]
            kw = [b.sb(f"ab_kw{i}", [128, NCH * 128], BF16, hst) for i in range(2)]
            vw = [b.sb(f"ab_vw{i}", [128, NCH * 128], BF16, hst) for i in range(2)]
            qh = [b.sb(f"ab_qh{i}", [128, S], BF16, hst) for i in range(2)]
            kh = [b.sb(f"ab_kh{i}", [128, S], BF16, hst) for i in range(2)]
            vh = [b.sb(f"ab_vh{i}", [128, S], BF16, hst) for i in range(2)]
            vt = [b.sb(f"ab_vt{i}", [128, S], BF16, hst) for i in range(2)]
            et = [b.sb(f"ab_et{i}", [128, 512], BF16, hst) for i in range(2)]
            avs = [b.sb(f"ab_avs{i}", [128, 512], F32, hst) for i in range(2)]
            rzr = [b.sb(f"ab_rz{i}", [1, 512], F32, hst) for i in range(2)]

            def load_head_w(h):
                for wt, dst in ((wqt, qw), (wkt, kw), (wvt, vw)):
                    b.dma("gp", f"abw{h % 2}", dst[h % 2][:, :].rearrange(
                        "dp (dc o) -> dp dc o", dc=NCH),
                        wt[:, h * 128:(h + 1) * 128].rearrange(
                            "(dc dp) o -> dp dc o", dp=128))

            load_head_w(0)
            load_head_w(1)
            with ExitStack() as pst:
                pq = [b.ps("ab_pq", [128, 512], F32, pst)]
                psc = [b.ps(f"ab_psc{i}", [128, 512], F32, pst) for i in range(2)]
                pz = [b.ps(f"ab_pz{i}", [1, 512], F32, pst) for i in range(2)]
                pav = [b.ps(f"ab_pav{i}", [128, 512], F32, pst) for i in range(2)]
                prz = b.ps("ab_prz", [128, 512], F32, pst)
                stt = {"pq_prev": [], "vtr": 0, "qv": 0, "pe_proj": {},
                       "ready": {}, "vhva": {}}

                def qkv_units(h):
                    k = h % 2

                    def mk_proj(di, tg):
                        def emit():
                            dst = (qh[k], kh[k], vh[k])[di]
                            wsb = (qw[k], kw[k], vw[k])[di]
                            q = 0
                            b.dwait("pe", f"abw{k}")
                            if stt["pq_prev"]:
                                b.wait("pe", "dve", stt["pq_prev"][-1])
                            for dc in range(NCH):
                                vp = b.inc("pe", nc.tensor.matmul(
                                    pq[q][:, :], wsb[:, dc * 128:(dc + 1) * 128],
                                    xn[:, dc * S + tg * 512:dc * S + tg * 512 + 512],
                                    start=(dc == 0), stop=(dc == NCH - 1)))
                            if di == 2 and tg == 3:
                                stt["pe_proj"][h] = vp
                            b.wait("dve", "pe", vp)
                            va = b.pre("dve").inc("dve", nc.vector.tensor_copy(
                                dst[:, tg * 512:(tg + 1) * 512], pq[q][:, :]))
                            stt["pq_prev"].append(va)
                            if di == 2:
                                stt["vhva"][(h, tg)] = va
                        return emit

                    def mk_vt(j):
                        def emit():
                            b.wait("sy", "dve", stt["vhva"][(h, j // 4)])
                            if j == 0:
                                b.wait("sy", "pe")
                            b.dma("sy", f"vtr{k}", vt[k][:, j * 128:(j + 1) * 128],
                                  vh[k][:, j * 128:(j + 1) * 128], transpose=True)
                            if j == 15:
                                stt["ready"][h] = stt["pq_prev"][-1]
                        return emit

                    return ([mk_proj(di, tg) for di in range(3) for tg in range(4)]
                            + [mk_vt(j) for j in range(16)])

                for u in qkv_units(0):
                    u()
                vd_tails = [0, 0]
                vpb_prev = [0, 0]
                vrz_prev = [0, 0]
                vavs_prev = [0, 0]
                vzav_e = [0, 0]
                pend = [None]

                def emit_tail():
                    if pend[0] is None:
                        return
                    te, ao_sl = pend[0]
                    pend[0] = None
                    b.wait("pe", "act", vrz_prev[te])
                    b.wait("pe", "dve", vd_tails[1 - te])
                    b.wait("pe", "dve", vd_tails[te])
                    vpb_prev[te] = b.inc("pe", nc.tensor.matmul(
                        prz[:, :], consts["ones_f32"][:, :], rzr[te][:, :],
                        start=True, stop=True))
                    b.wait("act", "dve", vd_tails[te])
                    b.wait("act", "pe", vzav_e[te])
                    vavs_prev[te] = b.pre("act").inc("act", nc.scalar.copy(
                        avs[te][:, :], pav[te][:, :]))
                    b.wait("dve", "act", vavs_prev[te])
                    b.wait("dve", "pe", vpb_prev[te])
                    vd_tails[te] = b.pre("dve").inc("dve", nc.vector.tensor_mul(
                        ao_sl, avs[te][:, :], prz[:, :]))

                for h in range(NHEADS):
                    k = h % 2
                    units = qkv_units(h + 1) if h + 1 < NHEADS else []
                    nslots = 44
                    slot = [0, 0]

                    def tick():
                        slot[0] += 1
                        tgt = (slot[0] * len(units)) // nslots
                        while slot[1] < min(tgt, len(units)):
                            units[slot[1]]()
                            slot[1] += 1

                    if h + 2 < NHEADS:
                        b.wait("gp", "pe", stt["pe_proj"][h])
                        load_head_w(h + 2)
                    b.dwait("pe", f"vtr{k}")
                    b.wait("pe", "dve", stt["ready"][h])
                    for ig in range(4):
                        jmax = 4 * ig + 3
                        vas = {}
                        vzav = {}

                        e = (h * 4 + ig) % 2

                        def emit_zav(j, start, stop):
                            q = j % 2
                            osj = max(0, j * 128 - ig * 512)
                            b.wait("pe", "act", vas[j])
                            b.inc("pe", nc.tensor.matmul(
                                pz[e][:, osj:512], consts["ones_bf"][:, :],
                                et[q][:, osj:512], start=start, stop=stop))
                            vzav[j] = b.inc("pe", nc.tensor.matmul(
                                pav[e][:, osj:512], vt[k][:, j * 128:(j + 1) * 128],
                                et[q][:, osj:512], start=start, stop=stop))

                        b.wait("pe", "act", vrz_prev[e])
                        b.wait("pe", "act", vavs_prev[e])
                        for j in range(jmax + 1):
                            osj = max(0, j * 128 - ig * 512)
                            q = j % 2
                            if j >= 2:
                                b.wait("pe", "act", vas[j - 2])
                            vp = b.inc("pe", nc.tensor.matmul(
                                psc[q][:, osj:512], kh[k][:, j * 128:(j + 1) * 128],
                                qh[k][:, ig * 512 + osj:(ig + 1) * 512],
                                start=True, stop=True))
                            vdm = 0
                            if j * 128 >= ig * 512:
                                b.wait("dve", "pe", vp)
                                vdm = b.pre("dve").inc("dve", nc.vector.tensor_add(
                                    psc[q][:, osj:osj + 128],
                                    psc[q][:, osj:osj + 128], consts["dmask"][:, :]))
                            if j >= 1:
                                emit_zav(j - 1, j - 1 == 0, False)
                            tick()
                            b.wait("act", "pe", vp)
                            if vdm:
                                b.wait("act", "dve", vdm)
                            if j >= 2:
                                b.wait("act", "pe", vzav[j - 2])
                            vas[j] = b.pre("act").inc("act", nc.scalar.activation(
                                et[q][:, osj:512], psc[q][:, osj:512], AF.Exp,
                                bias=consts["shift"][:, :], scale=INV_SQRT_HD))
                        emit_zav(jmax, jmax == 0, True)
                        vp = vzav[jmax]
                        vzav_e[e] = vp
                        b.wait("act", "pe", vp)
                        if vpb_prev[e]:
                            b.wait("act", "pe", vpb_prev[e])
                        b.pre("act").inc("act", nc.scalar.activation(
                            rzr[e][:, :], pz[e][:, :], AF.Ln))
                        vrz_prev[e] = b.pre("act").inc(
                            "act", nc.scalar.activation(
                                rzr[e][:, :], rzr[e][:, :], AF.Exp, scale=-1.0))
                        pend_new = (e, ao[:, h * S + ig * 512:h * S + (ig + 1) * 512])
                        emit_tail()
                        pend[0] = pend_new
                        tick()
                    for u in units[slot[1]:]:
                        u()
                    slot[1] = len(units)
                emit_tail()
                b.wait("pe", "dve", vd_tails[0])
                b.wait("pe", "dve", vd_tails[1])
        b.bar()
        with ExitStack() as wst:
            wob = [b.sb(f"ab_wo{i}", [128, NCH * 128], BF16, wst) for i in range(3)]
            xr = [b.sb(f"ab_xr{i}", [128, 512], F32, wst) for i in range(3)]
            oxt = [b.sb(f"ab_ox{i}", [128, 512], F32, wst) for i in range(3)]
            pwo = [b.ps(f"ab_pw{i}", [128, 512], F32, wst) for i in range(3)]

            def load_wo(do):
                return b.dma(
                    "gp", f"awo{do % 3}",
                    wob[do % 3][:, :].rearrange("dp (dc o) -> dp dc o", dc=NCH),
                    wot[:, do * 128:(do + 1) * 128].rearrange(
                        "(dc dp) o -> dp dc o", dp=128))

            load_wo(0)
            load_wo(1)
            pv_wo = {}
            seq = []
            for do in range(NCH):
                k = do % 3
                b.dwait("pe", f"awo{do % 3}")
                if do + 2 < NCH:
                    if do >= 1:
                        b.wait("gp", "pe", pv_wo[do - 1])
                    load_wo(do + 2)
                for tg in range(4):
                    idx = do * 4 + tg
                    q = idx % 3
                    if idx >= 3:
                        b.wait("pe", "dve", seq[idx - 3][0])
                    for dc in range(NCH):
                        vp = b.inc("pe", nc.tensor.matmul(
                            pwo[q][:, :], wob[k][:, dc * 128:(dc + 1) * 128],
                            ao[:, dc * S + tg * 512:dc * S + tg * 512 + 512],
                            start=(dc == 0), stop=(dc == NCH - 1)))
                    if idx >= 3:
                        b.wait("ac", "dve", seq[idx - 3][0])
                    b.dma("ac", f"abx{q}", xr[q][:, :],
                          xres[do * 128:(do + 1) * 128, tg * 512:(tg + 1) * 512])
                    if idx >= 3:
                        b.dwait("dve", f"abo{q}")
                    b.dwait("dve", f"abx{q}")
                    b.wait("dve", "pe", vp)
                    vd = b.pre("dve").inc("dve", nc.vector.tensor_add(
                        oxt[q][:, :], pwo[q][:, :], xr[q][:, :]))
                    b.wait("sy", "dve", vd)
                    vo = b.dma("sy", f"abo{q}", xres[do * 128:(do + 1) * 128,
                               tg * 512:(tg + 1) * 512], oxt[q][:, :])
                    pv_wo[do] = vp
                    seq.append((vd, vo))
            b.dwait("sy", "abo0")
            b.dwait("sy", "abo1")
            b.dwait("sy", "abo2")
        b.bar()


def build_nc(phases=("attn_a", "prep", "ffn1", "attn_b", "ffn2")):
    nc = bass.Bass("TRN2", target_bir_lowering=False)
    x_in = nc.dram_tensor("x", [D, S], F32, kind="ExternalInput").ap()
    w1t = nc.dram_tensor("w1t", [D, HID], F32, kind="ExternalInput").ap()
    w3t = nc.dram_tensor("w3t", [D, HID], F32, kind="ExternalInput").ap()
    w2t = nc.dram_tensor("w2t", [HID, D], F32, kind="ExternalInput").ap()
    wqt = nc.dram_tensor("wqt", [D, D], F32, kind="ExternalInput").ap()
    wkt = nc.dram_tensor("wkt", [D, D], F32, kind="ExternalInput").ap()
    wvt = nc.dram_tensor("wvt", [D, D], F32, kind="ExternalInput").ap()
    wot = nc.dram_tensor("wot", [D, D], F32, kind="ExternalInput").ap()
    out = nc.dram_tensor("out", [D, S], F32, kind="ExternalOutput").ap()
    xres = nc.dram_tensor("xres", [D, S], F32, kind="Internal").ap()
    w2tmp = nc.dram_tensor("w2tmp", [HID, D], BF16, kind="Internal").ap()
    w2dd = nc.dram_tensor("w2dd", [HID, D], BF16, kind="Internal").ap()
    hid = nc.dram_tensor("hid", [2, NHID, 128, 1024], BF16, kind="Internal")

    with ExitStack() as st:
        b = Bld(nc, st)
        consts = emit_consts(b)
        if "attn_a" in phases:
            emit_attn_a(b, x_in, xres, consts)
        if "prep" in phases and "ffn1" not in phases:
            with ExitStack() as pst:
                for _ in w2_prep_steps(b, w2t, w2tmp, w2dd, consts, pst):
                    pass
                b.dwait("sy", "wpd0")
                b.dwait("sy", "wpd1")
            b.bar()
        if "ffn1" in phases:
            pa = (w2t, w2tmp) if "prep" in phases else None
            emit_ffn(b, xres, xres, w1t, w3t, w2dd, hid, consts, prep_args=pa)
        if "attn_b" in phases:
            emit_attn_b(b, xres, wqt, wkt, wvt, wot, consts)
        if "ffn2" in phases:
            emit_ffn(b, xres, out, w1t, w3t, w2dd, hid, consts)
        b.dwait("sy", "fbo0")
        b.dwait("sy", "fbo1")
    return nc


def kernel(L0_attn_a_x, L0_attn_b_wq, L0_attn_b_wk, L0_attn_b_wv, L0_attn_b_wo,
           L0_ffn_W1, L0_ffn_W2, L0_ffn_W3):
    x = np.asarray(L0_attn_a_x, dtype=np.float32)
    wshared = {
        "w1t": np.ascontiguousarray(np.asarray(L0_ffn_W1, np.float32).T),
        "w3t": np.ascontiguousarray(np.asarray(L0_ffn_W3, np.float32).T),
        "w2t": np.ascontiguousarray(np.asarray(L0_ffn_W2, np.float32).T),
        "wqt": np.ascontiguousarray(np.asarray(L0_attn_b_wq, np.float32).T),
        "wkt": np.ascontiguousarray(np.asarray(L0_attn_b_wk, np.float32).T),
        "wvt": np.ascontiguousarray(np.asarray(L0_attn_b_wv, np.float32).T),
        "wot": np.ascontiguousarray(np.asarray(L0_attn_b_wo, np.float32).T),
    }
    nc = build_nc()
    in_maps = [dict(wshared, x=np.ascontiguousarray(x[i].T)) for i in range(8)]
    res = run_bass_kernel_spmd(nc, in_maps, core_ids=list(range(8)))
    outs = [np.ascontiguousarray(np.asarray(r["out"]).T) for r in res.results]
    return np.stack(outs, axis=0)


if __name__ == "__main__":
    nc = build_nc()
    print("instructions:", nc.n_instructions())


# version 17
# speedup vs baseline: 1.1105x; 1.0042x over previous
import numpy as np
from contextlib import ExitStack

import concourse.bass as bass
import concourse.mybir as mybir
from concourse.bass_utils import run_bass_kernel_spmd
from concourse import dve_ops

F32 = mybir.dt.float32
BF16 = mybir.dt.bfloat16
AF = mybir.ActivationFunctionType
ALU = mybir.AluOpType

D = 2048
S = 2048
HID = 8192
NCH = D // 128
NHID = HID // 128
NHEADS = 16
SEG = 512
NSEGS = S // SEG
PAR = 2
SEGH = SEG // PAR
INV_SQRT_HD = 0.08838834764831845
EXP_SHIFT = -15.0
LN_EPS = 1e-5
FWHT_SCALE = float(1.0 / np.sqrt(HID))


def _hadamard(n):
    h = np.array([[1.0]])
    while h.shape[0] < n:
        h = np.block([[h, h], [h, -h]])
    return h


def _bf16_u16(a):
    import ml_dtypes
    return np.ascontiguousarray(np.asarray(a, dtype=ml_dtypes.bfloat16).view(np.uint16))


class Bld:

    def __init__(self, nc, st):
        self.nc = nc
        self.st = st
        self.sems = {}
        self.val = {}
        self.waited = {}
        for name in ("pe", "act", "dve", "bar"):
            self.sems[name] = st.enter_context(nc.semaphore(name))
            self.val[name] = 0
        self.streams = {}
        self.engs = {
            "pe": nc.tensor, "act": nc.scalar, "dve": nc.vector,
            "gp": nc.gpsimd, "sy": nc.sync,
        }
        self.alias = {"ac": "act"}

    def inc(self, name, ins, n=1):
        ins.then_inc(self.sems[name], n)
        self.val[name] += n
        return self.val[name]

    def pre(self, name):
        self.wait(name, name)
        return self

    def wait(self, eng_name, sem_name, v=None):
        v = self.val[sem_name] if v is None else v
        if v <= 0:
            return
        eng_name = self.alias.get(eng_name, eng_name)
        key = (eng_name, sem_name)
        if self.waited.get(key, -1) < v:
            self.engs[eng_name].wait_ge(self.sems[sem_name], v)
            self.waited[key] = v

    def ds(self, name):
        if name not in self.streams:
            self.streams[name] = [self.st.enter_context(self.nc.semaphore(name)), 0]
        return self.streams[name]

    def dma(self, q, stream, out, in_, **kw):
        s = self.ds(stream)
        ins = self.engs[self.alias.get(q, q)].dma_start(out, in_, **kw)
        ins.then_inc(s[0], 16)
        s[1] += 16
        return s[1]

    def dwait(self, eng_name, stream, v=None):
        s = self.ds(stream)
        v = s[1] if v is None else v
        if v <= 0:
            return
        eng_name = self.alias.get(eng_name, eng_name)
        key = (eng_name, "D" + stream)
        if self.waited.get(key, -1) < v:
            self.engs[eng_name].wait_ge(s[0], v)
            self.waited[key] = v

    def bar(self):
        self.val["bar"] += len(self.engs)
        tgt = self.val["bar"]
        for e in self.engs.values():
            e.sem_inc(self.sems["bar"], 1)
        for name, e in self.engs.items():
            e.wait_ge(self.sems["bar"], tgt)
            self.waited[(name, "bar")] = tgt
            for s in ("pe", "act", "dve"):
                k = (name, s)
                self.waited[k] = max(self.waited.get(k, -1), self.val[s])

    _uid = 0

    def sb(self, name, shape, dt, st=None):
        Bld._uid += 1
        return (st or self.st).enter_context(
            self.nc.sbuf_tensor(f"{name}_{Bld._uid}", shape, dt))

    def ps(self, name, shape, dt=F32, st=None):
        Bld._uid += 1
        return (st or self.st).enter_context(
            self.nc.psum_tensor(f"{name}_{Bld._uid}", shape, dt))


def emit_consts(b):
    nc = b.nc
    h128 = _hadamard(128)
    h64 = _hadamard(64)
    h64blk = np.zeros((128, 128))
    h64blk[:64, :64] = h64
    h64blk[64:, 64:] = h64
    ident = np.eye(128)
    jj, ii = np.meshgrid(np.arange(128), np.arange(128), indexing="ij")
    dmask = np.where(jj <= ii, 0.0, -1e30).astype(np.float32)

    consts = {
        "h128": b.sb("c_h128", [128, 128], BF16),
        "h64blk": b.sb("c_h64b", [128, 128], BF16),
        "ident": b.sb("c_id", [128, 128], BF16),
        "dmask": b.sb("c_dmask", [128, 128], F32),
        "ones_bf": b.sb("c_onesb", [128, 1], BF16),
        "ones_f32": b.sb("c_onesf", [1, 128], F32),
        "eps": b.sb("c_eps", [128, 1], F32),
        "shift": b.sb("c_shift", [128, 1], F32),
    }
    d_h128 = nc.inline_tensor(_bf16_u16(h128), "i_h128")
    d_h64b = nc.inline_tensor(_bf16_u16(h64blk), "i_h64b")
    d_id = nc.inline_tensor(_bf16_u16(ident), "i_id")
    d_mask = nc.inline_tensor(dmask, "i_dmask")
    b.dma("sy", "cst", consts["h128"][:, :], d_h128.ap().bitcast(BF16))
    b.dma("sy", "cst", consts["h64blk"][:, :], d_h64b.ap().bitcast(BF16))
    b.dma("sy", "cst", consts["ident"][:, :], d_id.ap().bitcast(BF16))
    b.dma("sy", "cst", consts["dmask"][:, :], d_mask.ap())
    b.pre("dve").inc("dve", nc.vector.memset(consts["ones_bf"][:, :], 1.0))
    b.pre("dve").inc("dve", nc.vector.memset(consts["ones_f32"][:, :], 1.0))
    b.pre("dve").inc("dve", nc.vector.memset(consts["eps"][:, :], LN_EPS))
    b.pre("dve").inc("dve", nc.vector.memset(consts["shift"][:, :], EXP_SHIFT))
    b.dwait("pe", "cst")
    b.wait("pe", "dve")
    b.dwait("dve", "cst")
    b.dwait("act", "cst")
    return consts


def emit_ln(b, src_hbm, xn, cols0, ncols, consts, par_split=None, src_sbuf=None):
    nc = b.nc
    onesb, onesf = consts["ones_bf"], consts["ones_f32"]
    ngr = (ncols + 511) // 512
    with ExitStack() as st:
        xbf = xn if par_split is None else b.sb("ln_xbf", [128, NCH * ncols], BF16, st)
        rows = b.sb("ln_rows", [1, 4 * ncols], F32, st)
        ab = b.sb("ln_ab", [128, 2 * ncols], F32, st)
        scr = ExitStack()
        xf = [b.sb(f"ln_xf{i}", [128, ncols], F32, scr) for i in range(2)]
        m_r = rows[:, 0:ncols]
        rs_r = rows[:, ncols:2 * ncols]
        bm_r = rows[:, 2 * ncols:3 * ncols]
        sc_r = rows[:, 3 * ncols:4 * ncols]
        with ExitStack() as pst:
            zx = b.ps("ln_zx", [1, ncols], F32, pst)
            vd = 0
            vp = 0
            vcast = {}

            def s1_mm(dc):
                nonlocal vp
                b.wait("pe", "dve", vcast[dc])
                for g in range(ngr):
                    w = min(512, ncols - g * 512)
                    vp = b.inc("pe", nc.tensor.matmul(
                        zx[:, g * 512:g * 512 + w], onesb[:, :],
                        xbf[:, dc * ncols + g * 512: dc * ncols + g * 512 + w],
                        start=(dc == 0), stop=(dc == NCH - 1)))

            for dc in range(NCH):
                if src_sbuf is not None:
                    sb_t, sb_stream = src_sbuf
                    b.dwait("dve", sb_stream)
                    buf = sb_t[:, dc * ncols:(dc + 1) * ncols]
                else:
                    buf = xf[dc % 2]
                    if dc >= 2:
                        b.wait("sy", "dve", ld2[0])
                    b.dma("sy", f"lnx{dc % 2}", buf[:, :],
                          src_hbm[dc * 128:(dc + 1) * 128, cols0:cols0 + ncols])
                    b.dwait("dve", f"lnx{dc % 2}")
                    buf = buf[:, :]
                    ld2 = (vd,)
                vd = vcast[dc] = b.pre("dve").inc("dve", nc.vector.tensor_copy(
                    xbf[:, dc * ncols:(dc + 1) * ncols], buf))
                if dc >= 1:
                    s1_mm(dc - 1)
            s1_mm(NCH - 1)
            b.wait("dve", "pe", vp)
            vd = b.pre("dve").inc("dve", nc.vector.tensor_scalar_mul(m_r, zx[:, :], 1.0 / D))
            scr.close()
            scr = ExitStack()
            x2b = [b.sb(f"ln_x2b{i}", [128, ncols], BF16, scr) for i in range(2)]
            b.wait("pe", "dve", vd)
            b.wait("act", "dve", vd)
            vsq = {}
            vsqa = {}

            def s2_mm(dc):
                nonlocal vp
                b.wait("pe", "act", vsqa[dc])
                for g in range(ngr):
                    w = min(512, ncols - g * 512)
                    vp = b.inc("pe", nc.tensor.matmul(
                        zx[:, g * 512:g * 512 + w], onesb[:, :],
                        x2b[dc % 2][:, g * 512:g * 512 + w],
                        start=(dc == 0), stop=(dc == NCH - 1)))
                vsq[dc] = vp

            for dc in range(NCH):
                if dc >= 2:
                    b.wait("act", "pe", vsq.get(dc - 2, 0))
                vsqa[dc] = b.pre("act").inc("act", nc.scalar.activation(
                    x2b[dc % 2][:, :], xbf[:, dc * ncols:(dc + 1) * ncols],
                    AF.Square))
                if dc >= 1:
                    s2_mm(dc - 1)
            s2_mm(NCH - 1)
            b.wait("dve", "pe", vp)
            b.pre("dve").inc("dve", nc.vector.tensor_scalar_mul(sc_r, zx[:, :], 1.0 / D))
            b.pre("dve").inc("dve", nc.vector.tensor_mul(rs_r, m_r, m_r))
            vd = b.pre("dve").inc("dve", nc.vector.tensor_sub(sc_r, sc_r, rs_r))
            b.wait("act", "dve", vd)
            b.pre("act").inc("act", nc.scalar.activation(sc_r, sc_r, AF.Ln, bias=consts["eps"][0:1, 0:1]))
            va = b.pre("act").inc("act", nc.scalar.activation(rs_r, sc_r, AF.Exp, scale=-0.5))
            b.wait("dve", "act", va)
            vd = b.pre("dve").inc("dve", nc.vector.tensor_mul(bm_r, m_r, rs_r))
        with ExitStack() as pst:
            nbc = min(ncols, 1024)
            pa = b.ps("ln_pa", [128, nbc], F32, pst)
            pb = b.ps("ln_pb", [128, nbc], F32, pst)
            b.wait("pe", "act", va)
            b.wait("pe", "dve", vd)
            for c0 in range(0, ncols, nbc):
                if c0 > 0:
                    b.wait("pe", "act", va)
                for g in range(0, nbc, 512):
                    sl = slice(c0 + g, c0 + g + min(512, nbc - g))
                    psl = slice(g, g + min(512, nbc - g))
                    b.inc("pe", nc.tensor.matmul(pa[:, psl], onesf[:, :],
                                                 rs_r[:, sl], start=True, stop=True))
                    vp = b.inc("pe", nc.tensor.matmul(pb[:, psl], onesf[:, :],
                                                      bm_r[:, sl], start=True, stop=True))
                b.wait("act", "pe", vp)
                b.pre("act").inc("act", nc.scalar.copy(
                    ab[:, c0:c0 + nbc], pa[:, :]))
                va = b.pre("act").inc("act", nc.scalar.copy(
                    ab[:, ncols + c0:ncols + c0 + nbc], pb[:, :]))
        scr.close()
        scr = ExitStack()
        t1 = b.sb("ln_t1", [128, ncols], F32, scr)
        A = ab[:, 0:ncols]
        B = ab[:, ncols:2 * ncols]
        b.wait("dve", "act", va)
        if par_split is None:
            for dc in range(NCH):
                xsl = xbf[:, dc * ncols:(dc + 1) * ncols]
                b.pre("dve").inc("dve", nc.vector.tensor_mul(t1[:, :], xsl, A))
                vd = b.pre("dve").inc("dve", nc.vector.tensor_sub(
                    xn[:, dc * ncols:(dc + 1) * ncols], t1[:, :], B))
        else:
            xnp, chlists = par_split
            for par in range(PAR):
                for slot, dc in enumerate(chlists[par]):
                    xsl = xbf[:, dc * ncols + par:(dc + 1) * ncols:PAR]
                    b.pre("dve").inc("dve", nc.vector.tensor_mul(
                        t1[:, 0:ncols // PAR], xsl, A[:, par::PAR]))
                    vd = b.pre("dve").inc("dve", nc.vector.tensor_sub(
                        xnp[par][:, slot * (ncols // PAR):(slot + 1) * (ncols // PAR)],
                        t1[:, 0:ncols // PAR], B[:, par::PAR]))
        scr.close()
    return vd


def emit_w2_prep(b, w2t, w2tmp, w2dd, consts):
    nc = b.nc
    with ExitStack() as st:
        blk = [b.sb(f"wp_blk{i}", [128, D], BF16, st) for i in range(3)]
        ev = [b.sb(f"wp_ev{i}", [128, D], BF16, st) for i in range(2)]
        pp = [b.ps(f"wp_p{i}", [128, 1024], F32, st) for i in range(2)]
        vcopy = {}
        vstore = {}
        vmm = {}

        def p1_load(j1):
            b.dma("gp", f"wpg{j1 % 3}", blk[j1 % 3][:, :],
                  w2t[j1 * 128:(j1 + 1) * 128, :])

        for j in range(3):
            p1_load(j)
        for j1 in range(64):
            k = j1 % 3
            e = j1 % 2
            b.dwait("pe", f"wpg{k}")
            for half in range(2):
                b.wait("pe", "act", vcopy.get((j1 - 1, half), 0))
                for g in range(2):
                    o = half * 1024 + g * 512
                    vp = b.inc("pe", nc.tensor.matmul(
                        pp[half][:, g * 512:(g + 1) * 512], consts["h128"][:, :],
                        blk[k][:, o:o + 512], start=True, stop=True))
                vmm[j1] = vp
                b.wait("act", "pe", vp)
                if half == 0 and j1 >= 2:
                    b.dwait("act", f"wpo{e}")
                vcopy[(j1, half)] = b.pre("act").inc("act", nc.scalar.copy(
                    ev[e][:, half * 1024:(half + 1) * 1024], pp[half][:, :]))
            b.wait("sy", "act", vcopy[(j1, 1)])
            b.dma("sy", f"wpo{e}", w2tmp[j1 * 128:(j1 + 1) * 128, :], ev[e][:, :])
            if j1 + 3 < 64:
                b.wait("gp", "pe", vmm[j1])
                p1_load(j1 + 3)
        b.dwait("sy", "wpo0")
        b.dwait("sy", "wpo1")
        b.wait("gp", "pe")
        tmp_r = w2tmp.rearrange("(j1 k2) d -> k2 j1 d", k2=128)
        dd_r = w2dd.rearrange("(k1 k2) d -> k2 k1 d", k2=128)
        b.dwait("gp", "wpo0")
        b.dwait("gp", "wpo1")

        def p2_load(i):
            b.dma("gp", f"wps{i % 3}", blk[i % 3][:, :], tmp_r[2 * i:2 * i + 2])

        for i in range(3):
            p2_load(i)
        vcopy = {}
        vmm = {}
        for i in range(64):
            k = i % 3
            e = i % 2
            b.dwait("pe", f"wps{k}")
            for half in range(2):
                b.wait("pe", "act", vcopy.get((i - 1, half), 0))
                for g in range(2):
                    o = half * 1024 + g * 512
                    vp = b.inc("pe", nc.tensor.matmul(
                        pp[half][:, g * 512:(g + 1) * 512], consts["h64blk"][:, :],
                        blk[k][:, o:o + 512], start=True, stop=True))
                vmm[i] = vp
                b.wait("act", "pe", vp)
                if half == 0 and i >= 2:
                    b.dwait("act", f"wpd{e}")
                vcopy[(i, half)] = b.pre("act").inc("act", nc.scalar.activation(
                    ev[e][:, half * 1024:(half + 1) * 1024], pp[half][:, :],
                    AF.Copy, scale=FWHT_SCALE))
            b.wait("sy", "act", vcopy[(i, 1)])
            b.dma("sy", f"wpd{e}", dd_r[2 * i:2 * i + 2], ev[e][:, :])
            if i + 3 < 64:
                b.wait("gp", "pe", vmm[i])
                p2_load(i + 3)
        b.dwait("sy", "wpd0")
        b.dwait("sy", "wpd1")


def _load_band(b, w1t, w3t, d1, d3, band):
    sl = slice(band * 256, (band + 1) * 256)
    ds = f"fab{band % 2}"
    b.dma("gp", ds, d1[:, :].rearrange("dp (dc h) -> dp dc h", dc=NCH),
          w1t[:, sl].rearrange("(dc dp) h -> dp dc h", dp=128))
    return b.dma("gp", ds, d3[:, :].rearrange("dp (dc h) -> dp dc h", dc=NCH),
                 w3t[:, sl].rearrange("(dc dp) h -> dp dc h", dp=128))


def emit_ffn(b, xres_src, dst_hbm, w1t, w3t, w2dd, hid, consts, prep_args=None):
    nc = b.nc
    with ExitStack() as st:
        with ExitStack() as ast:
            xn = b.sb("fa_xn", [128, NCH * S], BF16, ast)
            prep_gen = None
            if prep_args is not None:
                prep_gen = w2_prep_steps(b, prep_args[0], prep_args[1],
                                         w2dd, consts, ast)
            w1b = [b.sb(f"fa_w1{i}", [128, NCH * 256], BF16, ast) for i in range(2)]
            w3b = [b.sb(f"fa_w3{i}", [128, NCH * 256], BF16, ast) for i in range(2)]
            sg = [b.sb(f"fa_sg{i}", [128, 1024], F32, ast) for i in range(2)]
            hbuf = [b.sb(f"fa_hb{i}", [128, 1024], BF16, ast) for i in range(2)]
            wload = [_load_band(b, w1t, w3t, w1b[0], w3b[0], 0),
                     _load_band(b, w1t, w3t, w1b[1], w3b[1], 1)]
            vxn = emit_ln(b, xres_src, xn, 0, S, consts)
            b.wait("pe", "dve", vxn)
            if prep_gen is not None:
                pg = [b.ps(f"fa_pg{i}", [128, 1024], F32, ast) for i in range(2)]
                pl = [b.ps("fa_pl", [128, 1024], F32, ast)]
            hseq = []
            if prep_gen is None:
                with ExitStack() as wps:
                    pgw = b.ps("fa_pgw", [128, 2048], F32, wps)
                    plw = b.ps("fa_plw", [128, 2048], F32, wps)
                    sgw = [b.sb(f"fa_sgw{i}", [128, 2048], F32, wps)
                           for i in range(2)]
                    hbw = [b.sb(f"fa_hbw{i}", [128, 2048], BF16, wps)
                           for i in range(2)]
                    vsil = {}
                    vmul = {}
                    for hc in range(NHID):
                        band, hcl = hc // 2, hc % 2
                        k = band % 2
                        b.dwait("pe", f"fab{k}")
                        q = hc % 2
                        if hc >= 1:
                            b.wait("pe", "act", vsil[hc - 1])
                        for dc in range(NCH):
                            l1 = w1b[k][:, dc * 256 + hcl * 128:
                                        dc * 256 + hcl * 128 + 128]
                            for g in range(4):
                                vpg = b.inc("pe", nc.tensor.matmul(
                                    pgw[:, g * 512:(g + 1) * 512], l1,
                                    xn[:, dc * S + g * 512:dc * S + g * 512 + 512],
                                    start=(dc == 0), stop=(dc == NCH - 1)))
                        if hc >= 2:
                            b.wait("act", "dve", vmul[hc - 2])
                        b.wait("act", "pe", vpg)
                        vsil[hc] = b.pre("act").inc("act", nc.scalar.activation(
                            sgw[q][:, :], pgw[:, :], AF.Silu))
                        if hc >= 1:
                            b.wait("pe", "dve", vmul[hc - 1])
                        for dc in range(NCH):
                            l3 = w3b[k][:, dc * 256 + hcl * 128:
                                        dc * 256 + hcl * 128 + 128]
                            for g in range(4):
                                vp = b.inc("pe", nc.tensor.matmul(
                                    plw[:, g * 512:(g + 1) * 512], l3,
                                    xn[:, dc * S + g * 512:dc * S + g * 512 + 512],
                                    start=(dc == 0), stop=(dc == NCH - 1)))
                        b.dwait("dve", f"fah{q}")
                        b.wait("dve", "act", vsil[hc])
                        b.wait("dve", "pe", vp)
                        vmul[hc] = b.pre("dve").inc("dve", nc.vector.tensor_mul(
                            hbw[q][:, :], sgw[q][:, :], plw[:, :]))
                        b.wait("sy", "dve", vmul[hc])
                        b.dma("sy", f"fah{q}", hid.ap()[0, hc], hbw[q][:, 0:1024])
                        b.dma("sy", f"fah{q}", hid.ap()[1, hc], hbw[q][:, 1024:2048])
                        if hc % 2 == 1 and band + 2 < 32:
                            b.wait("gp", "pe")
                            wload.append(_load_band(b, w1t, w3t, w1b[k], w3b[k],
                                                    band + 2))
                b.dwait("sy", "fah0")
                b.dwait("sy", "fah1")
            else:
              for band in range(32):
                k = band % 2
                b.dwait("pe", f"fab{band % 2}")
                for hcl in range(2):
                    hc = band * 2 + hcl
                    for th in range(2):
                        idx = hc * 2 + th
                        q = idx % 2
                        if len(hseq) >= 2:
                            b.wait("pe", "act", hseq[-2][0])
                        xo0 = th * 1024
                        for dc in range(NCH):
                            l1 = w1b[k][:, dc * 256 + hcl * 128:
                                        dc * 256 + hcl * 128 + 128]
                            xb = dc * S + xo0
                            b.inc("pe", nc.tensor.matmul(
                                pg[q][:, 0:512], l1, xn[:, xb:xb + 512],
                                start=(dc == 0), stop=(dc == NCH - 1)))
                            vpg = b.inc("pe", nc.tensor.matmul(
                                pg[q][:, 512:1024], l1, xn[:, xb + 512:xb + 1024],
                                start=(dc == 0), stop=(dc == NCH - 1)))
                        if len(hseq) >= 2:
                            b.wait("act", "dve", hseq[-2][1])
                        b.wait("act", "pe", vpg)
                        va = b.pre("act").inc("act", nc.scalar.activation(
                            sg[q][:, :], pg[q][:, :], AF.Silu))
                        if hseq:
                            b.wait("pe", "dve", hseq[-1][1])
                        for dc in range(NCH):
                            l3 = w3b[k][:, dc * 256 + hcl * 128:
                                        dc * 256 + hcl * 128 + 128]
                            xb = dc * S + xo0
                            b.inc("pe", nc.tensor.matmul(
                                pl[0][:, 0:512], l3, xn[:, xb:xb + 512],
                                start=(dc == 0), stop=(dc == NCH - 1)))
                            vp = b.inc("pe", nc.tensor.matmul(
                                pl[0][:, 512:1024], l3, xn[:, xb + 512:xb + 1024],
                                start=(dc == 0), stop=(dc == NCH - 1)))
                        b.dwait("dve", f"fah{q}")
                        b.wait("dve", "act", va)
                        b.wait("dve", "pe", vp)
                        vd = b.pre("dve").inc("dve", nc.vector.tensor_mul(
                            hbuf[q][:, :], sg[q][:, :], pl[0][:, :]))
                        b.wait("sy", "dve", vd)
                        vo = b.dma("sy", f"fah{q}", hid.ap()[th, hc], hbuf[q][:, :])
                        if prep_gen is not None:
                            next(prep_gen, None)
                        hseq.append((va, vd, vo))
                if band < 30:
                    b.wait("gp", "pe")
                    wload.append(_load_band(b, w1t, w3t, w1b[k], w3b[k], band + 2))
            b.dwait("sy", "fah0")
            b.dwait("sy", "fah1")
            if prep_gen is not None:
                for _ in prep_gen:
                    pass
                b.dwait("sy", "wpd0")
                b.dwait("sy", "wpd1")
        b.bar()
        with ExitStack() as bst:
            hstr = b.sb("fb_h", [128, NHID * 1024], BF16, bst)
            w2s = [b.sb(f"fb_w2{i}", [128, 256], BF16, bst) for i in range(16)]
            xr = [b.sb(f"fb_xr{i}", [128, 1024], F32, bst) for i in range(2)]
            ox = [b.sb(f"fb_ox{i}", [128, 1024], F32, bst) for i in range(2)]
            pb = [[b.ps(f"fb_p{i}{j}", [128, 1024], F32, bst) for j in range(2)]
                  for i in range(2)]
            hs_r = hstr[:, :].rearrange("dp (hc t) -> dp hc t", hc=NHID)
            for ts in range(2):
                if ts > 0:
                    b.wait("sy", "pe")
                    b.wait("ac", "pe")
                for part in range(8):
                    qn = "sy" if part % 2 == 0 else "ac"
                    b.dma(qn, f"fbh{part}", hs_r[:, part * 8:(part + 1) * 8],
                          hid.ap()[ts, part * 8:(part + 1) * 8].rearrange(
                              "hc dp t -> dp hc t"))
                pv = {}
                if ts > 0:
                    b.wait("ac", "pe")
                for pre in range(14):
                    pdb, ph = divmod(pre, NHID)
                    b.dma("ac", f"fbw{pre}", w2s[pre][:, :],
                          w2dd[ph * 128:(ph + 1) * 128, pdb * 256:(pdb + 1) * 256])
                ev = []
                evx = []
                for db in range(8):
                    q = db % 2
                    if db >= 2:
                        b.wait("pe", "dve", ev[db - 2][0])
                    for h in range(NHID):
                        idx = db * NHID + h
                        if db == 0:
                            b.dwait("pe", f"fbh{h // 8}")
                        b.dwait("pe", f"fbw{idx % 16}")
                        ws = w2s[idx % 16]
                        for ds in range(2):
                            lh = ws[:, ds * 128:(ds + 1) * 128]
                            for th in range(2):
                                vp = b.inc("pe", nc.tensor.matmul(
                                    pb[q][ds][:, th * 512:(th + 1) * 512], lh,
                                    hstr[:, h * 1024 + th * 512:h * 1024 + th * 512 + 512],
                                    start=(h == 0), stop=(h == NHID - 1)))
                        pv[idx] = vp
                        nidx = idx + 14
                        if nidx < 8 * NHID:
                            ndb, nh = divmod(nidx, NHID)
                            if nidx >= 16:
                                b.wait("ac", "pe", pv[nidx - 16])
                            b.dma("ac", f"fbw{nidx % 16}", w2s[nidx % 16][:, :],
                                  w2dd[nh * 128:(nh + 1) * 128,
                                       ndb * 256:(ndb + 1) * 256])
                    for ds in range(2):
                        r0 = db * 256 + ds * 128
                        kx = (db * 2 + ds) % 2
                        if len(evx) >= 2:
                            b.wait("sy", "dve", evx[-2][0])
                        b.dma("sy", f"fbx{kx}", xr[kx][:, :],
                              xres_src[r0:r0 + 128, ts * 1024:(ts + 1) * 1024])
                        if len(evx) >= 2:
                            b.dwait("dve", f"fbo{kx}")
                        b.dwait("dve", f"fbx{kx}")
                        b.wait("dve", "pe", vp)
                        vd = b.pre("dve").inc("dve", nc.vector.tensor_add(
                            ox[kx][:, :], pb[q][ds][:, :], xr[kx][:, :]))
                        b.wait("sy", "dve", vd)
                        vo = b.dma("sy", f"fbo{kx}", dst_hbm[r0:r0 + 128,
                                   ts * 1024:(ts + 1) * 1024], ox[kx][:, :])
                        evx.append((vd, vo))
                    ev.append((vd, vo))
                b.dwait("sy", "fbo0")
                b.dwait("sy", "fbo1")
        b.bar()


def w2_prep_steps(b, w2t, w2tmp, w2dd, consts, st):
    nc = b.nc
    blk = [b.sb(f"wp_blk{i}", [128, D], BF16, st) for i in range(3)]
    ev = [b.sb(f"wp_ev{i}", [128, D], BF16, st) for i in range(2)]
    pp = [b.ps(f"wp_p{i}", [128, 512], F32, st) for i in range(2)]

    def p1_load(j1):
        b.dma("gp", f"wpg{j1 % 3}", blk[j1 % 3][:, :],
              w2t[j1 * 128:(j1 + 1) * 128, :])

    for j in range(3):
        p1_load(j)
    return _w2_gen(b, w2t, w2tmp, w2dd, consts, blk, ev, pp, p1_load)


def _w2_gen(b, w2t, w2tmp, w2dd, consts, blk, ev, pp, p1_load):
    nc = b.nc

    def one_pass(load_stream, out_stream, hmat, scale, dst_r, loader):
        vq = {}
        vmm = {}

        def quarter(it, qr):
            e = it % 2
            p = qr % 2
            eng = "act" if p == 0 else "dve"
            b.dwait("pe", f"{load_stream}{it % 3}")
            b.wait("pe", eng, vq.get((it, qr - 2), vq.get((it - 1, qr + 2), 0)))
            vp = b.inc("pe", nc.tensor.matmul(
                pp[p][:, :], hmat[:, :],
                blk[it % 3][:, qr * 512:(qr + 1) * 512], start=True, stop=True))
            if qr == 3:
                vmm[it] = vp
            b.wait(eng, "pe", vp)
            if qr < 2 and it >= 2:
                b.dwait(eng, f"{out_stream}{e}")
            dsl = ev[e][:, qr * 512:(qr + 1) * 512]
            if eng == "act":
                vq[(it, qr)] = b.pre("act").inc("act", nc.scalar.activation(
                    dsl, pp[p][:, :], AF.Copy, scale=scale))
            elif scale != 1.0:
                vq[(it, qr)] = b.pre("dve").inc(
                    "dve", nc.vector.tensor_scalar_mul(dsl, pp[p][:, :], scale))
            else:
                vq[(it, qr)] = b.pre("dve").inc(
                    "dve", nc.vector.tensor_copy(dsl, pp[p][:, :]))

        quarter(0, 0)
        quarter(0, 1)
        yield
        for it in range(64):
            quarter(it, 2)
            quarter(it, 3)
            e = it % 2
            b.wait("sy", "act", vq[(it, 2)])
            b.wait("sy", "dve", vq[(it, 3)])
            b.dma("sy", f"{out_stream}{e}", dst_r(it), ev[e][:, :])
            if it + 3 < 64:
                b.wait("gp", "pe", vmm[it])
                loader(it + 3)
            if it + 1 < 64:
                quarter(it + 1, 0)
                quarter(it + 1, 1)
            yield

    yield from one_pass("wpg", "wpo", consts["h128"], 1.0,
                        lambda it: w2tmp[it * 128:(it + 1) * 128, :], p1_load)
    tmp_r = w2tmp.rearrange("(j1 k2) d -> k2 j1 d", k2=128)
    dd_r = w2dd.rearrange("(k1 k2) d -> k2 k1 d", k2=128)
    b.dwait("gp", "wpo0")
    b.dwait("gp", "wpo1")
    b.wait("gp", "pe")

    def p2_load(i):
        b.dma("gp", f"wps{i % 3}", blk[i % 3][:, :], tmp_r[2 * i:2 * i + 2])

    for i in range(3):
        p2_load(i)
    yield from one_pass("wps", "wpd", consts["h64blk"], FWHT_SCALE,
                        lambda it: dd_r[2 * it:2 * it + 2], p2_load)


def emit_attn_a(b, x_in, xres, consts, prep_gen=None):
    nc = b.nc
    chl = [[h for h in range(NHEADS) if h % 2 == p] for p in range(PAR)]
    slots = [(par, sl) for par in range(PAR) for sl in range(8)]
    for seg in range(NSEGS):
        c0 = seg * SEG
        with ExitStack() as st:
            xf = b.sb("aa_xf", [128, NCH * SEG], F32, st)
            xnp = [b.sb(f"aa_xnp{p}", [128, 8 * SEGH], BF16, st) for p in range(PAR)]
            gt = [b.sb(f"aa_gt{i}", [128, 2 * SEGH], BF16, st) for i in range(2)]
            et = [[b.sb(f"aa_et{i}{jb}", [128, SEGH], BF16, st) for jb in range(2)]
                  for i in range(2)]
            avs = [b.sb(f"aa_avs{i}", [128, SEGH], F32, st) for i in range(2)]
            rzr = [b.sb(f"aa_rz{i}", [1, SEGH], F32, st) for i in range(2)]
            an = b.sb("aa_an", [128, SEGH], F32, st)
            for dc in range(NCH):
                b.dma("sy", "aax", xf[:, dc * SEG:(dc + 1) * SEG],
                      x_in[dc * 128:(dc + 1) * 128, c0:c0 + SEG])
            vxn = emit_ln(b, x_in, None, c0, SEG, consts,
                          par_split=(xnp, chl), src_sbuf=(xf, "aax"))
            b.wait("pe", "dve", vxn)
            b.wait("sy", "dve", vxn)
            with ExitStack() as pst:
                psc = [b.ps(f"aa_psc{i}", [128, SEGH], F32, pst) for i in range(2)]
                pz = [b.ps(f"aa_pz{i}", [1, SEGH], F32, pst) for i in range(2)]
                pav = [b.ps(f"aa_pav{i}", [128, SEGH], F32, pst) for i in range(2)]
                prz = [b.ps(f"aa_prz{i}", [128, SEGH], F32, pst) for i in range(2)]
                stt = {"vas": {}, "vzav": {}, "vrz": [0, 0], "vavs": [0, 0],
                       "vpb": [0, 0], "vtail": [0, 0], "tail": 0}

                def stage1(s):
                    par, sl = slots[s]
                    g = xnp[par][:, sl * SEGH:(sl + 1) * SEGH]
                    sb = s % 2
                    if s >= 2:
                        b.wait("sy", "pe", stt["vzav"].get(s - 2, 0))
                    for jb in range(2):
                        b.dma("sy", f"agt{sb}", gt[sb][:, jb * 128:(jb + 1) * 128],
                              g[:, jb * 128:(jb + 1) * 128], transpose=True)
                    for jb in range(2):
                        b.wait("pe", "act", stt["vas"].get((s - 1, jb), 0))
                        vp = b.inc("pe", nc.tensor.matmul(
                            psc[jb][:, :], g[:, jb * 128:(jb + 1) * 128], g,
                            start=True, stop=True))
                        b.wait("act", "pe", vp)
                        if s >= 2:
                            b.wait("act", "pe", stt["vzav"].get(s - 2, 0))
                        stt["vas"][(s, jb)] = b.pre("act").inc(
                            "act", nc.scalar.activation(
                                et[sb][jb][:, :], psc[jb][:, :], AF.Exp,
                                bias=consts["shift"][:, :], scale=INV_SQRT_HD))

                def zav_recip(s):
                    par, sl = slots[s]
                    sb = s % 2
                    e = s % 2
                    b.wait("pe", "act", stt["vrz"][e])
                    b.wait("pe", "act", stt["vavs"][e])
                    b.dwait("pe", f"agt{sb}")
                    for jb in range(2):
                        b.wait("pe", "act", stt["vas"][(s, jb)])
                        b.inc("pe", nc.tensor.matmul(
                            pz[e][:, :], consts["ones_bf"][:, :], et[sb][jb][:, :],
                            start=(jb == 0), stop=(jb == 1)))
                        vp = b.inc("pe", nc.tensor.matmul(
                            pav[e][:, :], gt[sb][:, jb * 128:(jb + 1) * 128],
                            et[sb][jb][:, :], start=(jb == 0), stop=(jb == 1)))
                    stt["vzav"][s] = vp
                    b.wait("act", "pe", vp)
                    if stt["vpb"][e]:
                        b.wait("act", "pe", stt["vpb"][e])
                    b.pre("act").inc("act", nc.scalar.activation(
                        rzr[e][:, :], pz[e][:, :], AF.Ln))
                    stt["vrz"][e] = b.pre("act").inc("act", nc.scalar.activation(
                        rzr[e][:, :], rzr[e][:, :], AF.Exp, scale=-1.0))

                def tail(s):
                    par, sl = slots[s]
                    h = chl[par][sl]
                    e = s % 2
                    b.wait("pe", "act", stt["vrz"][e])
                    b.wait("pe", "dve", stt["vtail"][e])
                    stt["vpb"][e] = b.inc("pe", nc.tensor.matmul(
                        prz[e][:, :], consts["ones_f32"][:, :], rzr[e][:, :],
                        start=True, stop=True))
                    b.wait("act", "dve", stt["vtail"][e])
                    b.wait("act", "pe", stt["vzav"][s])
                    va = stt["vavs"][e] = b.pre("act").inc(
                        "act", nc.scalar.copy(avs[e][:, :], pav[e][:, :]))
                    b.wait("dve", "act", va)
                    b.wait("dve", "pe", stt["vpb"][e])
                    b.dwait("dve", "aax")
                    b.pre("dve").inc("dve", nc.vector.tensor_mul(
                        an[:, :], avs[e][:, :], prz[e][:, :]))
                    xsl = xf[:, h * SEG + par:(h + 1) * SEG:PAR]
                    stt["vtail"][e] = stt["tail"] = b.pre("dve").inc(
                        "dve", nc.vector.tensor_add(xsl, xsl, an[:, :]))

                stage1(0)
                stage1(1)
                for s in range(16):
                    zav_recip(s)
                    if s + 2 < 16:
                        stage1(s + 2)
                    if s >= 1:
                        tail(s - 1)
                tail(15)
            b.wait("sy", "dve")
            for dc in range(NCH):
                b.dma("sy", "aar", xres[dc * 128:(dc + 1) * 128, c0:c0 + SEG],
                      xf[:, dc * SEG:(dc + 1) * SEG])
            b.dwait("sy", "aar")
        b.bar()


def emit_attn_b(b, xres, wqt, wkt, wvt, wot, consts):
    nc = b.nc
    with ExitStack() as st:
        ao = b.sb("ab_ao", [128, NCH * S], BF16, st)
        with ExitStack() as hst:
            xn = b.sb("ab_xn", [128, NCH * S], BF16, hst)
            vxn = emit_ln(b, xres, xn, 0, S, consts)
            b.wait("pe", "dve", vxn)
            b.wait("gp", "dve", vxn)
            b.wait("gp", "act")
            qw = [b.sb(f"ab_qw{i}", [128, NCH * 128], BF16, hst) for i in range(2)]
            kw = [b.sb(f"ab_kw{i}", [128, NCH * 128], BF16, hst) for i in range(2)]
            vw = [b.sb(f"ab_vw{i}", [128, NCH * 128], BF16, hst) for i in range(2)]
            qh = [b.sb(f"ab_qh{i}", [128, S], BF16, hst) for i in range(2)]
            kh = [b.sb(f"ab_kh{i}", [128, S], BF16, hst) for i in range(2)]
            vh = [b.sb(f"ab_vh{i}", [128, S], BF16, hst) for i in range(2)]
            vt = [b.sb(f"ab_vt{i}", [128, S], BF16, hst) for i in range(2)]
            et = [b.sb(f"ab_et{i}", [128, 512], BF16, hst) for i in range(2)]
            avs = [b.sb(f"ab_avs{i}", [128, 512], F32, hst) for i in range(2)]
            rzr = [b.sb(f"ab_rz{i}", [1, 512], F32, hst) for i in range(2)]

            def load_head_w(h):
                for wt, dst in ((wqt, qw), (wkt, kw), (wvt, vw)):
                    b.dma("gp", f"abw{h % 2}", dst[h % 2][:, :].rearrange(
                        "dp (dc o) -> dp dc o", dc=NCH),
                        wt[:, h * 128:(h + 1) * 128].rearrange(
                            "(dc dp) o -> dp dc o", dp=128))

            load_head_w(0)
            load_head_w(1)
            with ExitStack() as pst:
                pq = [b.ps("ab_pq", [128, 512], F32, pst)]
                psc = [b.ps(f"ab_psc{i}", [128, 512], F32, pst) for i in range(2)]
                pz = [b.ps(f"ab_pz{i}", [1, 512], F32, pst) for i in range(2)]
                pav = [b.ps(f"ab_pav{i}", [128, 512], F32, pst) for i in range(2)]
                prz = b.ps("ab_prz", [128, 512], F32, pst)
                stt = {"pq_prev": [], "vtr": 0, "qv": 0, "pe_proj": {},
                       "ready": {}, "vhva": {}}

                def qkv_units(h):
                    k = h % 2

                    def mk_proj(di, tg):
                        def emit():
                            dst = (qh[k], kh[k], vh[k])[di]
                            wsb = (qw[k], kw[k], vw[k])[di]
                            q = 0
                            b.dwait("pe", f"abw{k}")
                            if stt["pq_prev"]:
                                b.wait("pe", "dve", stt["pq_prev"][-1])
                            for dc in range(NCH):
                                vp = b.inc("pe", nc.tensor.matmul(
                                    pq[q][:, :], wsb[:, dc * 128:(dc + 1) * 128],
                                    xn[:, dc * S + tg * 512:dc * S + tg * 512 + 512],
                                    start=(dc == 0), stop=(dc == NCH - 1)))
                            if di == 2 and tg == 3:
                                stt["pe_proj"][h] = vp
                            b.wait("dve", "pe", vp)
                            va = b.pre("dve").inc("dve", nc.vector.tensor_copy(
                                dst[:, tg * 512:(tg + 1) * 512], pq[q][:, :]))
                            stt["pq_prev"].append(va)
                            if di == 2:
                                stt["vhva"][(h, tg)] = va
                        return emit

                    def mk_vt(j):
                        def emit():
                            b.wait("sy", "dve", stt["vhva"][(h, j // 4)])
                            if j == 0:
                                b.wait("sy", "pe")
                            b.dma("sy", f"vtr{k}", vt[k][:, j * 128:(j + 1) * 128],
                                  vh[k][:, j * 128:(j + 1) * 128], transpose=True)
                            if j == 15:
                                stt["ready"][h] = stt["pq_prev"][-1]
                        return emit

                    return ([mk_proj(di, tg) for di in range(3) for tg in range(4)]
                            + [mk_vt(j) for j in range(16)])

                for u in qkv_units(0):
                    u()
                vd_tails = [0, 0]
                vpb_prev = [0, 0]
                vrz_prev = [0, 0]
                vavs_prev = [0, 0]
                vzav_e = [0, 0]
                pend = [None]

                def emit_tail():
                    if pend[0] is None:
                        return
                    te, ao_sl = pend[0]
                    pend[0] = None
                    b.wait("pe", "act", vrz_prev[te])
                    b.wait("pe", "dve", vd_tails[1 - te])
                    b.wait("pe", "dve", vd_tails[te])
                    vpb_prev[te] = b.inc("pe", nc.tensor.matmul(
                        prz[:, :], consts["ones_f32"][:, :], rzr[te][:, :],
                        start=True, stop=True))
                    b.wait("act", "dve", vd_tails[te])
                    b.wait("act", "pe", vzav_e[te])
                    vavs_prev[te] = b.pre("act").inc("act", nc.scalar.copy(
                        avs[te][:, :], pav[te][:, :]))
                    b.wait("dve", "act", vavs_prev[te])
                    b.wait("dve", "pe", vpb_prev[te])
                    vd_tails[te] = b.pre("dve").inc("dve", nc.vector.tensor_mul(
                        ao_sl, avs[te][:, :], prz[:, :]))

                for h in range(NHEADS):
                    k = h % 2
                    units = qkv_units(h + 1) if h + 1 < NHEADS else []
                    nslots = 44
                    slot = [0, 0]

                    def tick():
                        slot[0] += 1
                        tgt = (slot[0] * len(units)) // nslots
                        while slot[1] < min(tgt, len(units)):
                            units[slot[1]]()
                            slot[1] += 1

                    if h + 2 < NHEADS:
                        b.wait("gp", "pe", stt["pe_proj"][h])
                        load_head_w(h + 2)
                    b.dwait("pe", f"vtr{k}")
                    b.wait("pe", "dve", stt["ready"][h])
                    for ig in range(4):
                        jmax = 4 * ig + 3
                        vas = {}
                        vzav = {}

                        e = (h * 4 + ig) % 2

                        def emit_zav(j, start, stop):
                            q = j % 2
                            osj = max(0, j * 128 - ig * 512)
                            b.wait("pe", "act", vas[j])
                            b.inc("pe", nc.tensor.matmul(
                                pz[e][:, osj:512], consts["ones_bf"][:, :],
                                et[q][:, osj:512], start=start, stop=stop))
                            vzav[j] = b.inc("pe", nc.tensor.matmul(
                                pav[e][:, osj:512], vt[k][:, j * 128:(j + 1) * 128],
                                et[q][:, osj:512], start=start, stop=stop))

                        b.wait("pe", "act", vrz_prev[e])
                        b.wait("pe", "act", vavs_prev[e])
                        for j in range(jmax + 1):
                            osj = max(0, j * 128 - ig * 512)
                            q = j % 2
                            if j >= 2:
                                b.wait("pe", "act", vas[j - 2])
                            vp = b.inc("pe", nc.tensor.matmul(
                                psc[q][:, osj:512], kh[k][:, j * 128:(j + 1) * 128],
                                qh[k][:, ig * 512 + osj:(ig + 1) * 512],
                                start=True, stop=True))
                            vdm = 0
                            if j * 128 >= ig * 512:
                                b.wait("dve", "pe", vp)
                                vdm = b.pre("dve").inc("dve", nc.vector.tensor_add(
                                    psc[q][:, osj:osj + 128],
                                    psc[q][:, osj:osj + 128], consts["dmask"][:, :]))
                            if j >= 1:
                                emit_zav(j - 1, j - 1 == 0, False)
                            tick()
                            b.wait("act", "pe", vp)
                            if vdm:
                                b.wait("act", "dve", vdm)
                            if j >= 2:
                                b.wait("act", "pe", vzav[j - 2])
                            vas[j] = b.pre("act").inc("act", nc.scalar.activation(
                                et[q][:, osj:512], psc[q][:, osj:512], AF.Exp,
                                bias=consts["shift"][:, :], scale=INV_SQRT_HD))
                        emit_zav(jmax, jmax == 0, True)
                        vp = vzav[jmax]
                        vzav_e[e] = vp
                        b.wait("act", "pe", vp)
                        if vpb_prev[e]:
                            b.wait("act", "pe", vpb_prev[e])
                        b.pre("act").inc("act", nc.scalar.activation(
                            rzr[e][:, :], pz[e][:, :], AF.Ln))
                        vrz_prev[e] = b.pre("act").inc(
                            "act", nc.scalar.activation(
                                rzr[e][:, :], rzr[e][:, :], AF.Exp, scale=-1.0))
                        pend_new = (e, ao[:, h * S + ig * 512:h * S + (ig + 1) * 512])
                        emit_tail()
                        pend[0] = pend_new
                        tick()
                    for u in units[slot[1]:]:
                        u()
                    slot[1] = len(units)
                emit_tail()
                b.wait("pe", "dve", vd_tails[0])
                b.wait("pe", "dve", vd_tails[1])
        b.bar()
        with ExitStack() as wst:
            wob = [b.sb(f"ab_wo{i}", [128, NCH * 128], BF16, wst) for i in range(3)]
            xr = [b.sb(f"ab_xr{i}", [128, 512], F32, wst) for i in range(3)]
            oxt = [b.sb(f"ab_ox{i}", [128, 512], F32, wst) for i in range(3)]
            pwo = [b.ps(f"ab_pw{i}", [128, 512], F32, wst) for i in range(3)]

            def load_wo(do):
                return b.dma(
                    "gp", f"awo{do % 3}",
                    wob[do % 3][:, :].rearrange("dp (dc o) -> dp dc o", dc=NCH),
                    wot[:, do * 128:(do + 1) * 128].rearrange(
                        "(dc dp) o -> dp dc o", dp=128))

            load_wo(0)
            load_wo(1)
            pv_wo = {}
            seq = []
            for do in range(NCH):
                k = do % 3
                b.dwait("pe", f"awo{do % 3}")
                if do + 2 < NCH:
                    if do >= 1:
                        b.wait("gp", "pe", pv_wo[do - 1])
                    load_wo(do + 2)
                for tg in range(4):
                    idx = do * 4 + tg
                    q = idx % 3
                    if idx >= 3:
                        b.wait("pe", "dve", seq[idx - 3][0])
                    for dc in range(NCH):
                        vp = b.inc("pe", nc.tensor.matmul(
                            pwo[q][:, :], wob[k][:, dc * 128:(dc + 1) * 128],
                            ao[:, dc * S + tg * 512:dc * S + tg * 512 + 512],
                            start=(dc == 0), stop=(dc == NCH - 1)))
                    if idx >= 3:
                        b.wait("ac", "dve", seq[idx - 3][0])
                    b.dma("ac", f"abx{q}", xr[q][:, :],
                          xres[do * 128:(do + 1) * 128, tg * 512:(tg + 1) * 512])
                    if idx >= 3:
                        b.dwait("dve", f"abo{q}")
                    b.dwait("dve", f"abx{q}")
                    b.wait("dve", "pe", vp)
                    vd = b.pre("dve").inc("dve", nc.vector.tensor_add(
                        oxt[q][:, :], pwo[q][:, :], xr[q][:, :]))
                    b.wait("sy", "dve", vd)
                    vo = b.dma("sy", f"abo{q}", xres[do * 128:(do + 1) * 128,
                               tg * 512:(tg + 1) * 512], oxt[q][:, :])
                    pv_wo[do] = vp
                    seq.append((vd, vo))
            b.dwait("sy", "abo0")
            b.dwait("sy", "abo1")
            b.dwait("sy", "abo2")
        b.bar()


def build_nc(phases=("attn_a", "prep", "ffn1", "attn_b", "ffn2")):
    nc = bass.Bass("TRN2", target_bir_lowering=False)
    x_in = nc.dram_tensor("x", [D, S], F32, kind="ExternalInput").ap()
    w1t = nc.dram_tensor("w1t", [D, HID], F32, kind="ExternalInput").ap()
    w3t = nc.dram_tensor("w3t", [D, HID], F32, kind="ExternalInput").ap()
    w2t = nc.dram_tensor("w2t", [HID, D], F32, kind="ExternalInput").ap()
    wqt = nc.dram_tensor("wqt", [D, D], F32, kind="ExternalInput").ap()
    wkt = nc.dram_tensor("wkt", [D, D], F32, kind="ExternalInput").ap()
    wvt = nc.dram_tensor("wvt", [D, D], F32, kind="ExternalInput").ap()
    wot = nc.dram_tensor("wot", [D, D], F32, kind="ExternalInput").ap()
    out = nc.dram_tensor("out", [D, S], F32, kind="ExternalOutput").ap()
    xres = nc.dram_tensor("xres", [D, S], F32, kind="Internal").ap()
    w2tmp = nc.dram_tensor("w2tmp", [HID, D], BF16, kind="Internal").ap()
    w2dd = nc.dram_tensor("w2dd", [HID, D], BF16, kind="Internal").ap()
    hid = nc.dram_tensor("hid", [2, NHID, 128, 1024], BF16, kind="Internal")

    with ExitStack() as st:
        b = Bld(nc, st)
        consts = emit_consts(b)
        if "attn_a" in phases:
            emit_attn_a(b, x_in, xres, consts)
        if "prep" in phases and "ffn1" not in phases:
            with ExitStack() as pst:
                for _ in w2_prep_steps(b, w2t, w2tmp, w2dd, consts, pst):
                    pass
                b.dwait("sy", "wpd0")
                b.dwait("sy", "wpd1")
            b.bar()
        if "ffn1" in phases:
            pa = (w2t, w2tmp) if "prep" in phases else None
            emit_ffn(b, xres, xres, w1t, w3t, w2dd, hid, consts, prep_args=pa)
        if "attn_b" in phases:
            emit_attn_b(b, xres, wqt, wkt, wvt, wot, consts)
        if "ffn2" in phases:
            emit_ffn(b, xres, out, w1t, w3t, w2dd, hid, consts)
        b.dwait("sy", "fbo0")
        b.dwait("sy", "fbo1")
    return nc


def kernel(L0_attn_a_x, L0_attn_b_wq, L0_attn_b_wk, L0_attn_b_wv, L0_attn_b_wo,
           L0_ffn_W1, L0_ffn_W2, L0_ffn_W3):
    x = np.asarray(L0_attn_a_x, dtype=np.float32)
    wshared = {
        "w1t": np.ascontiguousarray(np.asarray(L0_ffn_W1, np.float32).T),
        "w3t": np.ascontiguousarray(np.asarray(L0_ffn_W3, np.float32).T),
        "w2t": np.ascontiguousarray(np.asarray(L0_ffn_W2, np.float32).T),
        "wqt": np.ascontiguousarray(np.asarray(L0_attn_b_wq, np.float32).T),
        "wkt": np.ascontiguousarray(np.asarray(L0_attn_b_wk, np.float32).T),
        "wvt": np.ascontiguousarray(np.asarray(L0_attn_b_wv, np.float32).T),
        "wot": np.ascontiguousarray(np.asarray(L0_attn_b_wo, np.float32).T),
    }
    nc = build_nc()
    in_maps = [dict(wshared, x=np.ascontiguousarray(x[i].T)) for i in range(8)]
    res = run_bass_kernel_spmd(nc, in_maps, core_ids=list(range(8)))
    outs = [np.ascontiguousarray(np.asarray(r["out"]).T) for r in res.results]
    return np.stack(outs, axis=0)


if __name__ == "__main__":
    nc = build_nc()
    print("instructions:", nc.n_instructions())
